# version 01
# baseline (speedup 1.0000x reference)
import numpy as np
import ml_dtypes
import concourse.bass as bass
import concourse.mybir as mybir
from concourse.bass_utils import run_bass_kernel_spmd
from contextlib import ExitStack

F32 = mybir.dt.float32
BF16 = mybir.dt.bfloat16
AF = mybir.ActivationFunctionType
ALU = mybir.AluOpType
AX = mybir.AxisListType

C = 8
T = 16384
TC = T // C
D = 4096
H = 32
HD = 128
BS = 256
NBL = TC // BS
NE = 8
NR = 16
Q_START, Q_END = 611, T - 1
NMASK = Q_END - Q_START
EPS = 1e-5

BF = ml_dtypes.bfloat16


def _decays():
    base = 1.0 / 2.0 ** (8.0 / H)
    factor = 1.0 - 1.0 / (31 + 1e-5) + 1e-5
    slope = (base ** (np.arange(H, dtype=np.float64) + 1.0)) * factor
    r = np.arange(BS, dtype=np.float64) + 1.0
    q_decay = np.exp(-slope[:, None] * r[None, :])
    k_decay = np.exp(-slope[:, None] * (BS - r)[None, :])
    diff = r[:, None] - r[None, :]
    diag = np.where(diff >= 0, np.exp(-slope[:, None, None] * diff[None]), 0.0)
    block_decay = np.exp(-slope * BS)
    return (q_decay.astype(np.float32), k_decay.astype(np.float32),
            diag.astype(np.float32), block_decay)



def build_l1():
    nc = bass.Bass("TRN2", target_bir_lowering=False, num_devices=C)

    xT = nc.dram_tensor("xT", [D, TC], F32, kind="ExternalInput").ap()
    WqT = nc.dram_tensor("WqT", [D, D], F32, kind="ExternalInput").ap()
    WkT = nc.dram_tensor("WkT", [D, D], F32, kind="ExternalInput").ap()
    WvT = nc.dram_tensor("WvT", [D, D], F32, kind="ExternalInput").ap()
    rWT = nc.dram_tensor("rWT", [D, NE], F32, kind="ExternalInput").ap()
    negm = nc.dram_tensor("negm", [NE, 2], F32, kind="ExternalInput").ap()
    kdext = nc.dram_tensor("kdext", [H, 16, 128], F32, kind="ExternalInput").ap()
    ident16 = nc.dram_tensor("ident16", [16, 16], F32, kind="ExternalInput").ap()
    ones_bf = nc.dram_tensor("ones_bf", [128, 1], BF16, kind="ExternalInput").ap()
    ones_f = nc.dram_tensor("ones_f", [1, 128], F32, kind="ExternalInput").ap()
    identb = nc.dram_tensor("identb", [128, 128], BF16, kind="ExternalInput").ap()

    xnT_o = nc.dram_tensor("xnT_o", [D, TC], BF16, kind="ExternalOutput").ap()
    qT_o = nc.dram_tensor("qT_o", [D, TC], BF16, kind="ExternalOutput").ap()
    kT_o = nc.dram_tensor("kT_o", [D, TC], BF16, kind="ExternalOutput").ap()
    vT_o = nc.dram_tensor("vT_o", [D, TC], BF16, kind="ExternalOutput").ap()
    S_o = nc.dram_tensor("S_o", [H * HD, HD], F32, kind="ExternalOutput").ap()
    zr_o = nc.dram_tensor("zr_o", [NE, 1], F32, kind="ExternalOutput").ap()

    NT = 32
    NTC = 4
    CW = 512

    cols = []
    for h in range(H):
        cols.append(("k", h))
        cols.append(("v", h))
    for h in range(H):
        cols.append(("q", h))
    NCOL = len(cols)
    w_src = {"q": WqT, "k": WkT, "v": WvT}
    out_t = {"q": qT_o, "k": kT_o, "v": vT_o}
    chain_at = {}
    for h in range(H):
        ci = min(2 * h + 5, NCOL - 1)
        chain_at.setdefault(ci, []).append(h)
    s_export_at = {}
    for h in range(H):
        ci = min(2 * h + 10, NCOL - 1)
        s_export_at.setdefault(ci, []).append(h)

    with ExitStack() as ctx:
        E = ctx.enter_context
        xnT = E(nc.sbuf_tensor("xnT", [128, NT * TC], BF16))
        xbuf = [E(nc.sbuf_tensor(f"xbuf{i}", [128, CW], F32)) for i in range(2)]
        xbf = [E(nc.sbuf_tensor(f"xbf{i}", [128, CW], BF16)) for i in range(2)]
        sqb = [E(nc.sbuf_tensor(f"sqb{i}", [128, CW], BF16)) for i in range(2)]
        tmpb = E(nc.sbuf_tensor("tmpb", [128, CW], F32))
        mrow = E(nc.sbuf_tensor("mrow", [1, CW], F32))
        erow = E(nc.sbuf_tensor("erow", [1, CW], F32))
        msq = E(nc.sbuf_tensor("msq", [1, CW], F32))
        vrow = E(nc.sbuf_tensor("vrow", [1, CW], F32))
        srow = E(nc.sbuf_tensor("srow", [1, CW], F32))
        rstd = E(nc.sbuf_tensor("rstd", [1, CW], F32))
        mrstd = E(nc.sbuf_tensor("mrstd", [1, CW], F32))
        ones_sb = E(nc.sbuf_tensor("ones_sb", [128, 1], BF16))
        onesf_sb = E(nc.sbuf_tensor("onesf_sb", [1, 128], F32))
        wbuf = [E(nc.sbuf_tensor(f"wbuf{i}", [128, NT * 128], BF16)) for i in range(2)]
        evb = [E(nc.sbuf_tensor(f"evb{i}", [128, CW], BF16)) for i in range(8)]
        rw_sb = E(nc.sbuf_tensor("rw_sb", [128, NT * NE], BF16))
        negm_sb = E(nc.sbuf_tensor("negm_sb", [NE, 2], F32))
        zparts = E(nc.sbuf_tensor("zparts", [NE, NTC + 4], F32))
        zr_sb = E(nc.sbuf_tensor("zr_sb", [NE, 1], F32))
        kdnat = [E(nc.sbuf_tensor(f"kdnat{i}", [16, 128], F32)) for i in range(2)]
        kdcols = E(nc.sbuf_tensor("kdcols", [128, 16], F32))
        id16 = E(nc.sbuf_tensor("id16", [16, 16], F32))
        kch = [E(nc.sbuf_tensor(f"kch{i}", [128, 16 * 128], BF16)) for i in range(2)]
        vch = [E(nc.sbuf_tensor(f"vch{i}", [128, 16 * 128], BF16)) for i in range(2)]
        kts = E(nc.sbuf_tensor("kts", [128, 16 * 128], BF16))
        vtt = E(nc.sbuf_tensor("vtt", [128, 16 * 128], BF16))
        idb = E(nc.sbuf_tensor("idb", [128, 128], BF16))
        S_sb = [E(nc.sbuf_tensor(f"S_sb{i}", [128, HD], F32)) for i in range(2)]
        dumcol = E(nc.sbuf_tensor("dumcol", [128, 1], F32))
        pb = [E(nc.psum_tensor(f"pb{i}", [128, 512], F32)) for i in range(6)]
        pt = [E(nc.psum_tensor(f"pt{i}", [128, 1024], BF16)) for i in range(2)]
        names = (["s_misc", "s_x0", "s_x1", "s_p1v", "s_p1a", "s_mmst", "s_vvar",
                  "s_astd", "s_vrs", "s_mmbc", "s_vxn", "s_wc0", "s_wc1", "s_rw",
                  "s_mmcol", "s_ev", "s_spx", "s_mmT", "s_kdn0", "s_kdn1",
                  "s_mmkd", "s_vkd", "s_vev", "s_kv0", "s_kv1", "s_mmS",
                  "s_vS", "s_spS0", "s_spS1", "s_mmz", "s_vz", "s_vzf", "s_spz"]
                 + [f"s_spq{i}" for i in range(8)] + ["s_vch"])
        sem = {n: E(nc.semaphore(n)) for n in names}
        block = E(nc.Block())

        def xl_idx(tc, p, cc):
            return tc * 64 + p * 32 + cc

        def wait_xload(eng, tc, p, cc):
            L = xl_idx(tc, p, cc)
            eng.wait_ge(sem[f"s_x{L % 2}"], 16 * (L // 2 + 1))

        def eidx(ci, tc):
            return 4 * ci + tc

        @block.sync
        def _(sync):
            sync.dma_start(out=ones_sb[:], in_=ones_bf).then_inc(sem["s_misc"], 16)
            sync.dma_start(out=onesf_sb[:], in_=ones_f).then_inc(sem["s_misc"], 16)
            sync.dma_start(out=id16[:], in_=ident16).then_inc(sem["s_misc"], 16)
            sync.dma_start(out=negm_sb[:], in_=negm).then_inc(sem["s_misc"], 16)
            sync.dma_start(out=idb[:], in_=identb).then_inc(sem["s_misc"], 16)
            for tc in range(NTC):
                for p in range(2):
                    for cc in range(NT):
                        li = tc * 64 + p * 32 + cc
                        if li >= 2:
                            ptc, pp, pcc = (li - 2) // 64, ((li - 2) % 64) // 32, (li - 2) % 32
                            if pp == 0:
                                sync.wait_ge(sem["s_p1v"], ptc * 32 + pcc + 1)
                                sync.wait_ge(sem["s_p1a"], ptc * 32 + pcc + 1)
                            else:
                                sync.wait_ge(sem["s_vxn"], ptc * 32 + pcc + 1)
                        sync.dma_start(
                            out=xbuf[li % 2][:],
                            in_=xT[cc * 128:(cc + 1) * 128, tc * CW:(tc + 1) * CW],
                        ).then_inc(sem[f"s_x{li % 2}"], 16)
            for cc in range(NT):
                sync.wait_ge(sem["s_vxn"], 3 * 32 + cc + 1)
                sync.dma_start(
                    out=xnT_o[cc * 128:(cc + 1) * 128, :],
                    in_=xnT[:, cc * TC:(cc + 1) * TC],
                ).then_inc(sem["s_spx"], 16)
            for ci, (kind, h) in enumerate(cols):
                for hh in s_export_at.get(ci, []):
                    sync.wait_ge(sem["s_vS"], hh + 1)
                    sync.dma_start(
                        out=S_o[hh * HD:(hh + 1) * HD, :], in_=S_sb[hh % 2][:]
                    ).then_inc(sem[f"s_spS{hh % 2}"], 16)
                for tc in range(NTC):
                    e = eidx(ci, tc)
                    sync.wait_ge(sem["s_ev"], e + 1)
                    sync.dma_start(
                        out=out_t[kind][h * 128:(h + 1) * 128, tc * CW:(tc + 1) * CW],
                        in_=evb[e % 8][:],
                    ).then_inc(sem[f"s_spq{e % 8}"], 16)
                if kind == "v":
                    hh = h
                    if hh >= 2:
                        sync.wait_ge(sem["s_vkd"], hh - 1)
                    sync.dma_start(out=kdnat[hh % 2][:], in_=kdext[hh]).then_inc(
                        sem[f"s_kdn{hh % 2}"], 16)
                    for sl in range(8):
                        sync.wait_ge(sem[f"s_spq{sl}"], 16 * (hh + 1))
                    if hh >= 2:
                        sync.wait_ge(sem["s_mmS"], hh - 1)
                    sync.dma_start(out=kch[hh % 2][:],
                                   in_=kT_o[hh * 128:(hh + 1) * 128, :]).then_inc(
                        sem[f"s_kv{hh % 2}"], 16)
                    sync.dma_start(out=vch[hh % 2][:],
                                   in_=vT_o[hh * 128:(hh + 1) * 128, :]).then_inc(
                        sem[f"s_kv{hh % 2}"], 16)
            sync.wait_ge(sem["s_vzf"], 1)
            sync.dma_start(out=zr_o, in_=zr_sb[:]).then_inc(sem["s_spz"], 16)
            for sl in range(8):
                sync.wait_ge(sem[f"s_spq{sl}"], 16 * (4 * NCOL // 8))
            sync.wait_ge(sem["s_spx"], 16 * NT)
            sync.wait_ge(sem["s_spS0"], 16 * (H // 2))
            sync.wait_ge(sem["s_spS1"], 16 * (H // 2))
            sync.wait_ge(sem["s_spz"], 16)

        @block.gpsimd
        def _(gpsimd):
            gpsimd.dma_start(
                out=rw_sb[:].rearrange("p (c e) -> p c e", c=NT),
                in_=rWT.rearrange("(c p) e -> p c e", p=128),
            ).then_inc(sem["s_rw"], 16)
            for ci, (kind, h) in enumerate(cols):
                if ci >= 2:
                    gpsimd.wait_ge(sem["s_mmcol"], 4 * (ci - 1))
                src = w_src[kind][:, h * 128:(h + 1) * 128]
                gpsimd.dma_start(
                    out=wbuf[ci % 2][:].rearrange("p (c o) -> p c o", c=NT),
                    in_=src.rearrange("(c p) o -> p c o", p=128),
                ).then_inc(sem[f"s_wc{ci % 2}"], 16)

        @block.vector
        def _(vector):
            vn = [0]

            def vstep(ins):
                vn[0] += 1
                ins.then_inc(sem["s_vch"], 1)
                return vn[0]

            def vafter(n):
                vector.wait_ge(sem["s_vch"], n)

            vector.wait_ge(sem["s_misc"], 80)
            for tc in range(NTC):
                for cc in range(NT):
                    wait_xload(vector, tc, 0, cc)
                    if tc * 32 + cc >= 2:
                        vector.wait_ge(sem["s_mmst"], tc * 32 + cc - 1)
                    vector.tensor_copy(xbf[cc % 2][:], xbuf[cc % 2][:]).then_inc(
                        sem["s_p1v"], 1)
                vector.wait_ge(sem["s_mmst"], 32 * (tc + 1))
                n_m = vstep(vector.tensor_scalar_mul(mrow[:], pb[0][0:1, :], 1.0 / D))
                vector.tensor_scalar_mul(erow[:], pb[1][0:1, :], 1.0 / D).then_inc(
                    sem["s_vvar"], 1)
                vafter(n_m)
                n_q = vstep(vector.tensor_mul(msq[:], mrow[:], mrow[:]))
                vafter(n_q)
                vector.scalar_tensor_tensor(
                    vrow[:], erow[:], EPS, msq[:], ALU.add, ALU.subtract
                ).then_inc(sem["s_vvar"], 1)
                vector.wait_ge(sem["s_astd"], tc + 1)
                n_r = vstep(vector.reciprocal(rstd[:], srow[:]))
                vafter(n_r)
                n_mr = vstep(vector.tensor_mul(mrstd[:], mrow[:], rstd[:]))
                vafter(n_mr)
                vector.engine_nop().then_inc(sem["s_vrs"], 1)
                for cc in range(NT):
                    wait_xload(vector, tc, 1, cc)
                    vector.wait_ge(sem["s_mmbc"], 2 * (tc + 1))
                    vector.wait_ge(sem["s_vxn"], tc * 32 + cc)
                    n_t = vstep(vector.tensor_mul(tmpb[:], xbuf[cc % 2][:], pb[2][:, :]))
                    vafter(n_t)
                    vector.tensor_tensor(
                        xnT[:, cc * TC + tc * CW: cc * TC + (tc + 1) * CW],
                        tmpb[:], pb[3][:, :], ALU.subtract,
                    ).then_inc(sem["s_vxn"], 1)
            zn = {}
            for tc in range(NTC):
                vector.wait_ge(sem["s_mmz"], tc + 1)
                zn[tc] = vstep(vector.tensor_reduce(
                    zparts[:, tc:tc + 1], pb[4][0:NE, :], AX.X, ALU.add))
                if tc == 1:
                    zn["a"] = vstep(vector.tensor_reduce(
                        zparts[:, 4:5], pb[4][0:NE, 0:99], AX.X, ALU.add))
                if tc == 3:
                    zn["b"] = vstep(vector.tensor_copy(
                        zparts[:, 5:6], pb[4][0:NE, 511:512]))
                vector.engine_nop().then_inc(sem["s_vz"], 1)
            vafter(max(zn.values()))
            n1 = vstep(vector.tensor_reduce(zparts[:, 6:7], zparts[:, 0:NTC],
                                            AX.X, ALU.add))
            n2 = vstep(vector.tensor_add(zparts[:, 7:8], zparts[:, 0:1],
                                         zparts[:, 4:5]))
            vafter(n2)
            n3 = vstep(vector.scalar_tensor_tensor(
                zparts[:, 6:7], zparts[:, 7:8], negm_sb[:, 0:1], zparts[:, 6:7],
                ALU.mult, ALU.add))
            vafter(n3)
            vector.scalar_tensor_tensor(
                zr_sb[:], zparts[:, 5:6], negm_sb[:, 1:2], zparts[:, 6:7],
                ALU.mult, ALU.add).then_inc(sem["s_vzf"], 1)
            for h in range(H):
                vector.wait_ge(sem["s_mmkd"], h + 1)
                n_kd = vstep(vector.tensor_copy(kdcols[:], pb[4][:, 0:16]))
                vafter(n_kd)
                vector.engine_nop().then_inc(sem["s_vkd"], 1)
                for c2 in range(16):
                    vector.wait_ge(sem["s_mmT"], 16 * h + c2 + 1)
                    bank = pt[c2 % 2]
                    vector.tensor_scalar_mul(
                        kts[:, c2 * 128:(c2 + 1) * 128],
                        bank[:, 0:128], kdcols[:, c2:c2 + 1])
                    vector.tensor_copy(
                        vtt[:, c2 * 128:(c2 + 1) * 128],
                        bank[:, 128:256]).then_inc(sem["s_vev"], 1)
                vector.wait_ge(sem["s_mmS"], h + 1)
                if h >= 2:
                    vector.wait_ge(sem[f"s_spS{h % 2}"], 16 * (h // 2))
                n_s = vstep(vector.tensor_copy(S_sb[h % 2][:], pb[5][:, 0:128]))
                vafter(n_s)
                vector.engine_nop().then_inc(sem["s_vS"], 1)

        @block.scalar
        def _(scalar):
            for tc in range(NTC):
                for cc in range(NT):
                    wait_xload(scalar, tc, 0, cc)
                    if tc * 32 + cc >= 2:
                        scalar.wait_ge(sem["s_mmst"], tc * 32 + cc - 1)
                    scalar.square(sqb[cc % 2][:], xbuf[cc % 2][:]).then_inc(
                        sem["s_p1a"], 1)
                scalar.wait_ge(sem["s_vvar"], 2 * (tc + 1))
                scalar.sqrt(srow[:], vrow[:]).then_inc(sem["s_astd"], 1)
            for ci in range(NCOL):
                for tc in range(NTC):
                    e = eidx(ci, tc)
                    scalar.wait_ge(sem["s_mmcol"], e + 1)
                    if e >= 8:
                        scalar.wait_ge(sem[f"s_spq{e % 8}"], 16 * (e // 8))
                    scalar.activation(evb[e % 8][:], pb[tc][:, :], AF.Silu).then_inc(
                        sem["s_ev"], 1)

        @block.tensor
        def _(tensor):
            tensor.wait_ge(sem["s_misc"], 80)
            for tc in range(NTC):
                if tc > 0:
                    tensor.wait_ge(sem["s_vvar"], 2 * tc)
                for cc in range(NT):
                    tensor.wait_ge(sem["s_p1v"], tc * 32 + cc + 1)
                    tensor.wait_ge(sem["s_p1a"], tc * 32 + cc + 1)
                    st = cc == 0
                    sp = cc == NT - 1
                    tensor.matmul(pb[0][0:1, :], ones_sb[:], xbf[cc % 2][:],
                                  start=st, stop=sp)
                    tensor.matmul(pb[1][0:1, :], ones_sb[:], sqb[cc % 2][:],
                                  start=st, stop=sp).then_inc(sem["s_mmst"], 1)
                tensor.wait_ge(sem["s_vrs"], tc + 1)
                if tc > 0:
                    tensor.wait_ge(sem["s_vxn"], 32 * tc)
                tensor.matmul(pb[2][:, :], onesf_sb[:], rstd[:], start=True, stop=True)
                tensor.matmul(pb[3][:, :], onesf_sb[:], mrstd[:],
                              start=True, stop=True).then_inc(sem["s_mmbc"], 2)
            for ci, (kind, h) in enumerate(cols):
                tensor.wait_ge(sem[f"s_wc{ci % 2}"], 16 * (ci // 2 + 1))
                if ci == 0:
                    tensor.wait_ge(sem["s_vxn"], 32 * NTC)
                if ci >= 1:
                    tensor.wait_ge(sem["s_ev"], 4 * ci)
                for dc in range(NT):
                    for tc in range(NTC):
                        ins = tensor.matmul(
                            pb[tc][:, :],
                            wbuf[ci % 2][:, dc * 128:(dc + 1) * 128],
                            xnT[:, dc * TC + tc * CW: dc * TC + (tc + 1) * CW],
                            start=(dc == 0), stop=(dc == NT - 1),
                        )
                        if dc == NT - 1:
                            ins.then_inc(sem["s_mmcol"], 1)
                if ci < NTC:
                    tc = ci
                    if tc == 0:
                        tensor.wait_ge(sem["s_rw"], 16)
                    if tc > 0:
                        tensor.wait_ge(sem["s_vz"], tc)
                    for dc in range(NT):
                        ins = tensor.matmul(
                            pb[4][0:NE, :],
                            rw_sb[:, dc * NE:(dc + 1) * NE],
                            xnT[:, dc * TC + tc * CW: dc * TC + (tc + 1) * CW],
                            start=(dc == 0), stop=(dc == NT - 1),
                        )
                        if dc == NT - 1:
                            ins.then_inc(sem["s_mmz"], 1)
                for hh in chain_at.get(ci, []):
                    tensor.wait_ge(sem[f"s_kdn{hh % 2}"], 16 * (hh // 2 + 1))
                    tensor.wait_ge(sem["s_vkd"], hh)
                    if hh == 0:
                        tensor.wait_ge(sem["s_vz"], NTC)
                    tensor.transpose(pb[4][:, 0:16], kdnat[hh % 2][:],
                                     id16[:]).then_inc(sem["s_mmkd"], 1)
                    tensor.wait_ge(sem[f"s_kv{hh % 2}"], 32 * (hh // 2 + 1))
                    for c2 in range(16):
                        if 16 * hh + c2 >= 2:
                            tensor.wait_ge(sem["s_vev"], 16 * hh + c2 - 1)
                        bank = pt[c2 % 2]
                        tensor.transpose(bank[:, 0:128],
                                         kch[hh % 2][:, c2 * 128:(c2 + 1) * 128],
                                         idb[:])
                        tensor.transpose(bank[:, 128:256],
                                         vch[hh % 2][:, c2 * 128:(c2 + 1) * 128],
                                         idb[:]).then_inc(sem["s_mmT"], 1)
                    tensor.wait_ge(sem["s_vev"], 16 * (hh + 1))
                    if hh >= 1:
                        tensor.wait_ge(sem["s_vS"], hh)
                    for c2 in range(16):
                        ins = tensor.matmul(
                            pb[5][:, 0:128],
                            kts[:, c2 * 128:(c2 + 1) * 128],
                            vtt[:, c2 * 128:(c2 + 1) * 128],
                            start=(c2 == 0), stop=(c2 == 15),
                        )
                        if c2 == 15:
                            ins.then_inc(sem["s_mmS"], 1)

    return nc



def _host_prep_l1(inputs):
    x = np.asarray(inputs["L0_attn_hidden_states"]).reshape(T, D)
    qkv_w = np.asarray(inputs["L0_attn_qkv_w"])
    w3 = qkv_w.reshape(H, 3, HD, D)
    Wq = w3[:, 0].reshape(H * HD, D)
    Wk = w3[:, 1].reshape(H * HD, D)
    Wv = w3[:, 2].reshape(H * HD, D)
    WqT = np.ascontiguousarray(Wq.T)
    WkT = np.ascontiguousarray(Wk.T)
    WvT = np.ascontiguousarray(Wv.T)
    rWT = np.ascontiguousarray(np.asarray(inputs["L0_ffn_router_W"]).T)

    q_decay, k_decay, diag, block_decay = _decays()
    t_ = np.arange(TC)
    kdfull = (k_decay[:, t_ % BS] *
              (block_decay[:, None] ** (NBL - 1 - t_ // BS)[None, :])).astype(np.float32)
    kdext = np.ascontiguousarray(kdfull.reshape(H, 16, 128))

    ident16 = np.eye(16, dtype=np.float32)
    identb = np.eye(128, dtype=BF)
    ones_bf = np.ones((128, 1), BF)
    ones_f = np.ones((1, 128), np.float32)

    in_maps = []
    for c in range(C):
        sl = slice(c * TC, (c + 1) * TC)
        mA = 1.0 if c == 0 else 0.0
        mB = 1.0 if c == C - 1 else 0.0
        negm = np.broadcast_to(
            np.array([-mA, -mB], np.float32), (NE, 2)).copy()
        in_maps.append(dict(
            xT=np.ascontiguousarray(x[sl].T),
            WqT=WqT, WkT=WkT, WvT=WvT, rWT=rWT,
            negm=negm, kdext=kdext, ident16=ident16, identb=identb,
            ones_bf=ones_bf, ones_f=ones_f,
        ))
    return in_maps





def build_l2():
    nc = bass.Bass("TRN2", target_bir_lowering=False, num_devices=C)

    xnT_i = nc.dram_tensor("xnT_i", [D, TC], BF16, kind="ExternalInput").ap()
    qT_i = nc.dram_tensor("qT_i", [D, TC], BF16, kind="ExternalInput").ap()
    kT_i = nc.dram_tensor("kT_i", [D, TC], BF16, kind="ExternalInput").ap()
    vT_i = nc.dram_tensor("vT_i", [D, TC], BF16, kind="ExternalInput").ap()
    S_all = nc.dram_tensor("S_all", [C, H * HD, HD], F32, kind="ExternalInput").ap()
    zr_allT = nc.dram_tensor("zr_allT", [C, NE], F32, kind="ExternalInput").ap()
    xT2 = nc.dram_tensor("xT2", [D, TC], F32, kind="ExternalInput").ap()
    gwT = nc.dram_tensor("gwT", [D, D], F32, kind="ExternalInput").ap()
    fwT = nc.dram_tensor("fwT", [D, D], F32, kind="ExternalInput").ap()
    owT = nc.dram_tensor("owT", [D, D], F32, kind="ExternalInput").ap()
    AT2 = nc.dram_tensor("AT2", [D, 128], F32, kind="ExternalInput").ap()
    BT2 = nc.dram_tensor("BT2", [128, D], F32, kind="ExternalInput").ap()
    rbrow = nc.dram_tensor("rbrow", [1, NE], F32, kind="ExternalInput").ap()
    selm = nc.dram_tensor("selm", [NE, 128], F32, kind="ExternalInput").ap()
    onessc = nc.dram_tensor("onessc", [1, 1], F32, kind="ExternalInput").ap()
    ones8 = nc.dram_tensor("ones8", [C, 1], F32, kind="ExternalInput").ap()
    bcols = nc.dram_tensor("bcols", [128, 32], F32, kind="ExternalInput").ap()
    nwcols = nc.dram_tensor("nwcols", [128, 32], F32, kind="ExternalInput").ap()
    coefb = nc.dram_tensor("coefb", [128, H * C], F32, kind="ExternalInput").ap()
    kd2 = nc.dram_tensor("kd2", [128, 2 * H], F32, kind="ExternalInput").ap()
    qdecb = nc.dram_tensor("qdecb", [H, 128, BS], F32, kind="ExternalInput").ap()
    maskb = nc.dram_tensor("maskb", [H, 128, 2 * BS], F32, kind="ExternalInput").ap()
    identb2 = nc.dram_tensor("identb2", [128, 128], BF16, kind="ExternalInput").ap()
    ones_bf2 = nc.dram_tensor("ones_bf2", [128, 1], BF16, kind="ExternalInput").ap()
    ones_f2 = nc.dram_tensor("ones_f2", [1, 128], F32, kind="ExternalInput").ap()

    outT_o = nc.dram_tensor("outT_o", [D, TC], F32, kind="ExternalOutput").ap()

    gate_hb = nc.dram_tensor("gate_hb", [D, TC], F32)
    g_hb = nc.dram_tensor("g_hb", [D, TC], F32)
    attn_hb = nc.dram_tensor("attn_hb", [D, TC], BF16)

    NT_ = 32
    TH = TC // 2
    CW_ = 512
    _, _, _, bd_arr = _decays()
    bd_list = [float(b) for b in bd_arr]

    ccols = [("g", i) for i in range(NT_)] + [("w", i) for i in range(NT_)]
    NCC = len(ccols)
    groups = []
    for half in range(2):
        for ci, (kind, idx) in enumerate(ccols):
            for tc in range(2):
                groups.append((half, ci, kind, idx, tc))
    NG = len(groups)
    g_bank = {}
    g_evac = {}
    bank_prev_evac = {}
    e_ctr = 0
    bank_last = {}
    for gidx, (half, ci, kind, idx, tc) in enumerate(groups):
        b = 2 * (ci % 2) + tc
        g_bank[gidx] = b
        g_evac[gidx] = e_ctr
        bank_prev_evac[gidx] = bank_last.get(b)
        bank_last[b] = e_ctr
        e_ctr += 1
    NOI = 2 * NT_ * 2

    NPC = 2 * NCC * 2

    with ExitStack() as ctx:
        E = ctx.enter_context
        actT = E(nc.sbuf_tensor("actT", [128, NT_ * TH], BF16))
        wcol = [E(nc.sbuf_tensor(f"wcol{i}", [128, 16 * 128], BF16)) for i in range(2)]
        at_sb = E(nc.sbuf_tensor("at_sb", [128, NT_ * 128], BF16))
        t_sb = E(nc.sbuf_tensor("t_sb", [128, TC], BF16))
        btsc = E(nc.sbuf_tensor("btsc", [128, D], BF16))
        btbf = E(nc.sbuf_tensor("btbf", [128, D], BF16))
        evf = [E(nc.sbuf_tensor(f"evf{i}", [128, CW_], F32)) for i in range(4)]
        zr_sb2 = E(nc.sbuf_tensor("zr_sb2", [C, NE], F32))
        rb_sb = E(nc.sbuf_tensor("rb_sb", [1, NE], F32))
        sel_sb = E(nc.sbuf_tensor("sel_sb", [NE, 128], F32))
        one1 = E(nc.sbuf_tensor("one1", [1, 1], F32))
        ones8_sb = E(nc.sbuf_tensor("ones8_sb", [C, 1], F32))
        lg = E(nc.sbuf_tensor("lg", [1, 6 * NE], F32))
        routT_sb = E(nc.sbuf_tensor("routT_sb", [NE, 1], F32))
        rout_col = E(nc.sbuf_tensor("rout_col", [128, 1], F32))
        qch = [E(nc.sbuf_tensor(f"qch2{i}", [128, TC], BF16)) for i in range(2)]
        kch = [E(nc.sbuf_tensor(f"kch2{i}", [128, TC], BF16)) for i in range(2)]
        vch = [E(nc.sbuf_tensor(f"vch2{i}", [128, TC], BF16)) for i in range(2)]
        sal = [E(nc.sbuf_tensor(f"sal{i}", [128, C * HD], F32)) for i in range(2)]
        msk = [E(nc.sbuf_tensor(f"msk{i}", [128, 2 * BS], F32)) for i in range(2)]
        qdc = [E(nc.sbuf_tensor(f"qdc{i}", [128, BS], F32)) for i in range(2)]
        kd2_sb = E(nc.sbuf_tensor("kd2_sb", [128, 2 * H], F32))
        kvA = E(nc.sbuf_tensor("kvA", [128, HD], F32))
        kvB = E(nc.sbuf_tensor("kvB", [128, HD], F32))
        kvb = E(nc.sbuf_tensor("kvb", [128, HD], BF16))
        scT = E(nc.sbuf_tensor("scT", [128, 2 * BS], BF16))
        qdT = E(nc.sbuf_tensor("qdT", [128, BS], BF16))
        vtt2 = E(nc.sbuf_tensor("vtt2", [128, BS], BF16))
        kt2 = E(nc.sbuf_tensor("kt2", [128, BS], BF16))
        atb = [E(nc.sbuf_tensor(f"atb{i}", [128, BS], BF16)) for i in range(8)]
        idb2 = E(nc.sbuf_tensor("idb2", [128, 128], BF16))
        onesb2 = E(nc.sbuf_tensor("onesb2", [128, 1], BF16))
        onesf2 = E(nc.sbuf_tensor("onesf2", [1, 128], F32))
        bc_sb = E(nc.sbuf_tensor("bc_sb", [128, NT_], F32))
        nw_sb = E(nc.sbuf_tensor("nw_sb", [128, NT_], F32))
        cf_sb = E(nc.sbuf_tensor("cf_sb", [128, H * C], F32))
        atr = [E(nc.sbuf_tensor(f"atr{i}", [128, CW_], BF16)) for i in range(2)]
        at2 = [E(nc.sbuf_tensor(f"at2_{i}", [128, CW_], BF16)) for i in range(2)]
        sqb2 = [E(nc.sbuf_tensor(f"sqb2{i}", [128, CW_], BF16)) for i in range(2)]
        gtr = [E(nc.sbuf_tensor(f"gtr{i}", [128, CW_], F32)) for i in range(2)]
        rrow = E(nc.sbuf_tensor("rrow", [1, 3 * CW_], F32))
        tmpf = E(nc.sbuf_tensor("tmpf", [128, CW_], F32))
        gre = [E(nc.sbuf_tensor(f"gre{i}", [128, CW_], F32)) for i in range(2)]
        xre = [E(nc.sbuf_tensor(f"xre{i}", [128, CW_], F32)) for i in range(2)]
        osb = [E(nc.sbuf_tensor(f"osb{i}", [128, CW_], F32)) for i in range(2)]
        pb = [E(nc.psum_tensor(f"pb{i}", [128, 512], F32)) for i in range(6)]
        pt = [E(nc.psum_tensor(f"pt{i}", [128, 1024], BF16)) for i in range(2)]
        names = (["s_misc", "s_at", "s_xn0", "s_xn1", "s_wc0", "s_wc1", "s_bt",
                  "s_mmA", "s_vA", "s_aA", "s_vt",
                  "s_mmcol", "s_ev",
                  "s_spe0", "s_spe1", "s_spe2", "s_spe3",
                  "s_ld0", "s_ld1", "s_vkv",
                  "s_mmsc", "s_vsc", "s_mmT2", "s_vev2", "s_mmat", "s_aat",
                  "s_mmC",
                  "s_are0", "s_are1", "s_ar20", "s_ar21", "s_gre0", "s_gre1",
                  "s_mmsq", "s_vrm", "s_arm", "s_mmbc2", "s_vgn",
                  "s_g2re0", "s_g2re1", "s_x2re0", "s_x2re1",
                  "s_mmout", "s_vout", "s_spo0", "s_spo1",
                  "s_vch2", "s_ach2"] + [f"s_spa{i}" for i in range(8)])
        sem = {n: E(nc.semaphore(n)) for n in names}
        block = E(nc.Block())

        NMISC = 12

        @block.sync
        def _(sync):
            for ap_, sb_ in ((zr_allT, zr_sb2), (rbrow, rb_sb), (selm, sel_sb),
                             (onessc, one1), (ones8, ones8_sb), (bcols, bc_sb),
                             (nwcols, nw_sb), (coefb, cf_sb), (kd2, kd2_sb),
                             (identb2, idb2), (ones_bf2, onesb2), (ones_f2, onesf2)):
                sync.dma_start(out=sb_[:], in_=ap_).then_inc(sem["s_misc"], 16)
            for half in range(2):
                if half == 1:
                    sync.wait_ge(sem["s_mmcol"], NG // 2)
                    sync.wait_ge(sem["s_mmA"], 5)
                for cc in range(NT_):
                    sync.dma_start(
                        out=actT[:, cc * TH:(cc + 1) * TH],
                        in_=xnT_i[cc * 128:(cc + 1) * 128,
                                  half * TH:(half + 1) * TH],
                    ).then_inc(sem[f"s_xn{half}"], 16)
                for gidx, (ghalf, ci, kind, idx, tc) in enumerate(groups):
                    if ghalf != half:
                        continue
                    e = g_evac[gidx]
                    sync.wait_ge(sem["s_ev"], e + 1)
                    tgt = gate_hb if kind == "g" else g_hb
                    sync.dma_start(
                        out=tgt[idx * 128:(idx + 1) * 128,
                                half * TH + tc * CW_:
                                half * TH + (tc + 1) * CW_],
                        in_=evf[e % 4][:],
                    ).then_inc(sem[f"s_spe{e % 4}"], 16)
            def attn_exports(hh):
                for j in range(NBL):
                    a = hh * NBL + j
                    sync.wait_ge(sem["s_aat"], a + 1)
                    sync.dma_start(
                        out=attn_hb[hh * 128:(hh + 1) * 128,
                                    j * BS:(j + 1) * BS],
                        in_=atb[j][:],
                    ).then_inc(sem[f"s_spa{j}"], 16)
            for h in range(H):
                if h >= 2:
                    sync.wait_ge(sem["s_mmat"], NBL * (h - 1))
                    sync.wait_ge(sem["s_mmT2"], NBL * (h - 1))
                    sync.wait_ge(sem["s_mmC"], (NBL - 1) * (h - 1))
                    sync.wait_ge(sem["s_vkv"], NBL * (h - 1))
                    sync.wait_ge(sem["s_vsc"], NBL * (h - 1))
                    sync.wait_ge(sem["s_vev2"], NBL * (h - 1))
                sl = sem[f"s_ld{h % 2}"]
                sync.dma_start(out=qch[h % 2][:],
                               in_=qT_i[h * 128:(h + 1) * 128, :]).then_inc(sl, 16)
                sync.dma_start(out=kch[h % 2][:],
                               in_=kT_i[h * 128:(h + 1) * 128, :]).then_inc(sl, 16)
                sync.dma_start(out=vch[h % 2][:],
                               in_=vT_i[h * 128:(h + 1) * 128, :]).then_inc(sl, 16)
                sync.dma_start(
                    out=sal[h % 2][:].rearrange("p (c d) -> p c d", c=C),
                    in_=S_all[:, h * 128:(h + 1) * 128, :].rearrange(
                        "c p d -> p c d"),
                ).then_inc(sl, 16)
                sync.dma_start(out=msk[h % 2][:], in_=maskb[h]).then_inc(sl, 16)
                sync.dma_start(out=qdc[h % 2][:], in_=qdecb[h]).then_inc(sl, 16)
                if h >= 1:
                    attn_exports(h - 1)
            attn_exports(H - 1)
            for slx in range(8):
                sync.wait_ge(sem[f"s_spa{slx}"], 16 * H)
            for slx in range(4):
                sync.wait_ge(sem[f"s_spe{slx}"], 16 * (NG // 4))
            for half in range(2):
                for tc in range(2):
                    tt = half * 2 + tc
                    tok0 = half * TH + tc * CW_
                    for cc in range(NT_):
                        li = tt * NT_ + cc
                        if li >= 2:
                            sync.wait_ge(sem["s_ach2"], li - 1)
                        sync.dma_start(
                            out=atr[li % 2][:],
                            in_=attn_hb[cc * 128:(cc + 1) * 128, tok0:tok0 + CW_],
                        ).then_inc(sem[f"s_are{li % 2}"], 16)
                    for cc in range(NT_):
                        li = tt * NT_ + cc
                        if li >= 2:
                            sync.wait_ge(sem["s_vgn"], li - 1)
                        sync.dma_start(
                            out=at2[li % 2][:],
                            in_=attn_hb[cc * 128:(cc + 1) * 128, tok0:tok0 + CW_],
                        ).then_inc(sem[f"s_ar2{li % 2}"], 16)
                        sync.dma_start(
                            out=gtr[li % 2][:],
                            in_=gate_hb[cc * 128:(cc + 1) * 128, tok0:tok0 + CW_],
                        ).then_inc(sem[f"s_gre{li % 2}"], 16)
                for oc in range(NT_):
                    for tc in range(2):
                        oi = (half * NT_ + oc) * 2 + tc
                        tok0 = half * TH + tc * CW_
                        if oi >= 2:
                            sync.wait_ge(sem["s_vout"], oi - 1)
                        sync.dma_start(
                            out=gre[oi % 2][:],
                            in_=g_hb[oc * 128:(oc + 1) * 128, tok0:tok0 + CW_],
                        ).then_inc(sem[f"s_g2re{oi % 2}"], 16)
                        sync.dma_start(
                            out=xre[oi % 2][:],
                            in_=xT2[oc * 128:(oc + 1) * 128, tok0:tok0 + CW_],
                        ).then_inc(sem[f"s_x2re{oi % 2}"], 16)
                        sync.wait_ge(sem["s_vout"], oi + 1)
                        sync.dma_start(
                            out=outT_o[oc * 128:(oc + 1) * 128, tok0:tok0 + CW_],
                            in_=osb[oi % 2][:],
                        ).then_inc(sem[f"s_spo{oi % 2}"], 16)
            sync.wait_ge(sem["s_spo0"], 16 * (NOI // 2))
            sync.wait_ge(sem["s_spo1"], 16 * (NOI // 2))
            for slx in range(4):
                sync.wait_ge(sem[f"s_spe{slx}"], 16 * (NG // 4))
            for slx in range(8):
                sync.wait_ge(sem[f"s_spa{slx}"], 16 * H)

        @block.gpsimd
        def _(gpsimd):
            gpsimd.dma_start(out=btbf[:], in_=BT2).then_inc(sem["s_bt"], 16)
            gpsimd.dma_start(
                out=at_sb[:].rearrange("p (c o) -> p c o", c=NT_),
                in_=AT2.rearrange("(c p) o -> p c o", p=128),
            ).then_inc(sem["s_at"], 16)
            p = 0
            for half in range(2):
                for ci, (kind, idx) in enumerate(ccols):
                    srcw = (gwT if kind == "g" else fwT)[:, idx * 128:(idx + 1) * 128]
                    for piece in range(2):
                        if p >= 2:
                            pc = (p - 2) // 2
                            gpsimd.wait_ge(sem["s_mmcol"], 2 * pc + 2)
                        gpsimd.dma_start(
                            out=wcol[p % 2][:].rearrange("p (c o) -> p c o", c=16),
                            in_=srcw[piece * 2048:(piece + 1) * 2048, :]
                            .rearrange("(c p) o -> p c o", p=128),
                        ).then_inc(sem[f"s_wc{p % 2}"], 16)
                        p += 1
            for half in range(2):
                for oc in range(NT_):
                    srcw = owT[:, oc * 128:(oc + 1) * 128]
                    for piece in range(2):
                        pc = (p - 2) // 2
                        if pc >= NPC // 2:
                            gpsimd.wait_ge(sem["s_mmout"], 2 * (pc - NPC // 2) + 2)
                        else:
                            gpsimd.wait_ge(sem["s_mmcol"], 2 * pc + 2)
                        gpsimd.dma_start(
                            out=wcol[p % 2][:].rearrange("p (c o) -> p c o", c=16),
                            in_=srcw[piece * 2048:(piece + 1) * 2048, :]
                            .rearrange("(c p) o -> p c o", p=128),
                        ).then_inc(sem[f"s_wc{p % 2}"], 16)
                        p += 1

        @block.vector
        def _(vector):
            vn = [0]

            def vstep(ins):
                vn[0] += 1
                ins.then_inc(sem["s_vch2"], 1)
                return vn[0]

            def vafter(n):
                vector.wait_ge(sem["s_vch2"], n)

            vector.wait_ge(sem["s_misc"], 16 * NMISC)
            vector.wait_ge(sem["s_mmA"], 1)
            n1 = vstep(vector.scalar_tensor_tensor(
                lg[:, 0:NE], pb[3][0:1, 0:NE], 1.0 / NMASK, rb_sb[:],
                ALU.mult, ALU.add))
            vafter(n1)
            n2 = vstep(vector.tensor_reduce(lg[:, NE:NE + 1], lg[:, 0:NE],
                                            AX.X, ALU.max))
            vafter(n2)
            n3 = vstep(vector.tensor_single_scalar(
                lg[:, 2 * NE:3 * NE], lg[:, 0:NE], lg[:, NE:NE + 1], ALU.subtract))
            vafter(n3)
            vector.engine_nop().then_inc(sem["s_vA"], 1)
            vector.wait_ge(sem["s_aA"], 1)
            n4 = vstep(vector.tensor_reduce(lg[:, 4 * NE:4 * NE + 1],
                                            lg[:, 3 * NE:4 * NE], AX.X, ALU.add))
            vafter(n4)
            n5 = vstep(vector.reciprocal(lg[:, 4 * NE + 1:4 * NE + 2],
                                         lg[:, 4 * NE:4 * NE + 1]))
            vafter(n5)
            n6 = vstep(vector.tensor_single_scalar(
                lg[:, 5 * NE:6 * NE], lg[:, 3 * NE:4 * NE],
                lg[:, 4 * NE + 1:4 * NE + 2], ALU.mult))
            vafter(n6)
            vector.engine_nop().then_inc(sem["s_vA"], 1)
            vector.wait_ge(sem["s_mmA"], 2)
            n7 = vstep(vector.tensor_copy(routT_sb[:], pb[3][0:NE, 8:9]))
            vafter(n7)
            vector.engine_nop().then_inc(sem["s_vA"], 1)
            vector.wait_ge(sem["s_mmA"], 3)
            n8 = vstep(vector.tensor_copy(rout_col[:], pb[3][:, 16:17]))
            vector.wait_ge(sem["s_bt"], 16)
            vafter(n8)
            n9 = vstep(vector.tensor_scalar_mul(btsc[:], btbf[:], rout_col[:, 0:1]))
            vafter(n9)
            vector.engine_nop().then_inc(sem["s_vA"], 2)
            for half in range(2):
                for tc in range(2):
                    vector.wait_ge(sem["s_mmA"], 4 + 2 * half + tc)
                    vector.tensor_copy(
                        t_sb[:, half * TH + tc * CW_: half * TH + (tc + 1) * CW_],
                        pb[4][:, :]).then_inc(sem["s_vt"], 1)
            for h in range(H):
                sl = sem[f"s_ld{h % 2}"]
                vector.wait_ge(sl, 16 * 6 * (h // 2 + 1))
                nacc = vstep(vector.tensor_scalar_mul(
                    kvA[:], sal[h % 2][:, 0:HD], cf_sb[:, h * C:h * C + 1]))
                cur, oth = kvA, kvB
                for c2 in range(1, C):
                    vafter(nacc)
                    nacc = vstep(vector.scalar_tensor_tensor(
                        oth[:], sal[h % 2][:, c2 * HD:(c2 + 1) * HD],
                        cf_sb[:, h * C + c2:h * C + c2 + 1], cur[:],
                        ALU.mult, ALU.add))
                    cur, oth = oth, cur
                vafter(nacc)
                if h >= 1:
                    vector.wait_ge(sem["s_mmat"], NBL * h)
                nkvb = vstep(vector.tensor_copy(kvb[:], cur[:]))
                vafter(nkvb)
                vector.engine_nop().then_inc(sem["s_vkv"], 1)
                for j in range(NBL):
                    a = h * NBL + j
                    vector.wait_ge(sem["s_mmsc"], a + 1)
                    if a >= 1:
                        vector.wait_ge(sem["s_mmat"], a)
                    vector.tensor_mul(scT[:], pb[0][:, :],
                                      msk[h % 2][:]).then_inc(sem["s_vsc"], 1)
                    nq = vstep(vector.tensor_mul(
                        qdT[:], qch[h % 2][:, j * BS:(j + 1) * BS], qdc[h % 2][:]))
                    vector.wait_ge(sem["s_mmT2"], a + 1)
                    if a >= 1:
                        vector.wait_ge(sem["s_mmC"], (NBL - 1) * h + j)
                    bank = pt[j % 2]
                    nv = vstep(vector.tensor_copy(vtt2[:], bank[:, 0:BS]))
                    if j < NBL - 1:
                        nk = vstep(vector.tensor_scalar_mul(
                            kt2[:, 0:128], bank[:, BS:BS + 128],
                            kd2_sb[:, 2 * h:2 * h + 1]))
                        nk2 = vstep(vector.tensor_scalar_mul(
                            kt2[:, 128:256], bank[:, BS + 128:2 * BS],
                            kd2_sb[:, 2 * h + 1:2 * h + 2]))
                    vafter(max(nq, nv) if j == NBL - 1 else max(nq, nv, nk2))
                    vector.engine_nop().then_inc(sem["s_vev2"], 1)
                    if j < NBL - 1:
                        vector.wait_ge(sem["s_mmC"], (NBL - 1) * h + j + 1)
                        dst = kvB if cur is kvA else kvA
                        nup = vstep(vector.scalar_tensor_tensor(
                            dst[:], cur[:], bd_list[h], pb[2][:, 0:HD],
                            ALU.mult, ALU.add))
                        cur = dst
                        vafter(nup)
                        vector.wait_ge(sem["s_mmat"], a + 1)
                        nkb = vstep(vector.tensor_copy(kvb[:], cur[:]))
                        vafter(nkb)
                        vector.engine_nop().then_inc(sem["s_vkv"], 1)
            vector.wait_ge(sem["s_mmcol"], NG)
            for half in range(2):
                if half == 1:
                    vector.wait_ge(sem["s_mmout"], NOI // 2)
                for tc in range(2):
                    tt = half * 2 + tc
                    vector.wait_ge(sem["s_mmsq"], (tt + 1) * NT_)
                    vector.wait_ge(sem["s_arm"], tt)
                    nr1 = vstep(vector.tensor_scalar(
                        rrow[:, 0:CW_], pb[4][0:1, :], 1.0 / D, EPS,
                        ALU.mult, ALU.add))
                    vafter(nr1)
                    vector.engine_nop().then_inc(sem["s_vrm"], 1)
                    vector.wait_ge(sem["s_arm"], tt + 1)
                    nr2 = vstep(vector.reciprocal(rrow[:, CW_:2 * CW_],
                                                  rrow[:, 2 * CW_:3 * CW_]))
                    vafter(nr2)
                    vector.engine_nop().then_inc(sem["s_vrm"], 1)
                    vector.wait_ge(sem["s_mmbc2"], tt + 1)
                    for cc in range(NT_):
                        li = tt * NT_ + cc
                        vector.wait_ge(sem[f"s_ar2{li % 2}"], 16 * (li // 2 + 1))
                        vector.wait_ge(sem[f"s_gre{li % 2}"], 16 * (li // 2 + 1))
                        vector.wait_ge(sem["s_vgn"], li)
                        ng1 = vstep(vector.scalar_tensor_tensor(
                            tmpf[:], at2[li % 2][:], nw_sb[:, cc:cc + 1],
                            pb[5][:, :], ALU.mult, ALU.mult))
                        vafter(ng1)
                        vector.tensor_mul(
                            actT[:, cc * TH + tc * CW_: cc * TH + (tc + 1) * CW_],
                            tmpf[:], gtr[li % 2][:]).then_inc(sem["s_vgn"], 1)
                for oc in range(NT_):
                    for tc in range(2):
                        oi = (half * NT_ + oc) * 2 + tc
                        vector.wait_ge(sem["s_mmout"], oi + 1)
                        vector.wait_ge(sem[f"s_g2re{oi % 2}"], 16 * (oi // 2 + 1))
                        vector.wait_ge(sem[f"s_x2re{oi % 2}"], 16 * (oi // 2 + 1))
                        if oi >= 2:
                            vector.wait_ge(sem[f"s_spo{oi % 2}"], 16 * (oi // 2))
                        vector.wait_ge(sem["s_vout"], oi)
                        nf1 = vstep(vector.tensor_mul(
                            tmpf[:], pb[oi % 2][:, :], gre[oi % 2][:]))
                        vafter(nf1)
                        vector.tensor_add(osb[oi % 2][:], tmpf[:],
                                          xre[oi % 2][:]).then_inc(sem["s_vout"], 1)

        @block.scalar
        def _(scalar):
            scalar.wait_ge(sem["s_misc"], 16 * NMISC)
            scalar.wait_ge(sem["s_vA"], 1)
            scalar.activation(lg[:, 3 * NE:4 * NE], lg[:, 2 * NE:3 * NE],
                              AF.Exp).then_inc(sem["s_aA"], 1)
            for gidx, (half, ci, kind, idx, tc) in enumerate(groups):
                e = g_evac[gidx]
                scalar.wait_ge(sem["s_mmcol"], gidx + 1)
                if e >= 4:
                    scalar.wait_ge(sem[f"s_spe{e % 4}"], 16 * (e // 4))
                b = g_bank[gidx]
                if kind == "g":
                    scalar.activation(evf[e % 4][:], pb[b][:, :],
                                      AF.Sigmoid).then_inc(sem["s_ev"], 1)
                else:
                    scalar.activation(evf[e % 4][:], pb[b][:, :], AF.Silu,
                                      bias=bc_sb[:, idx:idx + 1]).then_inc(
                        sem["s_ev"], 1)
            for h in range(H):
                for j in range(NBL):
                    a = h * NBL + j
                    scalar.wait_ge(sem["s_mmat"], a + 1)
                    if h >= 1:
                        scalar.wait_ge(sem[f"s_spa{j}"], 16 * h)
                    scalar.copy(atb[j][:], pb[1][:, 0:BS]).then_inc(
                        sem["s_aat"], 1)
            for half in range(2):
                for tc in range(2):
                    tt = half * 2 + tc
                    for cc in range(NT_):
                        li = tt * NT_ + cc
                        scalar.wait_ge(sem[f"s_are{li % 2}"], 16 * (li // 2 + 1))
                        if li >= 2:
                            scalar.wait_ge(sem["s_mmsq"], li - 1)
                        scalar.square(sqb2[li % 2][:], atr[li % 2][:]).then_inc(
                            sem["s_ach2"], 1)
                    scalar.wait_ge(sem["s_vrm"], 2 * tt + 1)
                    scalar.sqrt(rrow[:, 2 * CW_:3 * CW_],
                                rrow[:, 0:CW_]).then_inc(sem["s_arm"], 1)

        @block.tensor
        def _(tensor):
            tensor.wait_ge(sem["s_misc"], 16 * NMISC)
            tensor.matmul(pb[3][0:1, 0:NE], ones8_sb[:], zr_sb2[:],
                          start=True, stop=True).then_inc(sem["s_mmA"], 1)
            tensor.wait_ge(sem["s_vA"], 2)
            tensor.matmul(pb[3][0:NE, 8:9], lg[:, 5 * NE:6 * NE], one1[:],
                          start=True, stop=True).then_inc(sem["s_mmA"], 1)
            tensor.wait_ge(sem["s_vA"], 3)
            tensor.matmul(pb[3][:, 16:17], sel_sb[:], routT_sb[:],
                          start=True, stop=True).then_inc(sem["s_mmA"], 1)
            for half in range(2):
                tensor.wait_ge(sem[f"s_xn{half}"], 16 * NT_)
                tensor.wait_ge(sem["s_at"], 16)
                tensor.wait_ge(sem["s_vt"], 2 * half)
                for tc in range(2):
                    for dc in range(NT_):
                        ins = tensor.matmul(
                            pb[4][:, :],
                            at_sb[:, dc * 128:(dc + 1) * 128],
                            actT[:, dc * TH + tc * CW_: dc * TH + (tc + 1) * CW_],
                            start=(dc == 0), stop=(dc == NT_ - 1))
                    ins.then_inc(sem["s_mmA"], 1)
                    if tc == 0:
                        tensor.wait_ge(sem["s_vt"], 2 * half + 1)
                for ci, (kind, idx) in enumerate(ccols):
                    gbase = half * NCC * 2 + ci * 2
                    p0 = half * NCC * 2 + ci * 2
                    for piece in range(2):
                        pp = p0 + piece
                        tensor.wait_ge(sem[f"s_wc{pp % 2}"], 16 * (pp // 2 + 1))
                    if kind == "w":
                        tensor.wait_ge(sem["s_vA"], 5)
                        tensor.wait_ge(sem["s_vt"], 2 * half + 2)
                    for tc in range(2):
                        gidx = gbase + tc
                        b = g_bank[gidx]
                        pe_ = bank_prev_evac[gidx]
                        if pe_ is not None:
                            tensor.wait_ge(sem["s_ev"], pe_ + 1)
                        for dc in range(NT_):
                            wb = wcol[(p0 + dc // 16) % 2]
                            ins = tensor.matmul(
                                pb[b][:, :],
                                wb[:, (dc % 16) * 128:((dc % 16) + 1) * 128],
                                actT[:, dc * TH + tc * CW_:
                                     dc * TH + (tc + 1) * CW_],
                                start=(dc == 0),
                                stop=(dc == NT_ - 1 and kind == "g"))
                        if kind == "w":
                            ins = tensor.matmul(
                                pb[b][:, :],
                                btsc[:, idx * 128:(idx + 1) * 128],
                                t_sb[:, half * TH + tc * CW_:
                                     half * TH + (tc + 1) * CW_],
                                start=False, stop=True)
                        ins.then_inc(sem["s_mmcol"], 1)
            for h in range(H):
                sl = sem[f"s_ld{h % 2}"]
                tensor.wait_ge(sl, 16 * 6 * (h // 2 + 1))
                for j in range(NBL):
                    a = h * NBL + j
                    if a >= 1:
                        tensor.wait_ge(sem["s_vsc"], a)
                    for mi in range(2):
                        ins = tensor.matmul(
                            pb[0][:, mi * BS:(mi + 1) * BS],
                            kch[h % 2][:, j * BS + mi * 128:j * BS + (mi + 1) * 128],
                            qch[h % 2][:, j * BS:(j + 1) * BS],
                            start=True, stop=True)
                    ins.then_inc(sem["s_mmsc"], 1)
                    if a >= 2:
                        tensor.wait_ge(sem["s_vev2"], a - 1)
                    for mi in range(2):
                        tensor.transpose(
                            pt[j % 2][:, mi * 128:(mi + 1) * 128],
                            vch[h % 2][:, j * BS + mi * 128:j * BS + (mi + 1) * 128],
                            idb2[:])
                    for mi in range(2):
                        ins = tensor.transpose(
                            pt[j % 2][:, BS + mi * 128:BS + (mi + 1) * 128],
                            kch[h % 2][:, j * BS + mi * 128:j * BS + (mi + 1) * 128],
                            idb2[:])
                    ins.then_inc(sem["s_mmT2"], 1)
                    tensor.wait_ge(sem["s_vsc"], a + 1)
                    tensor.wait_ge(sem["s_vev2"], a + 1)
                    tensor.wait_ge(sem["s_vkv"], (NBL - 1) * h + h + j + 1)
                    if a >= 1:
                        tensor.wait_ge(sem["s_aat"], a)
                    for mi in range(2):
                        tensor.matmul(
                            pb[1][:, 0:BS], vtt2[:, mi * 128:(mi + 1) * 128],
                            scT[:, mi * BS:(mi + 1) * BS],
                            start=(mi == 0), stop=False)
                    tensor.matmul(pb[1][:, 0:BS], kvb[:], qdT[:],
                                  start=False, stop=True).then_inc(sem["s_mmat"], 1)
                    if j < NBL - 1:
                        if (NBL - 1) * h + j >= 1:
                            tensor.wait_ge(sem["s_vkv"],
                                           (NBL - 1) * h + h + j + 1)
                        for mi in range(2):
                            ins = tensor.matmul(
                                pb[2][:, 0:HD], kt2[:, mi * 128:(mi + 1) * 128],
                                vtt2[:, mi * 128:(mi + 1) * 128],
                                start=(mi == 0), stop=(mi == 1))
                        ins.then_inc(sem["s_mmC"], 1)
            tensor.wait_ge(sem["s_vt"], 4)
            tensor.wait_ge(sem["s_vsc"], H * NBL)
            tensor.wait_ge(sem["s_aat"], H * NBL)
            for half in range(2):
                for tc in range(2):
                    tt = half * 2 + tc
                    for cc in range(NT_):
                        li = tt * NT_ + cc
                        tensor.wait_ge(sem["s_ach2"], li + 1)
                        if cc == 0 and tt > 0:
                            tensor.wait_ge(sem["s_vrm"], 2 * tt)
                        tensor.matmul(pb[4][0:1, :], onesb2[:], sqb2[li % 2][:],
                                      start=(cc == 0), stop=(cc == NT_ - 1)
                                      ).then_inc(sem["s_mmsq"], 1)
                    tensor.wait_ge(sem["s_vrm"], 2 * tt + 2)
                    if tt > 0:
                        tensor.wait_ge(sem["s_vgn"], tt * NT_)
                    tensor.matmul(pb[5][:, :], onesf2[:], rrow[:, CW_:2 * CW_],
                                  start=True, stop=True).then_inc(sem["s_mmbc2"], 1)
                for oc in range(NT_):
                    woff = NPC + (half * NT_ + oc) * 2
                    for piece in range(2):
                        pp = woff + piece
                        tensor.wait_ge(sem[f"s_wc{pp % 2}"], 16 * (pp // 2 + 1))
                    tensor.wait_ge(sem["s_vgn"], (half + 1) * 2 * NT_)
                    for tc in range(2):
                        oi = (half * NT_ + oc) * 2 + tc
                        if oi >= 2:
                            tensor.wait_ge(sem["s_vout"], oi - 1)
                        for dc in range(NT_):
                            wb = wcol[(woff + dc // 16) % 2]
                            tensor.matmul(
                                pb[oi % 2][:, :],
                                wb[:, (dc % 16) * 128:((dc % 16) + 1) * 128],
                                actT[:, dc * TH + tc * CW_:
                                     dc * TH + (tc + 1) * CW_],
                                start=(dc == 0), stop=(dc == NT_ - 1),
                            ).then_inc(sem["s_mmout"], 1 if dc == NT_ - 1 else 0)

    return nc


def _host_prep_l2(inputs, res1):
    x = np.asarray(inputs["L0_attn_hidden_states"]).reshape(T, D)
    gwT = np.ascontiguousarray(np.asarray(inputs["L0_attn_gate_w"]).T)
    fwT = np.ascontiguousarray(np.asarray(inputs["L0_ffn_W"]).T)
    owT = np.ascontiguousarray(np.asarray(inputs["L0_attn_out_w"]).T)
    AT2 = np.ascontiguousarray(
        np.asarray(inputs["L0_ffn_A"]).reshape(NE * NR, D).T)
    BT2 = np.ascontiguousarray(
        np.asarray(inputs["L0_ffn_B"]).transpose(0, 2, 1).reshape(NE * NR, D))
    rbrow = np.asarray(inputs["L0_ffn_router_b"]).reshape(1, NE).astype(np.float32)
    norm_w = np.asarray(inputs["L0_attn_norm_w"]).astype(np.float32)
    b = np.asarray(inputs["L0_ffn_b"]).astype(np.float32)

    selm = np.zeros((NE, 128), np.float32)
    for p in range(128):
        selm[p // NR, p] = 2.0
    onessc = np.ones((1, 1), np.float32)
    ones8 = np.ones((C, 1), np.float32)
    bcols = np.ascontiguousarray(b.reshape(32, 128).T)
    nwcols = np.ascontiguousarray(norm_w.reshape(32, 128).T)

    q_decay, k_decay, diag, bd = _decays()
    kd2 = np.zeros((128, 2 * H), np.float32)
    for h in range(H):
        kd2[:, 2 * h] = k_decay[h, 0:128]
        kd2[:, 2 * h + 1] = k_decay[h, 128:256]
    qdecb = np.ascontiguousarray(
        np.broadcast_to(q_decay[:, None, :], (H, 128, BS))).astype(np.float32)
    maskT = diag.transpose(0, 2, 1)
    maskb = np.ascontiguousarray(
        maskT.reshape(H, 2, 128, BS).transpose(0, 2, 1, 3).reshape(H, 128, 2 * BS)
    ).astype(np.float32)
    identb2 = np.eye(128, dtype=BF)
    ones_bf2 = np.ones((128, 1), BF)
    ones_f2 = np.ones((1, 128), np.float32)

    S_allv = np.ascontiguousarray(
        np.stack([np.asarray(res1[c]["S_o"]) for c in range(C)]))
    zr_allT = np.ascontiguousarray(
        np.stack([np.asarray(res1[c]["zr_o"])[:, 0] for c in range(C)]))

    in_maps = []
    for c in range(C):
        coefb = np.zeros((128, H * C), np.float32)
        for h in range(H):
            for cp in range(c):
                coefb[:, h * C + cp] = bd[h] ** (NBL * (c - 1 - cp))
        in_maps.append(dict(
            xnT_i=np.asarray(res1[c]["xnT_o"]),
            qT_i=np.asarray(res1[c]["qT_o"]),
            kT_i=np.asarray(res1[c]["kT_o"]),
            vT_i=np.asarray(res1[c]["vT_o"]),
            S_all=S_allv, zr_allT=zr_allT,
            xT2=np.ascontiguousarray(x[c * TC:(c + 1) * TC].T),
            gwT=gwT, fwT=fwT, owT=owT, AT2=AT2, BT2=BT2,
            rbrow=rbrow, selm=selm, onessc=onessc, ones8=ones8,
            bcols=bcols, nwcols=nwcols, coefb=coefb, kd2=kd2,
            qdecb=qdecb, maskb=maskb, identb2=identb2,
            ones_bf2=ones_bf2, ones_f2=ones_f2,
        ))
    return in_maps


_NC_CACHE = {}


def kernel(**inputs):
    if "l1" not in _NC_CACHE:
        _NC_CACHE["l1"] = build_l1()
        _NC_CACHE["l2"] = build_l2()
    in1 = _host_prep_l1(inputs)
    res1 = run_bass_kernel_spmd(_NC_CACHE["l1"], in1,
                                core_ids=list(range(C))).results
    in2 = _host_prep_l2(inputs, res1)
    res2 = run_bass_kernel_spmd(_NC_CACHE["l2"], in2,
                                core_ids=list(range(C))).results
    out = np.concatenate(
        [np.asarray(res2[c]["outT_o"]).T for c in range(C)], axis=0)
    return out.reshape(1, T, D)


# version 03
# speedup vs baseline: 1.2331x; 1.2331x over previous
import numpy as np
import ml_dtypes
import concourse.bass as bass
import concourse.mybir as mybir
from concourse.bass_utils import run_bass_kernel_spmd
from contextlib import ExitStack

F32 = mybir.dt.float32
BF16 = mybir.dt.bfloat16
AF = mybir.ActivationFunctionType
ALU = mybir.AluOpType
AX = mybir.AxisListType

C = 8
T = 16384
TC = T // C
D = 4096
H = 32
HD = 128
BS = 256
NBL = TC // BS
NE = 8
NR = 16
Q_START, Q_END = 611, T - 1
NMASK = Q_END - Q_START
EPS = 1e-5

BF = ml_dtypes.bfloat16


def _decays():
    base = 1.0 / 2.0 ** (8.0 / H)
    factor = 1.0 - 1.0 / (31 + 1e-5) + 1e-5
    slope = (base ** (np.arange(H, dtype=np.float64) + 1.0)) * factor
    r = np.arange(BS, dtype=np.float64) + 1.0
    q_decay = np.exp(-slope[:, None] * r[None, :])
    k_decay = np.exp(-slope[:, None] * (BS - r)[None, :])
    diff = r[:, None] - r[None, :]
    diag = np.where(diff >= 0, np.exp(-slope[:, None, None] * diff[None]), 0.0)
    block_decay = np.exp(-slope * BS)
    return (q_decay.astype(np.float32), k_decay.astype(np.float32),
            diag.astype(np.float32), block_decay)



def build_l1():
    nc = bass.Bass("TRN2", target_bir_lowering=False, num_devices=C)

    xT = nc.dram_tensor("xT", [D, TC], F32, kind="ExternalInput").ap()
    WqT = nc.dram_tensor("WqT", [D, D], F32, kind="ExternalInput").ap()
    WkT = nc.dram_tensor("WkT", [D, D], F32, kind="ExternalInput").ap()
    WvT = nc.dram_tensor("WvT", [D, D], F32, kind="ExternalInput").ap()
    rWT = nc.dram_tensor("rWT", [D, NE], F32, kind="ExternalInput").ap()
    negm = nc.dram_tensor("negm", [NE, 2], F32, kind="ExternalInput").ap()
    kdext = nc.dram_tensor("kdext", [H, 16, 128], F32, kind="ExternalInput").ap()
    ident16 = nc.dram_tensor("ident16", [16, 16], F32, kind="ExternalInput").ap()
    ones_bf = nc.dram_tensor("ones_bf", [128, 1], BF16, kind="ExternalInput").ap()
    ones_f = nc.dram_tensor("ones_f", [1, 128], F32, kind="ExternalInput").ap()
    identb = nc.dram_tensor("identb", [128, 128], BF16, kind="ExternalInput").ap()

    xnT_o = nc.dram_tensor("xnT_o", [D, TC], BF16, kind="ExternalOutput").ap()
    qT_o = nc.dram_tensor("qT_o", [D, TC], BF16, kind="ExternalOutput").ap()
    kT_o = nc.dram_tensor("kT_o", [D, TC], BF16, kind="ExternalOutput").ap()
    vT_o = nc.dram_tensor("vT_o", [D, TC], BF16, kind="ExternalOutput").ap()
    S_o = nc.dram_tensor("S_o", [H * HD, HD], F32, kind="ExternalOutput").ap()
    zr_o = nc.dram_tensor("zr_o", [NE, 1], F32, kind="ExternalOutput").ap()

    NT = 32
    NTC = 4
    CW = 512

    cols = []
    for h in range(H):
        cols.append(("k", h))
        cols.append(("v", h))
    for h in range(H):
        cols.append(("q", h))
    NCOL = len(cols)
    w_src = {"q": WqT, "k": WkT, "v": WvT}
    out_t = {"q": qT_o, "k": kT_o, "v": vT_o}
    chain_at = {}
    for h in range(H):
        ci = min(2 * h + 5, NCOL - 1)
        chain_at.setdefault(ci, []).append(h)
    s_export_at = {}
    for h in range(H):
        ci = min(2 * h + 10, NCOL - 1)
        s_export_at.setdefault(ci, []).append(h)

    with ExitStack() as ctx:
        E = ctx.enter_context
        xnT = E(nc.sbuf_tensor("xnT", [128, NT * TC], BF16))
        xbuf = [E(nc.sbuf_tensor(f"xbuf{i}", [128, CW], F32)) for i in range(2)]
        xbf = [E(nc.sbuf_tensor(f"xbf{i}", [128, CW], BF16)) for i in range(2)]
        sqb = [E(nc.sbuf_tensor(f"sqb{i}", [128, CW], BF16)) for i in range(2)]
        tmpb = E(nc.sbuf_tensor("tmpb", [128, CW], F32))
        mrow = E(nc.sbuf_tensor("mrow", [1, CW], F32))
        erow = E(nc.sbuf_tensor("erow", [1, CW], F32))
        msq = E(nc.sbuf_tensor("msq", [1, CW], F32))
        vrow = E(nc.sbuf_tensor("vrow", [1, CW], F32))
        srow = E(nc.sbuf_tensor("srow", [1, CW], F32))
        rstd = E(nc.sbuf_tensor("rstd", [1, CW], F32))
        mrstd = E(nc.sbuf_tensor("mrstd", [1, CW], F32))
        ones_sb = E(nc.sbuf_tensor("ones_sb", [128, 1], BF16))
        onesf_sb = E(nc.sbuf_tensor("onesf_sb", [1, 128], F32))
        wbuf = [E(nc.sbuf_tensor(f"wbuf{i}", [128, NT * 128], BF16)) for i in range(2)]
        evb = [E(nc.sbuf_tensor(f"evb{i}", [128, CW], BF16)) for i in range(8)]
        rw_sb = E(nc.sbuf_tensor("rw_sb", [128, NT * NE], BF16))
        negm_sb = E(nc.sbuf_tensor("negm_sb", [NE, 2], F32))
        zparts = E(nc.sbuf_tensor("zparts", [NE, NTC + 4], F32))
        zr_sb = E(nc.sbuf_tensor("zr_sb", [NE, 1], F32))
        kdnat = [E(nc.sbuf_tensor(f"kdnat{i}", [16, 128], F32)) for i in range(2)]
        kdcols = E(nc.sbuf_tensor("kdcols", [128, 16], F32))
        id16 = E(nc.sbuf_tensor("id16", [16, 16], F32))
        kch = [E(nc.sbuf_tensor(f"kch{i}", [128, 16 * 128], BF16)) for i in range(2)]
        vch = [E(nc.sbuf_tensor(f"vch{i}", [128, 16 * 128], BF16)) for i in range(2)]
        kts = E(nc.sbuf_tensor("kts", [128, 16 * 128], BF16))
        vtt = E(nc.sbuf_tensor("vtt", [128, 16 * 128], BF16))
        idb = E(nc.sbuf_tensor("idb", [128, 128], BF16))
        S_sb = [E(nc.sbuf_tensor(f"S_sb{i}", [128, HD], F32)) for i in range(2)]
        dumcol = E(nc.sbuf_tensor("dumcol", [128, 1], F32))
        pb = [E(nc.psum_tensor(f"pb{i}", [128, 512], F32)) for i in range(6)]
        pt = [E(nc.psum_tensor(f"pt{i}", [128, 1024], BF16)) for i in range(2)]
        names = (["s_misc", "s_x0", "s_x1", "s_p1v", "s_p1a", "s_mmst", "s_vvar",
                  "s_astd", "s_vrs", "s_mmbc", "s_vxn", "s_wc0", "s_wc1", "s_rw",
                  "s_mmcol", "s_ev", "s_spx", "s_mmT", "s_kdn0", "s_kdn1",
                  "s_mmkd", "s_vkd", "s_vev", "s_kv0", "s_kv1", "s_mmS",
                  "s_vS", "s_spS0", "s_spS1", "s_mmz", "s_vz", "s_vzf", "s_spz"]
                 + [f"s_spq{i}" for i in range(8)] + ["s_vch"])
        sem = {n: E(nc.semaphore(n)) for n in names}
        block = E(nc.Block())

        def xl_idx(tc, p, cc):
            return tc * 64 + p * 32 + cc

        def wait_xload(eng, tc, p, cc):
            L = xl_idx(tc, p, cc)
            eng.wait_ge(sem[f"s_x{L % 2}"], 16 * (L // 2 + 1))

        def eidx(ci, tc):
            return 4 * ci + tc

        @block.sync
        def _(sync):
            sync.dma_start(out=ones_sb[:], in_=ones_bf).then_inc(sem["s_misc"], 16)
            sync.dma_start(out=onesf_sb[:], in_=ones_f).then_inc(sem["s_misc"], 16)
            sync.dma_start(out=id16[:], in_=ident16).then_inc(sem["s_misc"], 16)
            sync.dma_start(out=negm_sb[:], in_=negm).then_inc(sem["s_misc"], 16)
            sync.dma_start(out=idb[:], in_=identb).then_inc(sem["s_misc"], 16)
            for tc in range(NTC):
                for p in range(2):
                    for cc in range(NT):
                        li = tc * 64 + p * 32 + cc
                        if li >= 2:
                            ptc, pp, pcc = (li - 2) // 64, ((li - 2) % 64) // 32, (li - 2) % 32
                            if pp == 0:
                                sync.wait_ge(sem["s_p1v"], ptc * 32 + pcc + 1)
                                sync.wait_ge(sem["s_p1a"], ptc * 32 + pcc + 1)
                            else:
                                sync.wait_ge(sem["s_vxn"], ptc * 32 + pcc + 1)
                        sync.dma_start(
                            out=xbuf[li % 2][:],
                            in_=xT[cc * 128:(cc + 1) * 128, tc * CW:(tc + 1) * CW],
                        ).then_inc(sem[f"s_x{li % 2}"], 16)
            for cc in range(NT):
                sync.wait_ge(sem["s_vxn"], 3 * 32 + cc + 1)
                sync.dma_start(
                    out=xnT_o[cc * 128:(cc + 1) * 128, :],
                    in_=xnT[:, cc * TC:(cc + 1) * TC],
                ).then_inc(sem["s_spx"], 16)
            for ci, (kind, h) in enumerate(cols):
                for hh in s_export_at.get(ci, []):
                    sync.wait_ge(sem["s_vS"], hh + 1)
                    sync.dma_start(
                        out=S_o[hh * HD:(hh + 1) * HD, :], in_=S_sb[hh % 2][:]
                    ).then_inc(sem[f"s_spS{hh % 2}"], 16)
                for tc in range(NTC):
                    e = eidx(ci, tc)
                    sync.wait_ge(sem["s_ev"], e + 1)
                    sync.dma_start(
                        out=out_t[kind][h * 128:(h + 1) * 128, tc * CW:(tc + 1) * CW],
                        in_=evb[e % 8][:],
                    ).then_inc(sem[f"s_spq{e % 8}"], 16)
                if kind == "v":
                    hh = h
                    if hh >= 2:
                        sync.wait_ge(sem["s_vkd"], hh - 1)
                    sync.dma_start(out=kdnat[hh % 2][:], in_=kdext[hh]).then_inc(
                        sem[f"s_kdn{hh % 2}"], 16)
                    for sl in range(8):
                        sync.wait_ge(sem[f"s_spq{sl}"], 16 * (hh + 1))
                    if hh >= 2:
                        sync.wait_ge(sem["s_mmS"], hh - 1)
                    sync.dma_start(out=kch[hh % 2][:],
                                   in_=kT_o[hh * 128:(hh + 1) * 128, :]).then_inc(
                        sem[f"s_kv{hh % 2}"], 16)
                    sync.dma_start(out=vch[hh % 2][:],
                                   in_=vT_o[hh * 128:(hh + 1) * 128, :]).then_inc(
                        sem[f"s_kv{hh % 2}"], 16)
            sync.wait_ge(sem["s_vzf"], 1)
            sync.dma_start(out=zr_o, in_=zr_sb[:]).then_inc(sem["s_spz"], 16)
            for sl in range(8):
                sync.wait_ge(sem[f"s_spq{sl}"], 16 * (4 * NCOL // 8))
            sync.wait_ge(sem["s_spx"], 16 * NT)
            sync.wait_ge(sem["s_spS0"], 16 * (H // 2))
            sync.wait_ge(sem["s_spS1"], 16 * (H // 2))
            sync.wait_ge(sem["s_spz"], 16)

        @block.gpsimd
        def _(gpsimd):
            gpsimd.dma_start(
                out=rw_sb[:].rearrange("p (c e) -> p c e", c=NT),
                in_=rWT.rearrange("(c p) e -> p c e", p=128),
            ).then_inc(sem["s_rw"], 16)
            for ci, (kind, h) in enumerate(cols):
                if ci >= 2:
                    gpsimd.wait_ge(sem["s_mmcol"], 4 * (ci - 1))
                src = w_src[kind][:, h * 128:(h + 1) * 128]
                gpsimd.dma_start(
                    out=wbuf[ci % 2][:].rearrange("p (c o) -> p c o", c=NT),
                    in_=src.rearrange("(c p) o -> p c o", p=128),
                ).then_inc(sem[f"s_wc{ci % 2}"], 16)

        @block.vector
        def _(vector):
            vn = [0]

            def vstep(ins):
                vn[0] += 1
                ins.then_inc(sem["s_vch"], 1)
                return vn[0]

            def vafter(n):
                vector.wait_ge(sem["s_vch"], n)

            vector.wait_ge(sem["s_misc"], 80)
            for tc in range(NTC):
                for cc in range(NT):
                    wait_xload(vector, tc, 0, cc)
                    if tc * 32 + cc >= 2:
                        vector.wait_ge(sem["s_mmst"], tc * 32 + cc - 1)
                    vector.tensor_copy(xbf[cc % 2][:], xbuf[cc % 2][:]).then_inc(
                        sem["s_p1v"], 1)
                vector.wait_ge(sem["s_mmst"], 32 * (tc + 1))
                n_m = vstep(vector.tensor_scalar_mul(mrow[:], pb[0][0:1, :], 1.0 / D))
                vector.tensor_scalar_mul(erow[:], pb[1][0:1, :], 1.0 / D).then_inc(
                    sem["s_vvar"], 1)
                vafter(n_m)
                n_q = vstep(vector.tensor_mul(msq[:], mrow[:], mrow[:]))
                vafter(n_q)
                vector.scalar_tensor_tensor(
                    vrow[:], erow[:], EPS, msq[:], ALU.add, ALU.subtract
                ).then_inc(sem["s_vvar"], 1)
                vector.wait_ge(sem["s_astd"], tc + 1)
                n_r = vstep(vector.reciprocal(rstd[:], srow[:]))
                vafter(n_r)
                n_mr = vstep(vector.tensor_mul(mrstd[:], mrow[:], rstd[:]))
                vafter(n_mr)
                vector.engine_nop().then_inc(sem["s_vrs"], 1)
                for cc in range(NT):
                    wait_xload(vector, tc, 1, cc)
                    vector.wait_ge(sem["s_mmbc"], 2 * (tc + 1))
                    vector.wait_ge(sem["s_vxn"], tc * 32 + cc)
                    n_t = vstep(vector.tensor_mul(tmpb[:], xbuf[cc % 2][:], pb[2][:, :]))
                    vafter(n_t)
                    vector.tensor_tensor(
                        xnT[:, cc * TC + tc * CW: cc * TC + (tc + 1) * CW],
                        tmpb[:], pb[3][:, :], ALU.subtract,
                    ).then_inc(sem["s_vxn"], 1)
            zn = {}
            for tc in range(NTC):
                vector.wait_ge(sem["s_mmz"], tc + 1)
                zn[tc] = vstep(vector.tensor_reduce(
                    zparts[:, tc:tc + 1], pb[0][0:NE, :], AX.X, ALU.add))
                if tc == 1:
                    zn["a"] = vstep(vector.tensor_reduce(
                        zparts[:, 4:5], pb[0][0:NE, 0:99], AX.X, ALU.add))
                if tc == 3:
                    zn["b"] = vstep(vector.tensor_copy(
                        zparts[:, 5:6], pb[0][0:NE, 511:512]))
                vector.engine_nop().then_inc(sem["s_vz"], 1)
            vafter(max(zn.values()))
            n1 = vstep(vector.tensor_reduce(zparts[:, 6:7], zparts[:, 0:NTC],
                                            AX.X, ALU.add))
            n2 = vstep(vector.tensor_add(zparts[:, 7:8], zparts[:, 0:1],
                                         zparts[:, 4:5]))
            vafter(n2)
            n3 = vstep(vector.scalar_tensor_tensor(
                zparts[:, 6:7], zparts[:, 7:8], negm_sb[:, 0:1], zparts[:, 6:7],
                ALU.mult, ALU.add))
            vafter(n3)
            vector.scalar_tensor_tensor(
                zr_sb[:], zparts[:, 5:6], negm_sb[:, 1:2], zparts[:, 6:7],
                ALU.mult, ALU.add).then_inc(sem["s_vzf"], 1)
            for h in range(H):
                vector.wait_ge(sem["s_mmkd"], h + 1)
                n_kd = vstep(vector.tensor_copy(kdcols[:], pb[2][:, 0:16]))
                vafter(n_kd)
                vector.engine_nop().then_inc(sem["s_vkd"], 1)
                for c2 in range(16):
                    vector.wait_ge(sem["s_mmT"], 16 * h + c2 + 1)
                    bank = pt[c2 % 2]
                    vector.tensor_scalar_mul(
                        kts[:, c2 * 128:(c2 + 1) * 128],
                        bank[:, 0:128], kdcols[:, c2:c2 + 1])
                    vector.tensor_copy(
                        vtt[:, c2 * 128:(c2 + 1) * 128],
                        bank[:, 128:256]).then_inc(sem["s_vev"], 1)
                vector.wait_ge(sem["s_mmS"], h + 1)
                if h >= 2:
                    vector.wait_ge(sem[f"s_spS{h % 2}"], 16 * (h // 2))
                n_s = vstep(vector.tensor_copy(S_sb[h % 2][:], pb[1][:, 0:128]))
                vafter(n_s)
                vector.engine_nop().then_inc(sem["s_vS"], 1)

        @block.scalar
        def _(scalar):
            for tc in range(NTC):
                for cc in range(NT):
                    wait_xload(scalar, tc, 0, cc)
                    if tc * 32 + cc >= 2:
                        scalar.wait_ge(sem["s_mmst"], tc * 32 + cc - 1)
                    scalar.square(sqb[cc % 2][:], xbuf[cc % 2][:]).then_inc(
                        sem["s_p1a"], 1)
                scalar.wait_ge(sem["s_vvar"], 2 * (tc + 1))
                scalar.sqrt(srow[:], vrow[:]).then_inc(sem["s_astd"], 1)
            for ci in range(NCOL):
                for tc in range(NTC):
                    e = eidx(ci, tc)
                    scalar.wait_ge(sem["s_mmcol"], e + 1)
                    if e >= 8:
                        scalar.wait_ge(sem[f"s_spq{e % 8}"], 16 * (e // 8))
                    scalar.activation(evb[e % 8][:], pb[4 + e % 2][:, :],
                                      AF.Silu).then_inc(sem["s_ev"], 1)

        @block.tensor
        def _(tensor):
            tensor.wait_ge(sem["s_misc"], 80)
            for tc in range(NTC):
                if tc > 0:
                    tensor.wait_ge(sem["s_vvar"], 2 * tc)
                for cc in range(NT):
                    tensor.wait_ge(sem["s_p1v"], tc * 32 + cc + 1)
                    tensor.wait_ge(sem["s_p1a"], tc * 32 + cc + 1)
                    st = cc == 0
                    sp = cc == NT - 1
                    tensor.matmul(pb[0][0:1, :], ones_sb[:], xbf[cc % 2][:],
                                  start=st, stop=sp)
                    tensor.matmul(pb[1][0:1, :], ones_sb[:], sqb[cc % 2][:],
                                  start=st, stop=sp).then_inc(sem["s_mmst"], 1)
                tensor.wait_ge(sem["s_vrs"], tc + 1)
                if tc > 0:
                    tensor.wait_ge(sem["s_vxn"], 32 * tc)
                tensor.matmul(pb[2][:, :], onesf_sb[:], rstd[:], start=True, stop=True)
                tensor.matmul(pb[3][:, :], onesf_sb[:], mrstd[:],
                              start=True, stop=True).then_inc(sem["s_mmbc"], 2)
            for ci, (kind, h) in enumerate(cols):
                tensor.wait_ge(sem[f"s_wc{ci % 2}"], 16 * (ci // 2 + 1))
                for tc in range(NTC):
                    g = 4 * ci + tc
                    if g >= 2:
                        tensor.wait_ge(sem["s_ev"], g - 1)
                    for dc in range(NT):
                        if ci < 3:
                            tensor.wait_ge(sem["s_vxn"],
                                           min(tc * 32 + dc + 1, 32 * NTC))
                        ins = tensor.matmul(
                            pb[4 + g % 2][:, :],
                            wbuf[ci % 2][:, dc * 128:(dc + 1) * 128],
                            xnT[:, dc * TC + tc * CW: dc * TC + (tc + 1) * CW],
                            start=(dc == 0), stop=(dc == NT - 1),
                        )
                        if dc == NT - 1:
                            ins.then_inc(sem["s_mmcol"], 1)
                if ci == 4:
                    tensor.wait_ge(sem["s_rw"], 16)
                    tensor.wait_ge(sem["s_vvar"], 2 * NTC - 1)
                    for tc in range(NTC):
                        if tc > 0:
                            tensor.wait_ge(sem["s_vz"], tc)
                        tensor.wait_ge(sem["s_vxn"], 32 * (tc + 1))
                        for dc in range(NT):
                            ins = tensor.matmul(
                                pb[0][0:NE, :],
                                rw_sb[:, dc * NE:(dc + 1) * NE],
                                xnT[:, dc * TC + tc * CW: dc * TC + (tc + 1) * CW],
                                start=(dc == 0), stop=(dc == NT - 1),
                            )
                            if dc == NT - 1:
                                ins.then_inc(sem["s_mmz"], 1)
                for hh in chain_at.get(ci, []):
                    tensor.wait_ge(sem[f"s_kdn{hh % 2}"], 16 * (hh // 2 + 1))
                    tensor.wait_ge(sem["s_vkd"], hh)
                    if hh == 0:
                        tensor.wait_ge(sem["s_vxn"], 32 * NTC)
                        tensor.wait_ge(sem["s_vvar"], 2 * NTC)
                    tensor.transpose(pb[2][:, 0:16], kdnat[hh % 2][:],
                                     id16[:]).then_inc(sem["s_mmkd"], 1)
                    tensor.wait_ge(sem[f"s_kv{hh % 2}"], 32 * (hh // 2 + 1))
                    for c2 in range(16):
                        if 16 * hh + c2 >= 2:
                            tensor.wait_ge(sem["s_vev"], 16 * hh + c2 - 1)
                        bank = pt[c2 % 2]
                        tensor.transpose(bank[:, 0:128],
                                         kch[hh % 2][:, c2 * 128:(c2 + 1) * 128],
                                         idb[:])
                        tensor.transpose(bank[:, 128:256],
                                         vch[hh % 2][:, c2 * 128:(c2 + 1) * 128],
                                         idb[:]).then_inc(sem["s_mmT"], 1)
                    tensor.wait_ge(sem["s_vev"], 16 * (hh + 1))
                    if hh >= 1:
                        tensor.wait_ge(sem["s_vS"], hh)
                    for c2 in range(16):
                        ins = tensor.matmul(
                            pb[1][:, 0:128],
                            kts[:, c2 * 128:(c2 + 1) * 128],
                            vtt[:, c2 * 128:(c2 + 1) * 128],
                            start=(c2 == 0), stop=(c2 == 15),
                        )
                        if c2 == 15:
                            ins.then_inc(sem["s_mmS"], 1)

    return nc



def _host_prep_l1(inputs):
    x = np.asarray(inputs["L0_attn_hidden_states"]).reshape(T, D)
    qkv_w = np.asarray(inputs["L0_attn_qkv_w"])
    w3 = qkv_w.reshape(H, 3, HD, D)
    Wq = w3[:, 0].reshape(H * HD, D)
    Wk = w3[:, 1].reshape(H * HD, D)
    Wv = w3[:, 2].reshape(H * HD, D)
    WqT = np.ascontiguousarray(Wq.T)
    WkT = np.ascontiguousarray(Wk.T)
    WvT = np.ascontiguousarray(Wv.T)
    rWT = np.ascontiguousarray(np.asarray(inputs["L0_ffn_router_W"]).T)

    q_decay, k_decay, diag, block_decay = _decays()
    t_ = np.arange(TC)
    kdfull = (k_decay[:, t_ % BS] *
              (block_decay[:, None] ** (NBL - 1 - t_ // BS)[None, :])).astype(np.float32)
    kdext = np.ascontiguousarray(kdfull.reshape(H, 16, 128))

    ident16 = np.eye(16, dtype=np.float32)
    identb = np.eye(128, dtype=BF)
    ones_bf = np.ones((128, 1), BF)
    ones_f = np.ones((1, 128), np.float32)

    in_maps = []
    for c in range(C):
        sl = slice(c * TC, (c + 1) * TC)
        mA = 1.0 if c == 0 else 0.0
        mB = 1.0 if c == C - 1 else 0.0
        negm = np.broadcast_to(
            np.array([-mA, -mB], np.float32), (NE, 2)).copy()
        in_maps.append(dict(
            xT=np.ascontiguousarray(x[sl].T),
            WqT=WqT, WkT=WkT, WvT=WvT, rWT=rWT,
            negm=negm, kdext=kdext, ident16=ident16, identb=identb,
            ones_bf=ones_bf, ones_f=ones_f,
        ))
    return in_maps





def build_l2():
    nc = bass.Bass("TRN2", target_bir_lowering=False, num_devices=C)

    xnT_i = nc.dram_tensor("xnT_i", [D, TC], BF16, kind="ExternalInput").ap()
    qT_i = nc.dram_tensor("qT_i", [D, TC], BF16, kind="ExternalInput").ap()
    kT_i = nc.dram_tensor("kT_i", [D, TC], BF16, kind="ExternalInput").ap()
    vT_i = nc.dram_tensor("vT_i", [D, TC], BF16, kind="ExternalInput").ap()
    S_all = nc.dram_tensor("S_all", [C, H * HD, HD], F32, kind="ExternalInput").ap()
    zr_allT = nc.dram_tensor("zr_allT", [C, NE], F32, kind="ExternalInput").ap()
    xT2 = nc.dram_tensor("xT2", [D, TC], F32, kind="ExternalInput").ap()
    gwT = nc.dram_tensor("gwT", [D, D], F32, kind="ExternalInput").ap()
    fwT = nc.dram_tensor("fwT", [D, D], F32, kind="ExternalInput").ap()
    owT = nc.dram_tensor("owT", [D, D], F32, kind="ExternalInput").ap()
    AT2 = nc.dram_tensor("AT2", [D, 128], F32, kind="ExternalInput").ap()
    BT2 = nc.dram_tensor("BT2", [128, D], F32, kind="ExternalInput").ap()
    rbrow = nc.dram_tensor("rbrow", [1, NE], F32, kind="ExternalInput").ap()
    selm = nc.dram_tensor("selm", [NE, 128], F32, kind="ExternalInput").ap()
    onessc = nc.dram_tensor("onessc", [1, 1], F32, kind="ExternalInput").ap()
    ones8 = nc.dram_tensor("ones8", [C, 1], F32, kind="ExternalInput").ap()
    bcols = nc.dram_tensor("bcols", [128, 32], F32, kind="ExternalInput").ap()
    nwcols = nc.dram_tensor("nwcols", [128, 32], F32, kind="ExternalInput").ap()
    coefb = nc.dram_tensor("coefb", [128, H * C], F32, kind="ExternalInput").ap()
    kd2 = nc.dram_tensor("kd2", [128, 2 * H], F32, kind="ExternalInput").ap()
    qdecb = nc.dram_tensor("qdecb", [H, 128, BS], F32, kind="ExternalInput").ap()
    maskb = nc.dram_tensor("maskb", [H, 128, 2 * BS], F32, kind="ExternalInput").ap()
    identb2 = nc.dram_tensor("identb2", [128, 128], BF16, kind="ExternalInput").ap()
    ones_bf2 = nc.dram_tensor("ones_bf2", [128, 1], BF16, kind="ExternalInput").ap()
    ones_f2 = nc.dram_tensor("ones_f2", [1, 128], F32, kind="ExternalInput").ap()

    outT_o = nc.dram_tensor("outT_o", [D, TC], F32, kind="ExternalOutput").ap()

    gate_hb = nc.dram_tensor("gate_hb", [D, TC], F32)
    g_hb = nc.dram_tensor("g_hb", [D, TC], F32)
    attn_hb = nc.dram_tensor("attn_hb", [D, TC], BF16)

    NT_ = 32
    TH = TC // 2
    CW_ = 512
    _, _, _, bd_arr = _decays()
    bd_list = [float(b) for b in bd_arr]

    ccols = [("g", i) for i in range(NT_)] + [("w", i) for i in range(NT_)]
    NCC = len(ccols)
    groups = []
    for half in range(2):
        for ci, (kind, idx) in enumerate(ccols):
            for tc in range(2):
                groups.append((half, ci, kind, idx, tc))
    NG = len(groups)
    g_bank = {}
    g_evac = {}
    bank_prev_evac = {}
    e_ctr = 0
    bank_last = {}
    for gidx, (half, ci, kind, idx, tc) in enumerate(groups):
        b = 2 * (ci % 2) + tc
        g_bank[gidx] = b
        g_evac[gidx] = e_ctr
        bank_prev_evac[gidx] = bank_last.get(b)
        bank_last[b] = e_ctr
        e_ctr += 1
    NOI = 2 * NT_ * 2

    NPC = 2 * NCC * 2

    with ExitStack() as ctx:
        E = ctx.enter_context
        actT = E(nc.sbuf_tensor("actT", [128, NT_ * TH], BF16))
        wcol = [E(nc.sbuf_tensor(f"wcol{i}", [128, NT_ * 128], BF16)) for i in range(3)]
        at_sb = E(nc.sbuf_tensor("at_sb", [128, NT_ * 128], BF16))
        t_sb = E(nc.sbuf_tensor("t_sb", [128, TC], BF16))
        btsc = E(nc.sbuf_tensor("btsc", [128, D], BF16))
        evf = [E(nc.sbuf_tensor(f"evf{i}", [128, CW_], F32)) for i in range(4)]
        zr_sb2 = E(nc.sbuf_tensor("zr_sb2", [C, NE], F32))
        rb_sb = E(nc.sbuf_tensor("rb_sb", [1, NE], F32))
        sel_sb = E(nc.sbuf_tensor("sel_sb", [NE, 128], F32))
        one1 = E(nc.sbuf_tensor("one1", [1, 1], F32))
        ones8_sb = E(nc.sbuf_tensor("ones8_sb", [C, 1], F32))
        lg = E(nc.sbuf_tensor("lg", [1, 6 * NE], F32))
        routT_sb = E(nc.sbuf_tensor("routT_sb", [NE, 1], F32))
        rout_col = E(nc.sbuf_tensor("rout_col", [128, 1], F32))
        qch = [E(nc.sbuf_tensor(f"qch2{i}", [128, TC], BF16)) for i in range(2)]
        kch = [E(nc.sbuf_tensor(f"kch2{i}", [128, TC], BF16)) for i in range(2)]
        vch = [E(nc.sbuf_tensor(f"vch2{i}", [128, TC], BF16)) for i in range(2)]
        sal = [E(nc.sbuf_tensor(f"sal{i}", [128, C * HD], F32)) for i in range(2)]
        msk = [E(nc.sbuf_tensor(f"msk{i}", [128, 2 * BS], F32)) for i in range(2)]
        qdc = [E(nc.sbuf_tensor(f"qdc{i}", [128, BS], F32)) for i in range(2)]
        kd2_sb = E(nc.sbuf_tensor("kd2_sb", [128, 2 * H], F32))
        kvA = [E(nc.sbuf_tensor(f"kvA{i}", [128, HD], F32)) for i in range(2)]
        kvB = [E(nc.sbuf_tensor(f"kvB{i}", [128, HD], F32)) for i in range(2)]
        kvb = [E(nc.sbuf_tensor(f"kvb{i}", [128, HD], BF16)) for i in range(2)]
        scT = [E(nc.sbuf_tensor(f"scT{i}", [128, 2 * BS], BF16)) for i in range(4)]
        qdT = [E(nc.sbuf_tensor(f"qdT{i}", [128, BS], BF16)) for i in range(4)]
        vtt2 = [E(nc.sbuf_tensor(f"vtt2{i}", [128, BS], BF16)) for i in range(4)]
        kt2 = [E(nc.sbuf_tensor(f"kt2{i}", [128, BS], BF16)) for i in range(4)]
        atb = [E(nc.sbuf_tensor(f"atb{i}", [128, BS], BF16)) for i in range(16)]
        idb2 = E(nc.sbuf_tensor("idb2", [128, 128], BF16))
        onesb2 = E(nc.sbuf_tensor("onesb2", [128, 1], BF16))
        onesf2 = E(nc.sbuf_tensor("onesf2", [1, 128], F32))
        bc_sb = E(nc.sbuf_tensor("bc_sb", [128, NT_], F32))
        nw_sb = E(nc.sbuf_tensor("nw_sb", [128, NT_], F32))
        cf_sb = E(nc.sbuf_tensor("cf_sb", [128, H * C], F32))
        atr = [E(nc.sbuf_tensor(f"atr{i}", [128, CW_], BF16)) for i in range(2)]
        at2 = [E(nc.sbuf_tensor(f"at2_{i}", [128, CW_], BF16)) for i in range(2)]
        sqb2 = [E(nc.sbuf_tensor(f"sqb2{i}", [128, CW_], BF16)) for i in range(2)]
        gtr = [E(nc.sbuf_tensor(f"gtr{i}", [128, CW_], F32)) for i in range(2)]
        rrow = E(nc.sbuf_tensor("rrow", [1, 3 * CW_], F32))
        tmpf = E(nc.sbuf_tensor("tmpf", [128, CW_], F32))
        gre = [E(nc.sbuf_tensor(f"gre{i}", [128, CW_], F32)) for i in range(2)]
        xre = [E(nc.sbuf_tensor(f"xre{i}", [128, CW_], F32)) for i in range(2)]
        osb = [E(nc.sbuf_tensor(f"osb{i}", [128, CW_], F32)) for i in range(2)]
        pb = [E(nc.psum_tensor(f"pb{i}", [128, 512], F32)) for i in range(6)]
        pt = [E(nc.psum_tensor(f"pt{i}", [128, 1024], BF16)) for i in range(2)]
        names = (["s_misc", "s_at", "s_xn0", "s_xn1", "s_wc0", "s_wc1", "s_wc2", "s_bt",
                  "s_mmA", "s_vA", "s_aA", "s_vt",
                  "s_mmcol", "s_ev",
                  "s_spe0", "s_spe1", "s_spe2", "s_spe3",
                  "s_ld0", "s_ld1", "s_vkv0", "s_vkv1",
                  "s_mmsc0", "s_mmsc1", "s_vsc0", "s_vsc1",
                  "s_mmT20", "s_mmT21", "s_aev0", "s_aev1",
                  "s_mmat0", "s_mmat1", "s_aat0", "s_aat1",
                  "s_mmC0", "s_mmC1",
                  "s_are0", "s_are1", "s_ar20", "s_ar21", "s_gre0", "s_gre1",
                  "s_mmsq", "s_vrm", "s_arm", "s_mmbc2", "s_vgn",
                  "s_g2re0", "s_g2re1", "s_x2re0", "s_x2re1",
                  "s_mmout", "s_vout", "s_spo0", "s_spo1",
                  "s_vch2", "s_ach2"] + [f"s_spa{i}" for i in range(16)])
        sem = {n: E(nc.semaphore(n)) for n in names}
        block = E(nc.Block())

        NMISC = 12

        @block.sync
        def _(sync):
            for ap_, sb_ in ((zr_allT, zr_sb2), (rbrow, rb_sb), (selm, sel_sb),
                             (onessc, one1), (ones8, ones8_sb), (bcols, bc_sb),
                             (nwcols, nw_sb), (coefb, cf_sb), (kd2, kd2_sb),
                             (identb2, idb2), (ones_bf2, onesb2), (ones_f2, onesf2)):
                sync.dma_start(out=sb_[:], in_=ap_).then_inc(sem["s_misc"], 16)
            for half in range(2):
                if half == 1:
                    sync.wait_ge(sem["s_mmcol"], NG // 2)
                    sync.wait_ge(sem["s_mmA"], 5)
                for cc in range(NT_):
                    sync.dma_start(
                        out=actT[:, cc * TH:(cc + 1) * TH],
                        in_=xnT_i[cc * 128:(cc + 1) * 128,
                                  half * TH:(half + 1) * TH],
                    ).then_inc(sem[f"s_xn{half}"], 16)
                for gidx, (ghalf, ci, kind, idx, tc) in enumerate(groups):
                    if ghalf != half:
                        continue
                    e = g_evac[gidx]
                    sync.wait_ge(sem["s_ev"], e + 1)
                    tgt = gate_hb if kind == "g" else g_hb
                    sync.dma_start(
                        out=tgt[idx * 128:(idx + 1) * 128,
                                half * TH + tc * CW_:
                                half * TH + (tc + 1) * CW_],
                        in_=evf[e % 4][:],
                    ).then_inc(sem[f"s_spe{e % 4}"], 16)
            def attn_exports(hh):
                pi_ = hh % 2
                r_ = hh // 2
                for j in range(NBL):
                    sync.wait_ge(sem[f"s_aat{pi_}"], r_ * NBL + j + 1)
                    sync.dma_start(
                        out=attn_hb[hh * 128:(hh + 1) * 128,
                                    j * BS:(j + 1) * BS],
                        in_=atb[pi_ * 8 + j][:],
                    ).then_inc(sem[f"s_spa{pi_ * 8 + j}"], 16)
            for h in range(H):
                pi = h % 2
                r = h // 2
                if h >= 2:
                    sync.wait_ge(sem[f"s_mmat{pi}"], NBL * r)
                    sync.wait_ge(sem[f"s_mmT2{pi}"], NBL * r)
                    sync.wait_ge(sem[f"s_mmC{pi}"], (NBL - 1) * r)
                    sync.wait_ge(sem[f"s_vkv{pi}"], NBL * r)
                    sync.wait_ge(sem[f"s_vsc{pi}"], NBL * r)
                    sync.wait_ge(sem[f"s_aev{pi}"], NBL * r)
                sl = sem[f"s_ld{h % 2}"]
                sync.dma_start(out=qch[h % 2][:],
                               in_=qT_i[h * 128:(h + 1) * 128, :]).then_inc(sl, 16)
                sync.dma_start(out=kch[h % 2][:],
                               in_=kT_i[h * 128:(h + 1) * 128, :]).then_inc(sl, 16)
                sync.dma_start(out=vch[h % 2][:],
                               in_=vT_i[h * 128:(h + 1) * 128, :]).then_inc(sl, 16)
                sync.dma_start(
                    out=sal[h % 2][:].rearrange("p (c d) -> p c d", c=C),
                    in_=S_all[:, h * 128:(h + 1) * 128, :].rearrange(
                        "c p d -> p c d"),
                ).then_inc(sl, 16)
                sync.dma_start(out=msk[h % 2][:], in_=maskb[h]).then_inc(sl, 16)
                sync.dma_start(out=qdc[h % 2][:], in_=qdecb[h]).then_inc(sl, 16)
                if h >= 1:
                    attn_exports(h - 1)
            attn_exports(H - 1)
            for slx in range(16):
                sync.wait_ge(sem[f"s_spa{slx}"], 16 * (H // 2))
            for slx in range(4):
                sync.wait_ge(sem[f"s_spe{slx}"], 16 * (NG // 4))
            for half in range(2):
                for tc in range(2):
                    tt = half * 2 + tc
                    tok0 = half * TH + tc * CW_
                    for cc in range(NT_):
                        li = tt * NT_ + cc
                        if li >= 2:
                            sync.wait_ge(sem["s_ach2"], li - 1)
                        sync.dma_start(
                            out=atr[li % 2][:],
                            in_=attn_hb[cc * 128:(cc + 1) * 128, tok0:tok0 + CW_],
                        ).then_inc(sem[f"s_are{li % 2}"], 16)
                    for cc in range(NT_):
                        li = tt * NT_ + cc
                        if li >= 2:
                            sync.wait_ge(sem["s_vgn"], li - 1)
                        sync.dma_start(
                            out=at2[li % 2][:],
                            in_=attn_hb[cc * 128:(cc + 1) * 128, tok0:tok0 + CW_],
                        ).then_inc(sem[f"s_ar2{li % 2}"], 16)
                        sync.dma_start(
                            out=gtr[li % 2][:],
                            in_=gate_hb[cc * 128:(cc + 1) * 128, tok0:tok0 + CW_],
                        ).then_inc(sem[f"s_gre{li % 2}"], 16)
                for oc in range(NT_):
                    for tc in range(2):
                        oi = (half * NT_ + oc) * 2 + tc
                        tok0 = half * TH + tc * CW_
                        if oi >= 2:
                            sync.wait_ge(sem["s_vout"], oi - 1)
                        sync.dma_start(
                            out=gre[oi % 2][:],
                            in_=g_hb[oc * 128:(oc + 1) * 128, tok0:tok0 + CW_],
                        ).then_inc(sem[f"s_g2re{oi % 2}"], 16)
                        sync.dma_start(
                            out=xre[oi % 2][:],
                            in_=xT2[oc * 128:(oc + 1) * 128, tok0:tok0 + CW_],
                        ).then_inc(sem[f"s_x2re{oi % 2}"], 16)
                        sync.wait_ge(sem["s_vout"], oi + 1)
                        sync.dma_start(
                            out=outT_o[oc * 128:(oc + 1) * 128, tok0:tok0 + CW_],
                            in_=osb[oi % 2][:],
                        ).then_inc(sem[f"s_spo{oi % 2}"], 16)
            sync.wait_ge(sem["s_spo0"], 16 * (NOI // 2))
            sync.wait_ge(sem["s_spo1"], 16 * (NOI // 2))
            for slx in range(4):
                sync.wait_ge(sem[f"s_spe{slx}"], 16 * (NG // 4))
            for slx in range(16):
                sync.wait_ge(sem[f"s_spa{slx}"], 16 * (H // 2))

        @block.gpsimd
        def _(gpsimd):
            gpsimd.dma_start(out=btsc[:], in_=BT2).then_inc(sem["s_bt"], 16)
            gpsimd.dma_start(
                out=at_sb[:].rearrange("p (c o) -> p c o", c=NT_),
                in_=AT2.rearrange("(c p) o -> p c o", p=128),
            ).then_inc(sem["s_at"], 16)
            all_w = []
            for half in range(2):
                for ci, (kind, idx) in enumerate(ccols):
                    all_w.append((gwT if kind == "g" else fwT, idx))
            for half in range(2):
                for oc in range(NT_):
                    all_w.append((owT, oc))
            for gc, (wt, idx) in enumerate(all_w):
                if gc >= 3:
                    pc = gc - 3
                    if pc < 2 * NCC:
                        gpsimd.wait_ge(sem["s_mmcol"], 2 * pc + 2)
                    else:
                        gpsimd.wait_ge(sem["s_mmout"], 2 * (pc - 2 * NCC) + 2)
                gpsimd.dma_start(
                    out=wcol[gc % 3][:].rearrange("p (c o) -> p c o", c=NT_),
                    in_=wt[:, idx * 128:(idx + 1) * 128]
                    .rearrange("(c p) o -> p c o", p=128),
                ).then_inc(sem[f"s_wc{gc % 3}"], 16)

        @block.vector
        def _(vector):
            vn = [0]

            def vstep(ins):
                vn[0] += 1
                ins.then_inc(sem["s_vch2"], 1)
                return vn[0]

            def vafter(n):
                vector.wait_ge(sem["s_vch2"], n)

            vector.wait_ge(sem["s_misc"], 16 * NMISC)
            vector.wait_ge(sem["s_mmA"], 1)
            n1 = vstep(vector.scalar_tensor_tensor(
                lg[:, 0:NE], pb[3][0:1, 0:NE], 1.0 / NMASK, rb_sb[:],
                ALU.mult, ALU.add))
            vafter(n1)
            n2 = vstep(vector.tensor_reduce(lg[:, NE:NE + 1], lg[:, 0:NE],
                                            AX.X, ALU.max))
            vafter(n2)
            n3 = vstep(vector.tensor_single_scalar(
                lg[:, 2 * NE:3 * NE], lg[:, 0:NE], lg[:, NE:NE + 1], ALU.subtract))
            vafter(n3)
            vector.engine_nop().then_inc(sem["s_vA"], 1)
            vector.wait_ge(sem["s_aA"], 1)
            n4 = vstep(vector.tensor_reduce(lg[:, 4 * NE:4 * NE + 1],
                                            lg[:, 3 * NE:4 * NE], AX.X, ALU.add))
            vafter(n4)
            n5 = vstep(vector.reciprocal(lg[:, 4 * NE + 1:4 * NE + 2],
                                         lg[:, 4 * NE:4 * NE + 1]))
            vafter(n5)
            n6 = vstep(vector.tensor_single_scalar(
                lg[:, 5 * NE:6 * NE], lg[:, 3 * NE:4 * NE],
                lg[:, 4 * NE + 1:4 * NE + 2], ALU.mult))
            vafter(n6)
            vector.engine_nop().then_inc(sem["s_vA"], 1)
            vector.wait_ge(sem["s_mmA"], 2)
            n7 = vstep(vector.tensor_copy(routT_sb[:], pb[3][0:NE, 8:9]))
            vafter(n7)
            vector.engine_nop().then_inc(sem["s_vA"], 1)
            vector.wait_ge(sem["s_mmA"], 3)
            n8 = vstep(vector.tensor_copy(rout_col[:], pb[3][:, 16:17]))
            vector.wait_ge(sem["s_bt"], 16)
            vafter(n8)
            n9 = vstep(vector.tensor_scalar_mul(btsc[:], btsc[:], rout_col[:, 0:1]))
            vafter(n9)
            vector.engine_nop().then_inc(sem["s_vA"], 2)
            for half in range(2):
                for tc in range(2):
                    vector.wait_ge(sem["s_mmA"], 4 + 2 * half + tc)
                    vector.tensor_copy(
                        t_sb[:, half * TH + tc * CW_: half * TH + (tc + 1) * CW_],
                        pb[4][:, :]).then_inc(sem["s_vt"], 1)
            cur = [None, None]
            psc = {0: pb[0], 1: pb[3]}
            pC = {0: pb[2], 1: pb[5]}
            for r in range(H // 2):
                for pi in range(2):
                    h = 2 * r + pi
                    vector.wait_ge(sem[f"s_ld{pi}"], 16 * 6 * (r + 1))
                    nacc = vstep(vector.tensor_scalar_mul(
                        kvA[pi][:], sal[pi][:, 0:HD], cf_sb[:, h * C:h * C + 1]))
                    cur[pi], oth = kvA[pi], kvB[pi]
                    for c2 in range(1, C):
                        vafter(nacc)
                        nacc = vstep(vector.scalar_tensor_tensor(
                            oth[:], sal[pi][:, c2 * HD:(c2 + 1) * HD],
                            cf_sb[:, h * C + c2:h * C + c2 + 1], cur[pi][:],
                            ALU.mult, ALU.add))
                        cur[pi], oth = oth, cur[pi]
                    vafter(nacc)
                    if r >= 1:
                        vector.wait_ge(sem[f"s_mmat{pi}"], NBL * r)
                    nkvb = vstep(vector.tensor_copy(kvb[pi][:], cur[pi][:]))
                    vafter(nkvb)
                    vector.engine_nop().then_inc(sem[f"s_vkv{pi}"], 1)
                for j in range(NBL):
                    for pi in range(2):
                        h = 2 * r + pi
                        al = r * NBL + j
                        vector.wait_ge(sem[f"s_mmsc{pi}"], al + 1)
                        if al >= 2:
                            vector.wait_ge(sem[f"s_mmat{pi}"], al - 1)
                        nsc = vstep(vector.tensor_mul(
                            scT[pi * 2 + j % 2][:], psc[pi][:, :], msk[pi][:]))
                        nq = vstep(vector.tensor_mul(
                            qdT[pi * 2 + j % 2][:],
                            qch[pi][:, j * BS:(j + 1) * BS], qdc[pi][:]))
                        vafter(max(nsc, nq))
                        vector.engine_nop().then_inc(sem[f"s_vsc{pi}"], 1)
                        if j < NBL - 1:
                            vector.wait_ge(sem[f"s_mmC{pi}"],
                                           (NBL - 1) * r + j + 1)
                            dst = kvB[pi] if cur[pi] is kvA[pi] else kvA[pi]
                            nup = vstep(vector.scalar_tensor_tensor(
                                dst[:], cur[pi][:], bd_list[h],
                                pC[pi][:, 0:HD], ALU.mult, ALU.add))
                            cur[pi] = dst
                            vafter(nup)
                            vector.wait_ge(sem[f"s_mmat{pi}"], al + 1)
                            nkb = vstep(vector.tensor_copy(kvb[pi][:],
                                                           cur[pi][:]))
                            vafter(nkb)
                            vector.engine_nop().then_inc(sem[f"s_vkv{pi}"], 1)
            vector.wait_ge(sem["s_mmcol"], NG)
            for half in range(2):
                if half == 1:
                    vector.wait_ge(sem["s_mmout"], NOI // 2)
                for tc in range(2):
                    tt = half * 2 + tc
                    vector.wait_ge(sem["s_mmsq"], (tt + 1) * NT_)
                    vector.wait_ge(sem["s_arm"], tt)
                    nr1 = vstep(vector.tensor_scalar(
                        rrow[:, 0:CW_], pb[4][0:1, :], 1.0 / D, EPS,
                        ALU.mult, ALU.add))
                    vafter(nr1)
                    vector.engine_nop().then_inc(sem["s_vrm"], 1)
                    vector.wait_ge(sem["s_arm"], tt + 1)
                    nr2 = vstep(vector.reciprocal(rrow[:, CW_:2 * CW_],
                                                  rrow[:, 2 * CW_:3 * CW_]))
                    vafter(nr2)
                    vector.engine_nop().then_inc(sem["s_vrm"], 1)
                    vector.wait_ge(sem["s_mmbc2"], tt + 1)
                    for cc in range(NT_):
                        li = tt * NT_ + cc
                        vector.wait_ge(sem[f"s_ar2{li % 2}"], 16 * (li // 2 + 1))
                        vector.wait_ge(sem[f"s_gre{li % 2}"], 16 * (li // 2 + 1))
                        vector.wait_ge(sem["s_vgn"], li)
                        ng1 = vstep(vector.scalar_tensor_tensor(
                            tmpf[:], at2[li % 2][:], nw_sb[:, cc:cc + 1],
                            pb[5][:, :], ALU.mult, ALU.mult))
                        vafter(ng1)
                        vector.tensor_mul(
                            actT[:, cc * TH + tc * CW_: cc * TH + (tc + 1) * CW_],
                            tmpf[:], gtr[li % 2][:]).then_inc(sem["s_vgn"], 1)
                for oc in range(NT_):
                    for tc in range(2):
                        oi = (half * NT_ + oc) * 2 + tc
                        vector.wait_ge(sem["s_mmout"], oi + 1)
                        vector.wait_ge(sem[f"s_g2re{oi % 2}"], 16 * (oi // 2 + 1))
                        vector.wait_ge(sem[f"s_x2re{oi % 2}"], 16 * (oi // 2 + 1))
                        if oi >= 2:
                            vector.wait_ge(sem[f"s_spo{oi % 2}"], 16 * (oi // 2))
                        vector.wait_ge(sem["s_vout"], oi)
                        nf1 = vstep(vector.tensor_mul(
                            tmpf[:], pb[oi % 2][:, :], gre[oi % 2][:]))
                        vafter(nf1)
                        vector.tensor_add(osb[oi % 2][:], tmpf[:],
                                          xre[oi % 2][:]).then_inc(sem["s_vout"], 1)

        @block.scalar
        def _(scalar):
            scalar.wait_ge(sem["s_misc"], 16 * NMISC)
            scalar.wait_ge(sem["s_vA"], 1)
            scalar.activation(lg[:, 3 * NE:4 * NE], lg[:, 2 * NE:3 * NE],
                              AF.Exp).then_inc(sem["s_aA"], 1)
            for gidx, (half, ci, kind, idx, tc) in enumerate(groups):
                e = g_evac[gidx]
                scalar.wait_ge(sem["s_mmcol"], gidx + 1)
                if e >= 4:
                    scalar.wait_ge(sem[f"s_spe{e % 4}"], 16 * (e // 4))
                b = g_bank[gidx]
                if kind == "g":
                    scalar.activation(evf[e % 4][:], pb[b][:, :],
                                      AF.Sigmoid).then_inc(sem["s_ev"], 1)
                else:
                    scalar.activation(evf[e % 4][:], pb[b][:, :], AF.Silu,
                                      bias=bc_sb[:, idx:idx + 1]).then_inc(
                        sem["s_ev"], 1)
            pat = {0: pb[1], 1: pb[4]}
            for r in range(H // 2):
                for j in range(NBL):
                    for pi in range(2):
                        h = 2 * r + pi
                        al = r * NBL + j
                        scalar.wait_ge(sem[f"s_mmT2{pi}"], al + 1)
                        if al >= 2:
                            scalar.wait_ge(sem[f"s_mmat{pi}"], al - 1)
                            scalar.wait_ge(sem[f"s_mmC{pi}"],
                                           min((NBL - 1) * (r + 1),
                                               (NBL - 1) * r + j))
                        bank = pt[pi]
                        scalar.copy(vtt2[pi * 2 + j % 2][:], bank[:, 0:BS])
                        if j < NBL - 1:
                            scalar.mul(kt2[pi * 2 + j % 2][:, 0:128],
                                       bank[:, BS:BS + 128],
                                       kd2_sb[:, 2 * h:2 * h + 1])
                            scalar.mul(kt2[pi * 2 + j % 2][:, 128:256],
                                       bank[:, BS + 128:2 * BS],
                                       kd2_sb[:, 2 * h + 1:2 * h + 2]).then_inc(
                                sem[f"s_aev{pi}"], 1)
                        else:
                            scalar.copy(kt2[pi * 2 + j % 2][:, 0:1],
                                        bank[:, BS:BS + 1]).then_inc(
                                sem[f"s_aev{pi}"], 1)
                        scalar.wait_ge(sem[f"s_mmat{pi}"], al + 1)
                        if r >= 1 or pi >= 1:
                            pass
                        if r >= 1:
                            scalar.wait_ge(sem[f"s_spa{pi * 8 + j}"], 16 * r)
                        scalar.copy(atb[pi * 8 + j][:],
                                    pat[pi][:, 0:BS]).then_inc(
                            sem[f"s_aat{pi}"], 1)
            for half in range(2):
                for tc in range(2):
                    tt = half * 2 + tc
                    for cc in range(NT_):
                        li = tt * NT_ + cc
                        scalar.wait_ge(sem[f"s_are{li % 2}"], 16 * (li // 2 + 1))
                        if li >= 2:
                            scalar.wait_ge(sem["s_mmsq"], li - 1)
                        scalar.square(sqb2[li % 2][:], atr[li % 2][:]).then_inc(
                            sem["s_ach2"], 1)
                    scalar.wait_ge(sem["s_vrm"], 2 * tt + 1)
                    scalar.sqrt(rrow[:, 2 * CW_:3 * CW_],
                                rrow[:, 0:CW_]).then_inc(sem["s_arm"], 1)

        @block.tensor
        def _(tensor):
            tensor.wait_ge(sem["s_misc"], 16 * NMISC)
            tensor.matmul(pb[3][0:1, 0:NE], ones8_sb[:], zr_sb2[:],
                          start=True, stop=True).then_inc(sem["s_mmA"], 1)
            tensor.wait_ge(sem["s_vA"], 2)
            tensor.matmul(pb[3][0:NE, 8:9], lg[:, 5 * NE:6 * NE], one1[:],
                          start=True, stop=True).then_inc(sem["s_mmA"], 1)
            tensor.wait_ge(sem["s_vA"], 3)
            tensor.matmul(pb[3][:, 16:17], sel_sb[:], routT_sb[:],
                          start=True, stop=True).then_inc(sem["s_mmA"], 1)
            for half in range(2):
                tensor.wait_ge(sem[f"s_xn{half}"], 16 * NT_)
                tensor.wait_ge(sem["s_at"], 16)
                tensor.wait_ge(sem["s_vt"], 2 * half)
                for tc in range(2):
                    for dc in range(NT_):
                        ins = tensor.matmul(
                            pb[4][:, :],
                            at_sb[:, dc * 128:(dc + 1) * 128],
                            actT[:, dc * TH + tc * CW_: dc * TH + (tc + 1) * CW_],
                            start=(dc == 0), stop=(dc == NT_ - 1))
                    ins.then_inc(sem["s_mmA"], 1)
                    if tc == 0:
                        tensor.wait_ge(sem["s_vt"], 2 * half + 1)
                for ci, (kind, idx) in enumerate(ccols):
                    gbase = half * NCC * 2 + ci * 2
                    gc = half * NCC + ci
                    tensor.wait_ge(sem[f"s_wc{gc % 3}"], 16 * (gc // 3 + 1))
                    if kind == "w":
                        tensor.wait_ge(sem["s_vA"], 5)
                        tensor.wait_ge(sem["s_vt"], 2 * half + 2)
                    for tc in range(2):
                        gidx = gbase + tc
                        b = g_bank[gidx]
                        pe_ = bank_prev_evac[gidx]
                        if pe_ is not None:
                            tensor.wait_ge(sem["s_ev"], pe_ + 1)
                        for dc in range(NT_):
                            ins = tensor.matmul(
                                pb[b][:, :],
                                wcol[gc % 3][:, dc * 128:(dc + 1) * 128],
                                actT[:, dc * TH + tc * CW_:
                                     dc * TH + (tc + 1) * CW_],
                                start=(dc == 0),
                                stop=(dc == NT_ - 1 and kind == "g"))
                        if kind == "w":
                            ins = tensor.matmul(
                                pb[b][:, :],
                                btsc[:, idx * 128:(idx + 1) * 128],
                                t_sb[:, half * TH + tc * CW_:
                                     half * TH + (tc + 1) * CW_],
                                start=False, stop=True)
                        ins.then_inc(sem["s_mmcol"], 1)
            psc = {0: pb[0], 1: pb[3]}
            pat = {0: pb[1], 1: pb[4]}
            pC = {0: pb[2], 1: pb[5]}
            tensor.wait_ge(sem["s_vA"], 5)
            tensor.wait_ge(sem["s_vt"], 4)
            for r in range(H // 2):
                for pi in range(2):
                    tensor.wait_ge(sem[f"s_ld{pi}"], 16 * 6 * (r + 1))
                for j in range(NBL):
                    for pi in range(2):
                        h = 2 * r + pi
                        al = r * NBL + j
                        if al >= 1:
                            tensor.wait_ge(sem[f"s_vsc{pi}"], al)
                        for mi in range(2):
                            ins = tensor.matmul(
                                psc[pi][:, mi * BS:(mi + 1) * BS],
                                kch[pi][:, j * BS + mi * 128:
                                         j * BS + (mi + 1) * 128],
                                qch[pi][:, j * BS:(j + 1) * BS],
                                start=True, stop=True)
                        ins.then_inc(sem[f"s_mmsc{pi}"], 1)
                        if al >= 1:
                            tensor.wait_ge(sem[f"s_aev{pi}"], al)
                        for mi in range(2):
                            tensor.transpose(
                                pt[pi][:, mi * 128:(mi + 1) * 128],
                                vch[pi][:, j * BS + mi * 128:
                                          j * BS + (mi + 1) * 128],
                                idb2[:])
                        for mi in range(2):
                            ins = tensor.transpose(
                                pt[pi][:, BS + mi * 128:BS + (mi + 1) * 128],
                                kch[pi][:, j * BS + mi * 128:
                                          j * BS + (mi + 1) * 128],
                                idb2[:])
                        ins.then_inc(sem[f"s_mmT2{pi}"], 1)
                        tensor.wait_ge(sem[f"s_vsc{pi}"], al + 1)
                        tensor.wait_ge(sem[f"s_aev{pi}"], al + 1)
                        tensor.wait_ge(sem[f"s_vkv{pi}"], NBL * r + j + 1)
                        if al >= 1:
                            tensor.wait_ge(sem[f"s_aat{pi}"], al)
                        for mi in range(2):
                            tensor.matmul(
                                pat[pi][:, 0:BS],
                                vtt2[pi * 2 + j % 2][:, mi * 128:(mi + 1) * 128],
                                scT[pi * 2 + j % 2][:, mi * BS:(mi + 1) * BS],
                                start=(mi == 0), stop=False)
                        tensor.matmul(
                            pat[pi][:, 0:BS], kvb[pi][:],
                            qdT[pi * 2 + j % 2][:],
                            start=False, stop=True).then_inc(
                            sem[f"s_mmat{pi}"], 1)
                        if j < NBL - 1:
                            for mi in range(2):
                                ins = tensor.matmul(
                                    pC[pi][:, 0:HD],
                                    kt2[pi * 2 + j % 2][:,
                                        mi * 128:(mi + 1) * 128],
                                    vtt2[pi * 2 + j % 2][:,
                                         mi * 128:(mi + 1) * 128],
                                    start=(mi == 0), stop=(mi == 1))
                            ins.then_inc(sem[f"s_mmC{pi}"], 1)
            for pi in range(2):
                tensor.wait_ge(sem[f"s_vsc{pi}"], H * NBL // 2)
                tensor.wait_ge(sem[f"s_aat{pi}"], H * NBL // 2)
                tensor.wait_ge(sem[f"s_vkv{pi}"], H * NBL // 2)
                tensor.wait_ge(sem[f"s_aev{pi}"], H * NBL // 2)
            for half in range(2):
                for tc in range(2):
                    tt = half * 2 + tc
                    for cc in range(NT_):
                        li = tt * NT_ + cc
                        tensor.wait_ge(sem["s_ach2"], li + 1)
                        if cc == 0 and tt > 0:
                            tensor.wait_ge(sem["s_vrm"], 2 * tt)
                        tensor.matmul(pb[4][0:1, :], onesb2[:], sqb2[li % 2][:],
                                      start=(cc == 0), stop=(cc == NT_ - 1)
                                      ).then_inc(sem["s_mmsq"], 1)
                    tensor.wait_ge(sem["s_vrm"], 2 * tt + 2)
                    if tt > 0:
                        tensor.wait_ge(sem["s_vgn"], tt * NT_)
                    tensor.matmul(pb[5][:, :], onesf2[:], rrow[:, CW_:2 * CW_],
                                  start=True, stop=True).then_inc(sem["s_mmbc2"], 1)
                for oc in range(NT_):
                    gc = 2 * NCC + half * NT_ + oc
                    tensor.wait_ge(sem[f"s_wc{gc % 3}"], 16 * (gc // 3 + 1))
                    tensor.wait_ge(sem["s_vgn"], (half + 1) * 2 * NT_)
                    for tc in range(2):
                        oi = (half * NT_ + oc) * 2 + tc
                        if oi >= 2:
                            tensor.wait_ge(sem["s_vout"], oi - 1)
                        for dc in range(NT_):
                            tensor.matmul(
                                pb[oi % 2][:, :],
                                wcol[gc % 3][:, dc * 128:(dc + 1) * 128],
                                actT[:, dc * TH + tc * CW_:
                                     dc * TH + (tc + 1) * CW_],
                                start=(dc == 0), stop=(dc == NT_ - 1),
                            ).then_inc(sem["s_mmout"], 1 if dc == NT_ - 1 else 0)

    return nc


def _host_prep_l2(inputs, res1):
    x = np.asarray(inputs["L0_attn_hidden_states"]).reshape(T, D)
    gwT = np.ascontiguousarray(np.asarray(inputs["L0_attn_gate_w"]).T)
    fwT = np.ascontiguousarray(np.asarray(inputs["L0_ffn_W"]).T)
    owT = np.ascontiguousarray(np.asarray(inputs["L0_attn_out_w"]).T)
    AT2 = np.ascontiguousarray(
        np.asarray(inputs["L0_ffn_A"]).reshape(NE * NR, D).T)
    BT2 = np.ascontiguousarray(
        np.asarray(inputs["L0_ffn_B"]).transpose(0, 2, 1).reshape(NE * NR, D))
    rbrow = np.asarray(inputs["L0_ffn_router_b"]).reshape(1, NE).astype(np.float32)
    norm_w = np.asarray(inputs["L0_attn_norm_w"]).astype(np.float32)
    b = np.asarray(inputs["L0_ffn_b"]).astype(np.float32)

    selm = np.zeros((NE, 128), np.float32)
    for p in range(128):
        selm[p // NR, p] = 2.0
    onessc = np.ones((1, 1), np.float32)
    ones8 = np.ones((C, 1), np.float32)
    bcols = np.ascontiguousarray(b.reshape(32, 128).T)
    nwcols = np.ascontiguousarray(norm_w.reshape(32, 128).T)

    q_decay, k_decay, diag, bd = _decays()
    kd2 = np.zeros((128, 2 * H), np.float32)
    for h in range(H):
        kd2[:, 2 * h] = k_decay[h, 0:128]
        kd2[:, 2 * h + 1] = k_decay[h, 128:256]
    qdecb = np.ascontiguousarray(
        np.broadcast_to(q_decay[:, None, :], (H, 128, BS))).astype(np.float32)
    maskT = diag.transpose(0, 2, 1)
    maskb = np.ascontiguousarray(
        maskT.reshape(H, 2, 128, BS).transpose(0, 2, 1, 3).reshape(H, 128, 2 * BS)
    ).astype(np.float32)
    identb2 = np.eye(128, dtype=BF)
    ones_bf2 = np.ones((128, 1), BF)
    ones_f2 = np.ones((1, 128), np.float32)

    S_allv = np.ascontiguousarray(
        np.stack([np.asarray(res1[c]["S_o"]) for c in range(C)]))
    zr_allT = np.ascontiguousarray(
        np.stack([np.asarray(res1[c]["zr_o"])[:, 0] for c in range(C)]))

    in_maps = []
    for c in range(C):
        coefb = np.zeros((128, H * C), np.float32)
        for h in range(H):
            for cp in range(c):
                coefb[:, h * C + cp] = bd[h] ** (NBL * (c - 1 - cp))
        in_maps.append(dict(
            xnT_i=np.asarray(res1[c]["xnT_o"]),
            qT_i=np.asarray(res1[c]["qT_o"]),
            kT_i=np.asarray(res1[c]["kT_o"]),
            vT_i=np.asarray(res1[c]["vT_o"]),
            S_all=S_allv, zr_allT=zr_allT,
            xT2=np.ascontiguousarray(x[c * TC:(c + 1) * TC].T),
            gwT=gwT, fwT=fwT, owT=owT, AT2=AT2, BT2=BT2,
            rbrow=rbrow, selm=selm, onessc=onessc, ones8=ones8,
            bcols=bcols, nwcols=nwcols, coefb=coefb, kd2=kd2,
            qdecb=qdecb, maskb=maskb, identb2=identb2,
            ones_bf2=ones_bf2, ones_f2=ones_f2,
        ))
    return in_maps


_NC_CACHE = {}


def kernel(**inputs):
    if "l1" not in _NC_CACHE:
        _NC_CACHE["l1"] = build_l1()
        _NC_CACHE["l2"] = build_l2()
    in1 = _host_prep_l1(inputs)
    res1 = run_bass_kernel_spmd(_NC_CACHE["l1"], in1,
                                core_ids=list(range(C))).results
    in2 = _host_prep_l2(inputs, res1)
    res2 = run_bass_kernel_spmd(_NC_CACHE["l2"], in2,
                                core_ids=list(range(C))).results
    out = np.concatenate(
        [np.asarray(res2[c]["outT_o"]).T for c in range(C)], axis=0)
    return out.reshape(1, T, D)


# version 04 (final)
# speedup vs baseline: 1.2882x; 1.0446x over previous
import numpy as np
import ml_dtypes
import concourse.bass as bass
import concourse.mybir as mybir
from concourse.bass_utils import run_bass_kernel_spmd
from contextlib import ExitStack

F32 = mybir.dt.float32
BF16 = mybir.dt.bfloat16
AF = mybir.ActivationFunctionType
ALU = mybir.AluOpType
AX = mybir.AxisListType

C = 8
T = 16384
TC = T // C
D = 4096
H = 32
HD = 128
BS = 256
NBL = TC // BS
NE = 8
NR = 16
Q_START, Q_END = 611, T - 1
NMASK = Q_END - Q_START
EPS = 1e-5

BF = ml_dtypes.bfloat16


def _decays():
    base = 1.0 / 2.0 ** (8.0 / H)
    factor = 1.0 - 1.0 / (31 + 1e-5) + 1e-5
    slope = (base ** (np.arange(H, dtype=np.float64) + 1.0)) * factor
    r = np.arange(BS, dtype=np.float64) + 1.0
    q_decay = np.exp(-slope[:, None] * r[None, :])
    k_decay = np.exp(-slope[:, None] * (BS - r)[None, :])
    diff = r[:, None] - r[None, :]
    diag = np.where(diff >= 0, np.exp(-slope[:, None, None] * diff[None]), 0.0)
    block_decay = np.exp(-slope * BS)
    return (q_decay.astype(np.float32), k_decay.astype(np.float32),
            diag.astype(np.float32), block_decay)



def build_l1():
    nc = bass.Bass("TRN2", target_bir_lowering=False, num_devices=C)

    xT = nc.dram_tensor("xT", [D, TC], F32, kind="ExternalInput").ap()
    WqT = nc.dram_tensor("WqT", [D, D], F32, kind="ExternalInput").ap()
    WkT = nc.dram_tensor("WkT", [D, D], F32, kind="ExternalInput").ap()
    WvT = nc.dram_tensor("WvT", [D, D], F32, kind="ExternalInput").ap()
    rWT = nc.dram_tensor("rWT", [D, NE], F32, kind="ExternalInput").ap()
    negm = nc.dram_tensor("negm", [NE, 2], F32, kind="ExternalInput").ap()
    kdext = nc.dram_tensor("kdext", [H, 16, 128], F32, kind="ExternalInput").ap()
    ident16 = nc.dram_tensor("ident16", [16, 16], F32, kind="ExternalInput").ap()
    ones_bf = nc.dram_tensor("ones_bf", [128, 1], BF16, kind="ExternalInput").ap()
    ones_f = nc.dram_tensor("ones_f", [1, 128], F32, kind="ExternalInput").ap()
    identb = nc.dram_tensor("identb", [128, 128], BF16, kind="ExternalInput").ap()

    xnT_o = nc.dram_tensor("xnT_o", [D, TC], BF16, kind="ExternalOutput").ap()
    qT_o = nc.dram_tensor("qT_o", [D, TC], BF16, kind="ExternalOutput").ap()
    kT_o = nc.dram_tensor("kT_o", [D, TC], BF16, kind="ExternalOutput").ap()
    vT_o = nc.dram_tensor("vT_o", [D, TC], BF16, kind="ExternalOutput").ap()
    S_o = nc.dram_tensor("S_o", [H * HD, HD], F32, kind="ExternalOutput").ap()
    zr_o = nc.dram_tensor("zr_o", [NE, 1], F32, kind="ExternalOutput").ap()

    NT = 32
    NTC = 4
    CW = 512

    cols = []
    for h in range(H):
        cols.append(("k", h))
        cols.append(("v", h))
    for h in range(H):
        cols.append(("q", h))
    NCOL = len(cols)
    w_src = {"q": WqT, "k": WkT, "v": WvT}
    out_t = {"q": qT_o, "k": kT_o, "v": vT_o}
    chain_at = {}
    for h in range(H):
        ci = min(2 * h + 5, NCOL - 1)
        chain_at.setdefault(ci, []).append(h)
    s_export_at = {}
    for h in range(H):
        ci = min(2 * h + 10, NCOL - 1)
        s_export_at.setdefault(ci, []).append(h)

    with ExitStack() as ctx:
        E = ctx.enter_context
        xnT = E(nc.sbuf_tensor("xnT", [128, NT * TC], BF16))
        xbuf = [E(nc.sbuf_tensor(f"xbuf{i}", [128, CW], F32)) for i in range(2)]
        xbf = [E(nc.sbuf_tensor(f"xbf{i}", [128, CW], BF16)) for i in range(2)]
        sqb = [E(nc.sbuf_tensor(f"sqb{i}", [128, CW], BF16)) for i in range(2)]
        tmpb = E(nc.sbuf_tensor("tmpb", [128, CW], F32))
        mrow = E(nc.sbuf_tensor("mrow", [1, CW], F32))
        erow = E(nc.sbuf_tensor("erow", [1, CW], F32))
        msq = E(nc.sbuf_tensor("msq", [1, CW], F32))
        vrow = E(nc.sbuf_tensor("vrow", [1, CW], F32))
        srow = E(nc.sbuf_tensor("srow", [1, CW], F32))
        rstd = E(nc.sbuf_tensor("rstd", [1, CW], F32))
        mrstd = E(nc.sbuf_tensor("mrstd", [1, CW], F32))
        ones_sb = E(nc.sbuf_tensor("ones_sb", [128, 1], BF16))
        onesf_sb = E(nc.sbuf_tensor("onesf_sb", [1, 128], F32))
        wbuf = [E(nc.sbuf_tensor(f"wbuf{i}", [128, NT * 128], BF16)) for i in range(2)]
        evb = [E(nc.sbuf_tensor(f"evb{i}", [128, CW], BF16)) for i in range(8)]
        rw_sb = E(nc.sbuf_tensor("rw_sb", [128, NT * NE], BF16))
        negm_sb = E(nc.sbuf_tensor("negm_sb", [NE, 2], F32))
        zparts = E(nc.sbuf_tensor("zparts", [NE, NTC + 4], F32))
        zr_sb = E(nc.sbuf_tensor("zr_sb", [NE, 1], F32))
        kdnat = [E(nc.sbuf_tensor(f"kdnat{i}", [16, 128], F32)) for i in range(2)]
        kdcols = E(nc.sbuf_tensor("kdcols", [128, 16], F32))
        id16 = E(nc.sbuf_tensor("id16", [16, 16], F32))
        kch = [E(nc.sbuf_tensor(f"kch{i}", [128, 16 * 128], BF16)) for i in range(2)]
        vch = [E(nc.sbuf_tensor(f"vch{i}", [128, 16 * 128], BF16)) for i in range(2)]
        kts = E(nc.sbuf_tensor("kts", [128, 16 * 128], BF16))
        vtt = E(nc.sbuf_tensor("vtt", [128, 16 * 128], BF16))
        idb = E(nc.sbuf_tensor("idb", [128, 128], BF16))
        S_sb = [E(nc.sbuf_tensor(f"S_sb{i}", [128, HD], F32)) for i in range(2)]
        dumcol = E(nc.sbuf_tensor("dumcol", [128, 1], F32))
        pb = [E(nc.psum_tensor(f"pb{i}", [128, 512], F32)) for i in range(6)]
        pt = [E(nc.psum_tensor(f"pt{i}", [128, 1024], BF16)) for i in range(2)]
        names = (["s_misc", "s_x0", "s_x1", "s_p1v", "s_p1a", "s_mmst", "s_vvar",
                  "s_astd", "s_vrs", "s_mmbc", "s_vxn", "s_wc0", "s_wc1", "s_rw",
                  "s_mmcol", "s_ev", "s_spx", "s_mmT", "s_kdn0", "s_kdn1",
                  "s_mmkd", "s_vkd", "s_vev", "s_kv0", "s_kv1", "s_mmS",
                  "s_vS", "s_spS0", "s_spS1", "s_mmz", "s_vz", "s_vzf", "s_spz"]
                 + [f"s_spq{i}" for i in range(8)] + ["s_vch"])
        sem = {n: E(nc.semaphore(n)) for n in names}
        block = E(nc.Block())

        def xl_idx(tc, p, cc):
            return tc * 64 + p * 32 + cc

        def wait_xload(eng, tc, p, cc):
            L = xl_idx(tc, p, cc)
            eng.wait_ge(sem[f"s_x{L % 2}"], 16 * (L // 2 + 1))

        def eidx(ci, tc):
            return 4 * ci + tc

        @block.sync
        def _(sync):
            sync.dma_start(out=ones_sb[:], in_=ones_bf).then_inc(sem["s_misc"], 16)
            sync.dma_start(out=onesf_sb[:], in_=ones_f).then_inc(sem["s_misc"], 16)
            sync.dma_start(out=id16[:], in_=ident16).then_inc(sem["s_misc"], 16)
            sync.dma_start(out=negm_sb[:], in_=negm).then_inc(sem["s_misc"], 16)
            sync.dma_start(out=idb[:], in_=identb).then_inc(sem["s_misc"], 16)
            for tc in range(NTC):
                for cc in range(NT):
                    li = tc * 64 + cc
                    if li >= 2:
                        ptc, pp, pcc = (li - 2) // 64, ((li - 2) % 64) // 32, (li - 2) % 32
                        if pp == 0:
                            sync.wait_ge(sem["s_p1v"], ptc * 32 + pcc + 1)
                            sync.wait_ge(sem["s_p1a"], ptc * 32 + pcc + 1)
                        else:
                            sync.wait_ge(sem["s_vxn"], ptc * 32 + pcc + 1)
                    sync.dma_start(
                        out=xbuf[li % 2][:],
                        in_=xT[cc * 128:(cc + 1) * 128, tc * CW:(tc + 1) * CW],
                    ).then_inc(sem[f"s_x{li % 2}"], 16)
            for cc in range(NT):
                sync.wait_ge(sem["s_vxn"], 3 * 32 + cc + 1)
                sync.dma_start(
                    out=xnT_o[cc * 128:(cc + 1) * 128, :],
                    in_=xnT[:, cc * TC:(cc + 1) * TC],
                ).then_inc(sem["s_spx"], 16)
            for ci, (kind, h) in enumerate(cols):
                for hh in s_export_at.get(ci, []):
                    sync.wait_ge(sem["s_vS"], hh + 1)
                    sync.dma_start(
                        out=S_o[hh * HD:(hh + 1) * HD, :], in_=S_sb[hh % 2][:]
                    ).then_inc(sem[f"s_spS{hh % 2}"], 16)
                for tc in range(NTC):
                    e = eidx(ci, tc)
                    sync.wait_ge(sem["s_ev"], e + 1)
                    sync.dma_start(
                        out=out_t[kind][h * 128:(h + 1) * 128, tc * CW:(tc + 1) * CW],
                        in_=evb[e % 8][:],
                    ).then_inc(sem[f"s_spq{e % 8}"], 16)
                if kind == "v":
                    hh = h
                    if hh >= 2:
                        sync.wait_ge(sem["s_vkd"], hh - 1)
                    sync.dma_start(out=kdnat[hh % 2][:], in_=kdext[hh]).then_inc(
                        sem[f"s_kdn{hh % 2}"], 16)
                    for sl in range(8):
                        sync.wait_ge(sem[f"s_spq{sl}"], 16 * (hh + 1))
                    if hh >= 2:
                        sync.wait_ge(sem["s_mmS"], hh - 1)
                    sync.dma_start(out=kch[hh % 2][:],
                                   in_=kT_o[hh * 128:(hh + 1) * 128, :]).then_inc(
                        sem[f"s_kv{hh % 2}"], 16)
                    sync.dma_start(out=vch[hh % 2][:],
                                   in_=vT_o[hh * 128:(hh + 1) * 128, :]).then_inc(
                        sem[f"s_kv{hh % 2}"], 16)
            sync.wait_ge(sem["s_vzf"], 1)
            sync.dma_start(out=zr_o, in_=zr_sb[:]).then_inc(sem["s_spz"], 16)
            for sl in range(8):
                sync.wait_ge(sem[f"s_spq{sl}"], 16 * (4 * NCOL // 8))
            sync.wait_ge(sem["s_spx"], 16 * NT)
            sync.wait_ge(sem["s_spS0"], 16 * (H // 2))
            sync.wait_ge(sem["s_spS1"], 16 * (H // 2))
            sync.wait_ge(sem["s_spz"], 16)

        @block.gpsimd
        def _(gpsimd):
            gpsimd.dma_start(
                out=rw_sb[:].rearrange("p (c e) -> p c e", c=NT),
                in_=rWT.rearrange("(c p) e -> p c e", p=128),
            ).then_inc(sem["s_rw"], 16)
            for ci, (kind, h) in enumerate(cols):
                if ci >= 2:
                    gpsimd.wait_ge(sem["s_mmcol"], 4 * (ci - 1))
                src = w_src[kind][:, h * 128:(h + 1) * 128]
                gpsimd.dma_start(
                    out=wbuf[ci % 2][:].rearrange("p (c o) -> p c o", c=NT),
                    in_=src.rearrange("(c p) o -> p c o", p=128),
                ).then_inc(sem[f"s_wc{ci % 2}"], 16)

        @block.vector
        def _(vector):
            vn = [0]

            def vstep(ins):
                vn[0] += 1
                ins.then_inc(sem["s_vch"], 1)
                return vn[0]

            def vafter(n):
                vector.wait_ge(sem["s_vch"], n)

            vector.wait_ge(sem["s_misc"], 80)
            for tc in range(NTC):
                for cc in range(NT):
                    wait_xload(vector, tc, 0, cc)
                    if tc * 32 + cc >= 2:
                        vector.wait_ge(sem["s_mmst"], tc * 32 + cc - 1)
                    vector.tensor_copy(xbf[cc % 2][:], xbuf[cc % 2][:]).then_inc(
                        sem["s_p1v"], 1)
                vector.wait_ge(sem["s_mmst"], 32 * (tc + 1))
                n_m = vstep(vector.tensor_scalar_mul(mrow[:], pb[0][0:1, :], 1.0 / D))
                vector.tensor_scalar_mul(erow[:], pb[1][0:1, :], 1.0 / D).then_inc(
                    sem["s_vvar"], 1)
                vafter(n_m)
                n_q = vstep(vector.tensor_mul(msq[:], mrow[:], mrow[:]))
                vafter(n_q)
                vector.scalar_tensor_tensor(
                    vrow[:], erow[:], EPS, msq[:], ALU.add, ALU.subtract
                ).then_inc(sem["s_vvar"], 1)
                vector.wait_ge(sem["s_astd"], tc + 1)
                n_r = vstep(vector.reciprocal(rstd[:], srow[:]))
                vafter(n_r)
                n_mr = vstep(vector.tensor_mul(mrstd[:], mrow[:], rstd[:]))
                vafter(n_mr)
                vector.engine_nop().then_inc(sem["s_vrs"], 1)
                for cc in range(NT):
                    wait_xload(vector, tc, 1, cc)
                    vector.wait_ge(sem["s_mmbc"], 2 * (tc + 1))
                    vector.wait_ge(sem["s_vxn"], tc * 32 + cc)
                    n_t = vstep(vector.tensor_mul(tmpb[:], xbuf[cc % 2][:], pb[2][:, :]))
                    vafter(n_t)
                    vector.tensor_tensor(
                        xnT[:, cc * TC + tc * CW: cc * TC + (tc + 1) * CW],
                        tmpb[:], pb[3][:, :], ALU.subtract,
                    ).then_inc(sem["s_vxn"], 1)
            zn = {}
            for tc in range(NTC):
                vector.wait_ge(sem["s_mmz"], tc + 1)
                zn[tc] = vstep(vector.tensor_reduce(
                    zparts[:, tc:tc + 1], pb[0][0:NE, :], AX.X, ALU.add))
                if tc == 1:
                    zn["a"] = vstep(vector.tensor_reduce(
                        zparts[:, 4:5], pb[0][0:NE, 0:99], AX.X, ALU.add))
                if tc == 3:
                    zn["b"] = vstep(vector.tensor_copy(
                        zparts[:, 5:6], pb[0][0:NE, 511:512]))
                vector.engine_nop().then_inc(sem["s_vz"], 1)
            vafter(max(zn.values()))
            n1 = vstep(vector.tensor_reduce(zparts[:, 6:7], zparts[:, 0:NTC],
                                            AX.X, ALU.add))
            n2 = vstep(vector.tensor_add(zparts[:, 7:8], zparts[:, 0:1],
                                         zparts[:, 4:5]))
            vafter(n2)
            n3 = vstep(vector.scalar_tensor_tensor(
                zparts[:, 6:7], zparts[:, 7:8], negm_sb[:, 0:1], zparts[:, 6:7],
                ALU.mult, ALU.add))
            vafter(n3)
            vector.scalar_tensor_tensor(
                zr_sb[:], zparts[:, 5:6], negm_sb[:, 1:2], zparts[:, 6:7],
                ALU.mult, ALU.add).then_inc(sem["s_vzf"], 1)
            for h in range(H):
                vector.wait_ge(sem["s_mmkd"], h + 1)
                n_kd = vstep(vector.tensor_copy(kdcols[:], pb[2][:, 0:16]))
                vafter(n_kd)
                vector.engine_nop().then_inc(sem["s_vkd"], 1)
                for c2 in range(16):
                    vector.wait_ge(sem["s_mmT"], 16 * h + c2 + 1)
                    bank = pt[c2 % 2]
                    vector.tensor_scalar_mul(
                        kts[:, c2 * 128:(c2 + 1) * 128],
                        bank[:, 0:128], kdcols[:, c2:c2 + 1])
                    vector.tensor_copy(
                        vtt[:, c2 * 128:(c2 + 1) * 128],
                        bank[:, 128:256]).then_inc(sem["s_vev"], 1)
                vector.wait_ge(sem["s_mmS"], h + 1)
                if h >= 2:
                    vector.wait_ge(sem[f"s_spS{h % 2}"], 16 * (h // 2))
                n_s = vstep(vector.tensor_copy(S_sb[h % 2][:], pb[1][:, 0:128]))
                vafter(n_s)
                vector.engine_nop().then_inc(sem["s_vS"], 1)

        @block.scalar
        def _(scalar):
            for tc in range(NTC):
                for cc in range(NT):
                    wait_xload(scalar, tc, 0, cc)
                    if tc * 32 + cc >= 2:
                        scalar.wait_ge(sem["s_mmst"], tc * 32 + cc - 1)
                    scalar.square(sqb[cc % 2][:], xbuf[cc % 2][:]).then_inc(
                        sem["s_p1a"], 1)
                scalar.wait_ge(sem["s_vvar"], 2 * (tc + 1))
                scalar.sqrt(srow[:], vrow[:]).then_inc(sem["s_astd"], 1)
                for cc in range(NT):
                    li = tc * 64 + 32 + cc
                    ptc, pp, pcc = (li - 2) // 64, ((li - 2) % 64) // 32, (li - 2) % 32
                    if pp == 0:
                        scalar.wait_ge(sem["s_p1v"], ptc * 32 + pcc + 1)
                        scalar.wait_ge(sem["s_p1a"], ptc * 32 + pcc + 1)
                    else:
                        scalar.wait_ge(sem["s_vxn"], ptc * 32 + pcc + 1)
                    scalar.dma_start(
                        out=xbuf[li % 2][:],
                        in_=xT[cc * 128:(cc + 1) * 128, tc * CW:(tc + 1) * CW],
                    ).then_inc(sem[f"s_x{li % 2}"], 16)
            for ci in range(NCOL):
                for tc in range(NTC):
                    e = eidx(ci, tc)
                    scalar.wait_ge(sem["s_mmcol"], e + 1)
                    if e >= 8:
                        scalar.wait_ge(sem[f"s_spq{e % 8}"], 16 * (e // 8))
                    scalar.activation(evb[e % 8][:], pb[4 + e % 2][:, :],
                                      AF.Silu).then_inc(sem["s_ev"], 1)

        @block.tensor
        def _(tensor):
            tensor.wait_ge(sem["s_misc"], 80)
            for tc in range(NTC):
                if tc > 0:
                    tensor.wait_ge(sem["s_vvar"], 2 * tc)
                for cc in range(NT):
                    tensor.wait_ge(sem["s_p1v"], tc * 32 + cc + 1)
                    tensor.wait_ge(sem["s_p1a"], tc * 32 + cc + 1)
                    st = cc == 0
                    sp = cc == NT - 1
                    tensor.matmul(pb[0][0:1, :], ones_sb[:], xbf[cc % 2][:],
                                  start=st, stop=sp)
                    tensor.matmul(pb[1][0:1, :], ones_sb[:], sqb[cc % 2][:],
                                  start=st, stop=sp).then_inc(sem["s_mmst"], 1)
                tensor.wait_ge(sem["s_vrs"], tc + 1)
                if tc > 0:
                    tensor.wait_ge(sem["s_vxn"], 32 * tc)
                tensor.matmul(pb[2][:, :], onesf_sb[:], rstd[:], start=True, stop=True)
                tensor.matmul(pb[3][:, :], onesf_sb[:], mrstd[:],
                              start=True, stop=True).then_inc(sem["s_mmbc"], 2)
            chain_pend = []
            for ci, (kind, h) in enumerate(cols):
                tensor.wait_ge(sem[f"s_wc{ci % 2}"], 16 * (ci // 2 + 1))
                for tc in range(NTC):
                    g = 4 * ci + tc
                    if g >= 2:
                        tensor.wait_ge(sem["s_ev"], g - 1)
                    for dc in range(NT):
                        if ci < 3:
                            tensor.wait_ge(sem["s_vxn"],
                                           min(tc * 32 + dc + 1, 32 * NTC))
                        ins = tensor.matmul(
                            pb[4 + g % 2][:, :],
                            wbuf[ci % 2][:, dc * 128:(dc + 1) * 128],
                            xnT[:, dc * TC + tc * CW: dc * TC + (tc + 1) * CW],
                            start=(dc == 0), stop=(dc == NT - 1),
                        )
                        if dc == NT - 1:
                            ins.then_inc(sem["s_mmcol"], 1)
                    for _ in range(3):
                        if chain_pend:
                            chain_pend.pop(0)()
                if ci == 4:
                    tensor.wait_ge(sem["s_rw"], 16)
                    tensor.wait_ge(sem["s_vvar"], 2 * NTC - 1)
                    for tc in range(NTC):
                        if tc > 0:
                            tensor.wait_ge(sem["s_vz"], tc)
                        tensor.wait_ge(sem["s_vxn"], 32 * (tc + 1))
                        for dc in range(NT):
                            ins = tensor.matmul(
                                pb[0][0:NE, :],
                                rw_sb[:, dc * NE:(dc + 1) * NE],
                                xnT[:, dc * TC + tc * CW: dc * TC + (tc + 1) * CW],
                                start=(dc == 0), stop=(dc == NT - 1),
                            )
                            if dc == NT - 1:
                                ins.then_inc(sem["s_mmz"], 1)
                for hh in chain_at.get(ci, []):
                    def mk_kd(hh=hh):
                        tensor.wait_ge(sem[f"s_kdn{hh % 2}"], 16 * (hh // 2 + 1))
                        tensor.wait_ge(sem["s_vkd"], hh)
                        if hh == 0:
                            tensor.wait_ge(sem["s_vxn"], 32 * NTC)
                            tensor.wait_ge(sem["s_vvar"], 2 * NTC)
                        tensor.transpose(pb[2][:, 0:16], kdnat[hh % 2][:],
                                         id16[:]).then_inc(sem["s_mmkd"], 1)
                        tensor.wait_ge(sem[f"s_kv{hh % 2}"], 32 * (hh // 2 + 1))
                    chain_pend.append(mk_kd)

                    def mk_pair(hh=hh, c2=0):
                        if 16 * hh + c2 >= 2:
                            tensor.wait_ge(sem["s_vev"], 16 * hh + c2 - 1)
                        bank = pt[c2 % 2]
                        tensor.transpose(bank[:, 0:128],
                                         kch[hh % 2][:, c2 * 128:(c2 + 1) * 128],
                                         idb[:])
                        tensor.transpose(bank[:, 128:256],
                                         vch[hh % 2][:, c2 * 128:(c2 + 1) * 128],
                                         idb[:]).then_inc(sem["s_mmT"], 1)
                    for c2_ in range(16):
                        chain_pend.append(
                            (lambda hh=hh, c2=c2_: mk_pair(hh, c2)))

                    def mk_smm(hh=hh, q=0):
                        if q == 0:
                            tensor.wait_ge(sem["s_vev"], 16 * (hh + 1))
                            if hh >= 1:
                                tensor.wait_ge(sem["s_vS"], hh)
                        for c2 in range(4 * q, 4 * q + 4):
                            ins = tensor.matmul(
                                pb[1][:, 0:128],
                                kts[:, c2 * 128:(c2 + 1) * 128],
                                vtt[:, c2 * 128:(c2 + 1) * 128],
                                start=(c2 == 0), stop=(c2 == 15),
                            )
                            if c2 == 15:
                                ins.then_inc(sem["s_mmS"], 1)
                    for q in range(4):
                        chain_pend.append((lambda hh=hh, q=q: mk_smm(hh, q)))
            while chain_pend:
                chain_pend.pop(0)()

    return nc



def _host_prep_l1(inputs):
    x = np.asarray(inputs["L0_attn_hidden_states"]).reshape(T, D)
    qkv_w = np.asarray(inputs["L0_attn_qkv_w"])
    w3 = qkv_w.reshape(H, 3, HD, D)
    Wq = w3[:, 0].reshape(H * HD, D)
    Wk = w3[:, 1].reshape(H * HD, D)
    Wv = w3[:, 2].reshape(H * HD, D)
    WqT = np.ascontiguousarray(Wq.T)
    WkT = np.ascontiguousarray(Wk.T)
    WvT = np.ascontiguousarray(Wv.T)
    rWT = np.ascontiguousarray(np.asarray(inputs["L0_ffn_router_W"]).T)

    q_decay, k_decay, diag, block_decay = _decays()
    t_ = np.arange(TC)
    kdfull = (k_decay[:, t_ % BS] *
              (block_decay[:, None] ** (NBL - 1 - t_ // BS)[None, :])).astype(np.float32)
    kdext = np.ascontiguousarray(kdfull.reshape(H, 16, 128))

    ident16 = np.eye(16, dtype=np.float32)
    identb = np.eye(128, dtype=BF)
    ones_bf = np.ones((128, 1), BF)
    ones_f = np.ones((1, 128), np.float32)

    in_maps = []
    for c in range(C):
        sl = slice(c * TC, (c + 1) * TC)
        mA = 1.0 if c == 0 else 0.0
        mB = 1.0 if c == C - 1 else 0.0
        negm = np.broadcast_to(
            np.array([-mA, -mB], np.float32), (NE, 2)).copy()
        in_maps.append(dict(
            xT=np.ascontiguousarray(x[sl].T),
            WqT=WqT, WkT=WkT, WvT=WvT, rWT=rWT,
            negm=negm, kdext=kdext, ident16=ident16, identb=identb,
            ones_bf=ones_bf, ones_f=ones_f,
        ))
    return in_maps





def build_l2():
    nc = bass.Bass("TRN2", target_bir_lowering=False, num_devices=C)

    xnT_i = nc.dram_tensor("xnT_i", [D, TC], BF16, kind="ExternalInput").ap()
    qT_i = nc.dram_tensor("qT_i", [D, TC], BF16, kind="ExternalInput").ap()
    kT_i = nc.dram_tensor("kT_i", [D, TC], BF16, kind="ExternalInput").ap()
    vT_i = nc.dram_tensor("vT_i", [D, TC], BF16, kind="ExternalInput").ap()
    S_all = nc.dram_tensor("S_all", [C, H * HD, HD], F32, kind="ExternalInput").ap()
    zr_allT = nc.dram_tensor("zr_allT", [C, NE], F32, kind="ExternalInput").ap()
    xT2 = nc.dram_tensor("xT2", [D, TC], F32, kind="ExternalInput").ap()
    gwT = nc.dram_tensor("gwT", [D, D], F32, kind="ExternalInput").ap()
    fwT = nc.dram_tensor("fwT", [D, D], F32, kind="ExternalInput").ap()
    owT = nc.dram_tensor("owT", [D, D], F32, kind="ExternalInput").ap()
    AT2 = nc.dram_tensor("AT2", [D, 128], F32, kind="ExternalInput").ap()
    BT2 = nc.dram_tensor("BT2", [128, D], F32, kind="ExternalInput").ap()
    rbrow = nc.dram_tensor("rbrow", [1, NE], F32, kind="ExternalInput").ap()
    selm = nc.dram_tensor("selm", [NE, 128], F32, kind="ExternalInput").ap()
    onessc = nc.dram_tensor("onessc", [1, 1], F32, kind="ExternalInput").ap()
    ones8 = nc.dram_tensor("ones8", [C, 1], F32, kind="ExternalInput").ap()
    bcols = nc.dram_tensor("bcols", [128, 32], F32, kind="ExternalInput").ap()
    nwcols = nc.dram_tensor("nwcols", [128, 32], F32, kind="ExternalInput").ap()
    coefb = nc.dram_tensor("coefb", [128, H * C], F32, kind="ExternalInput").ap()
    kd2 = nc.dram_tensor("kd2", [128, 2 * H], F32, kind="ExternalInput").ap()
    qdecb = nc.dram_tensor("qdecb", [H, 128, BS], F32, kind="ExternalInput").ap()
    maskb = nc.dram_tensor("maskb", [H, 128, 2 * BS], F32, kind="ExternalInput").ap()
    identb2 = nc.dram_tensor("identb2", [128, 128], BF16, kind="ExternalInput").ap()
    ones_bf2 = nc.dram_tensor("ones_bf2", [128, 1], BF16, kind="ExternalInput").ap()
    ones_f2 = nc.dram_tensor("ones_f2", [1, 128], F32, kind="ExternalInput").ap()

    outT_o = nc.dram_tensor("outT_o", [D, TC], F32, kind="ExternalOutput").ap()

    gate_hb = nc.dram_tensor("gate_hb", [D, TC], F32)
    g_hb = nc.dram_tensor("g_hb", [D, TC], F32)
    attn_hb = nc.dram_tensor("attn_hb", [D, TC], BF16)

    NT_ = 32
    TH = TC // 2
    CW_ = 512
    _, _, _, bd_arr = _decays()
    bd_list = [float(b) for b in bd_arr]

    ccols = [("g", i) for i in range(NT_)] + [("w", i) for i in range(NT_)]
    NCC = len(ccols)
    groups = []
    for half in range(2):
        for ci, (kind, idx) in enumerate(ccols):
            for tc in range(2):
                groups.append((half, ci, kind, idx, tc))
    NG = len(groups)
    g_bank = {}
    g_evac = {}
    bank_prev_evac = {}
    e_ctr = 0
    bank_last = {}
    for gidx, (half, ci, kind, idx, tc) in enumerate(groups):
        b = 2 * (ci % 2) + tc
        g_bank[gidx] = b
        g_evac[gidx] = e_ctr
        bank_prev_evac[gidx] = bank_last.get(b)
        bank_last[b] = e_ctr
        e_ctr += 1
    NOI = 2 * NT_ * 2

    NPC = 2 * NCC * 2

    with ExitStack() as ctx:
        E = ctx.enter_context
        actT = E(nc.sbuf_tensor("actT", [128, NT_ * TH], BF16))
        wcol = [E(nc.sbuf_tensor(f"wcol{i}", [128, NT_ * 128], BF16)) for i in range(3)]
        at_sb = E(nc.sbuf_tensor("at_sb", [128, NT_ * 128], BF16))
        t_sb = E(nc.sbuf_tensor("t_sb", [128, TC], BF16))
        btsc = E(nc.sbuf_tensor("btsc", [128, D], BF16))
        evf = [E(nc.sbuf_tensor(f"evf{i}", [128, CW_], F32)) for i in range(4)]
        zr_sb2 = E(nc.sbuf_tensor("zr_sb2", [C, NE], F32))
        rb_sb = E(nc.sbuf_tensor("rb_sb", [1, NE], F32))
        sel_sb = E(nc.sbuf_tensor("sel_sb", [NE, 128], F32))
        one1 = E(nc.sbuf_tensor("one1", [1, 1], F32))
        ones8_sb = E(nc.sbuf_tensor("ones8_sb", [C, 1], F32))
        lg = E(nc.sbuf_tensor("lg", [1, 6 * NE], F32))
        routT_sb = E(nc.sbuf_tensor("routT_sb", [NE, 1], F32))
        rout_col = E(nc.sbuf_tensor("rout_col", [128, 1], F32))
        qch = [E(nc.sbuf_tensor(f"qch2{i}", [128, TC], BF16)) for i in range(2)]
        kch = [E(nc.sbuf_tensor(f"kch2{i}", [128, TC], BF16)) for i in range(2)]
        vch = [E(nc.sbuf_tensor(f"vch2{i}", [128, TC], BF16)) for i in range(2)]
        sal = [E(nc.sbuf_tensor(f"sal{i}", [128, C * HD], F32)) for i in range(2)]
        msk = [E(nc.sbuf_tensor(f"msk{i}", [128, 2 * BS], F32)) for i in range(2)]
        qdc = [E(nc.sbuf_tensor(f"qdc{i}", [128, BS], F32)) for i in range(2)]
        kd2_sb = E(nc.sbuf_tensor("kd2_sb", [128, 2 * H], F32))
        kvA = [E(nc.sbuf_tensor(f"kvA{i}", [128, HD], F32)) for i in range(2)]
        kvB = [E(nc.sbuf_tensor(f"kvB{i}", [128, HD], F32)) for i in range(2)]
        kvb = [E(nc.sbuf_tensor(f"kvb{i}", [128, HD], BF16)) for i in range(2)]
        scT = [E(nc.sbuf_tensor(f"scT{i}", [128, 2 * BS], BF16)) for i in range(4)]
        qdT = [E(nc.sbuf_tensor(f"qdT{i}", [128, BS], BF16)) for i in range(4)]
        vtt2 = [E(nc.sbuf_tensor(f"vtt2{i}", [128, BS], BF16)) for i in range(4)]
        kt2 = [E(nc.sbuf_tensor(f"kt2{i}", [128, BS], BF16)) for i in range(4)]
        atb = [E(nc.sbuf_tensor(f"atb{i}", [128, BS], BF16)) for i in range(16)]
        idb2 = E(nc.sbuf_tensor("idb2", [128, 128], BF16))
        onesb2 = E(nc.sbuf_tensor("onesb2", [128, 1], BF16))
        onesf2 = E(nc.sbuf_tensor("onesf2", [1, 128], F32))
        bc_sb = E(nc.sbuf_tensor("bc_sb", [128, NT_], F32))
        nw_sb = E(nc.sbuf_tensor("nw_sb", [128, NT_], F32))
        cf_sb = E(nc.sbuf_tensor("cf_sb", [128, H * C], F32))
        atr = [E(nc.sbuf_tensor(f"atr{i}", [128, CW_], BF16)) for i in range(2)]
        at2 = [E(nc.sbuf_tensor(f"at2_{i}", [128, CW_], BF16)) for i in range(2)]
        sqb2 = [E(nc.sbuf_tensor(f"sqb2{i}", [128, CW_], BF16)) for i in range(2)]
        gtr = [E(nc.sbuf_tensor(f"gtr{i}", [128, CW_], F32)) for i in range(2)]
        rrow = E(nc.sbuf_tensor("rrow", [1, 3 * CW_], F32))
        tmpf = E(nc.sbuf_tensor("tmpf", [128, CW_], F32))
        gre = [E(nc.sbuf_tensor(f"gre{i}", [128, CW_], F32)) for i in range(2)]
        xre = [E(nc.sbuf_tensor(f"xre{i}", [128, CW_], F32)) for i in range(2)]
        osb = [E(nc.sbuf_tensor(f"osb{i}", [128, CW_], F32)) for i in range(2)]
        pb = [E(nc.psum_tensor(f"pb{i}", [128, 512], F32)) for i in range(6)]
        pt = [E(nc.psum_tensor(f"pt{i}", [128, 1024], BF16)) for i in range(2)]
        names = (["s_misc", "s_at", "s_xn0", "s_xn1", "s_wc0", "s_wc1", "s_wc2", "s_bt",
                  "s_mmA", "s_vA", "s_aA", "s_vt",
                  "s_mmcol", "s_ev",
                  "s_spe0", "s_spe1", "s_spe2", "s_spe3",
                  "s_ld0", "s_ld1", "s_lg0", "s_lg1", "s_vkv0", "s_vkv1",
                  "s_mmsc0", "s_mmsc1", "s_vsc0", "s_vsc1",
                  "s_mmT20", "s_mmT21", "s_aev0", "s_aev1",
                  "s_mmat0", "s_mmat1", "s_aat0", "s_aat1",
                  "s_mmC0", "s_mmC1",
                  "s_are0", "s_are1", "s_ar20", "s_ar21", "s_gre0", "s_gre1",
                  "s_mmsq", "s_vrm", "s_arm", "s_mmbc2", "s_vgn",
                  "s_g2re0", "s_g2re1", "s_x2re0", "s_x2re1",
                  "s_mmout", "s_vout", "s_spo0", "s_spo1",
                  "s_vch2", "s_ach2"] + [f"s_spa{i}" for i in range(16)])
        sem = {n: E(nc.semaphore(n)) for n in names}
        block = E(nc.Block())

        NMISC = 12

        @block.sync
        def _(sync):
            for ap_, sb_ in ((zr_allT, zr_sb2), (rbrow, rb_sb), (selm, sel_sb),
                             (onessc, one1), (ones8, ones8_sb), (bcols, bc_sb),
                             (nwcols, nw_sb), (coefb, cf_sb), (kd2, kd2_sb),
                             (identb2, idb2), (ones_bf2, onesb2), (ones_f2, onesf2)):
                sync.dma_start(out=sb_[:], in_=ap_).then_inc(sem["s_misc"], 16)
            for half in range(2):
                if half == 1:
                    sync.wait_ge(sem["s_mmcol"], NG // 2)
                    sync.wait_ge(sem["s_mmA"], 5)
                for cc in range(NT_):
                    sync.dma_start(
                        out=actT[:, cc * TH:(cc + 1) * TH],
                        in_=xnT_i[cc * 128:(cc + 1) * 128,
                                  half * TH:(half + 1) * TH],
                    ).then_inc(sem[f"s_xn{half}"], 16)
                for gidx, (ghalf, ci, kind, idx, tc) in enumerate(groups):
                    if ghalf != half:
                        continue
                    e = g_evac[gidx]
                    sync.wait_ge(sem["s_ev"], e + 1)
                    tgt = gate_hb if kind == "g" else g_hb
                    sync.dma_start(
                        out=tgt[idx * 128:(idx + 1) * 128,
                                half * TH + tc * CW_:
                                half * TH + (tc + 1) * CW_],
                        in_=evf[e % 4][:],
                    ).then_inc(sem[f"s_spe{e % 4}"], 16)
            def attn_exports(hh):
                pi_ = hh % 2
                r_ = hh // 2
                for j in range(NBL):
                    sync.wait_ge(sem[f"s_aat{pi_}"], r_ * NBL + j + 1)
                    sync.dma_start(
                        out=attn_hb[hh * 128:(hh + 1) * 128,
                                    j * BS:(j + 1) * BS],
                        in_=atb[pi_ * 8 + j][:],
                    ).then_inc(sem[f"s_spa{pi_ * 8 + j}"], 16)
            for h in range(H):
                pi = h % 2
                r = h // 2
                if h >= 2:
                    sync.wait_ge(sem[f"s_mmat{pi}"], NBL * r)
                    sync.wait_ge(sem[f"s_mmT2{pi}"], NBL * r)
                    sync.wait_ge(sem[f"s_mmC{pi}"], (NBL - 1) * r)
                    sync.wait_ge(sem[f"s_vkv{pi}"], NBL * r)
                    sync.wait_ge(sem[f"s_vsc{pi}"], NBL * r)
                    sync.wait_ge(sem[f"s_aev{pi}"], NBL * r)
                sl = sem[f"s_ld{h % 2}"]
                sync.dma_start(out=qch[h % 2][:],
                               in_=qT_i[h * 128:(h + 1) * 128, :]).then_inc(sl, 16)
                sync.dma_start(out=kch[h % 2][:],
                               in_=kT_i[h * 128:(h + 1) * 128, :]).then_inc(sl, 16)
                sync.dma_start(out=vch[h % 2][:],
                               in_=vT_i[h * 128:(h + 1) * 128, :]).then_inc(sl, 16)
                if h >= 1:
                    attn_exports(h - 1)
            attn_exports(H - 1)
            for slx in range(4):
                sync.wait_ge(sem[f"s_spe{slx}"], 16 * (NG // 4))
            for half in range(2):
                for tc in range(2):
                    tt = half * 2 + tc
                    tok0 = half * TH + tc * CW_
                    for cc in range(NT_):
                        li = tt * NT_ + cc
                        pi3 = cc % 2
                        r3 = cc // 2
                        j0 = tok0 // BS
                        sync.wait_ge(sem[f"s_spa{pi3 * 8 + j0}"], 16 * (r3 + 1))
                        sync.wait_ge(sem[f"s_spa{pi3 * 8 + j0 + 1}"], 16 * (r3 + 1))
                        if li >= 2:
                            sync.wait_ge(sem["s_ach2"], li - 1)
                        sync.dma_start(
                            out=atr[li % 2][:],
                            in_=attn_hb[cc * 128:(cc + 1) * 128, tok0:tok0 + CW_],
                        ).then_inc(sem[f"s_are{li % 2}"], 16)
                    for cc in range(NT_):
                        li = tt * NT_ + cc
                        if li >= 2:
                            sync.wait_ge(sem["s_vgn"], li - 1)
                        sync.dma_start(
                            out=at2[li % 2][:],
                            in_=attn_hb[cc * 128:(cc + 1) * 128, tok0:tok0 + CW_],
                        ).then_inc(sem[f"s_ar2{li % 2}"], 16)
                        sync.dma_start(
                            out=gtr[li % 2][:],
                            in_=gate_hb[cc * 128:(cc + 1) * 128, tok0:tok0 + CW_],
                        ).then_inc(sem[f"s_gre{li % 2}"], 16)
                for oc in range(NT_):
                    for tc in range(2):
                        oi = (half * NT_ + oc) * 2 + tc
                        tok0 = half * TH + tc * CW_
                        if oi >= 2:
                            sync.wait_ge(sem["s_vout"], oi - 1)
                        sync.dma_start(
                            out=gre[oi % 2][:],
                            in_=g_hb[oc * 128:(oc + 1) * 128, tok0:tok0 + CW_],
                        ).then_inc(sem[f"s_g2re{oi % 2}"], 16)
                        sync.dma_start(
                            out=xre[oi % 2][:],
                            in_=xT2[oc * 128:(oc + 1) * 128, tok0:tok0 + CW_],
                        ).then_inc(sem[f"s_x2re{oi % 2}"], 16)
                        sync.wait_ge(sem["s_vout"], oi + 1)
                        sync.dma_start(
                            out=outT_o[oc * 128:(oc + 1) * 128, tok0:tok0 + CW_],
                            in_=osb[oi % 2][:],
                        ).then_inc(sem[f"s_spo{oi % 2}"], 16)
            sync.wait_ge(sem["s_spo0"], 16 * (NOI // 2))
            sync.wait_ge(sem["s_spo1"], 16 * (NOI // 2))
            for slx in range(4):
                sync.wait_ge(sem[f"s_spe{slx}"], 16 * (NG // 4))
            for slx in range(16):
                sync.wait_ge(sem[f"s_spa{slx}"], 16 * (H // 2))

        @block.gpsimd
        def _(gpsimd):
            gpsimd.dma_start(out=btsc[:], in_=BT2).then_inc(sem["s_bt"], 16)
            gpsimd.dma_start(
                out=at_sb[:].rearrange("p (c o) -> p c o", c=NT_),
                in_=AT2.rearrange("(c p) o -> p c o", p=128),
            ).then_inc(sem["s_at"], 16)
            all_w = []
            for half in range(2):
                for ci, (kind, idx) in enumerate(ccols):
                    all_w.append((gwT if kind == "g" else fwT, idx))
            for half in range(2):
                for oc in range(NT_):
                    all_w.append((owT, oc))
            for gc, (wt, idx) in enumerate(all_w):
                if gc >= 3:
                    pc = gc - 3
                    if pc < 2 * NCC:
                        gpsimd.wait_ge(sem["s_mmcol"], 2 * pc + 2)
                    else:
                        gpsimd.wait_ge(sem["s_mmout"], 2 * (pc - 2 * NCC) + 2)
                gpsimd.dma_start(
                    out=wcol[gc % 3][:].rearrange("p (c o) -> p c o", c=NT_),
                    in_=wt[:, idx * 128:(idx + 1) * 128]
                    .rearrange("(c p) o -> p c o", p=128),
                ).then_inc(sem[f"s_wc{gc % 3}"], 16)
                if gc == 2 * NCC - 1:
                    for h in range(H):
                        pi2 = h % 2
                        r2 = h // 2
                        if h >= 2:
                            gpsimd.wait_ge(sem[f"s_vkv{pi2}"], NBL * r2)
                            gpsimd.wait_ge(sem[f"s_vsc{pi2}"], NBL * r2)
                        sl2 = sem[f"s_lg{pi2}"]
                        gpsimd.dma_start(
                            out=sal[pi2][:].rearrange("p (c d) -> p c d", c=C),
                            in_=S_all[:, h * 128:(h + 1) * 128, :].rearrange(
                                "c p d -> p c d"),
                        ).then_inc(sl2, 16)
                        gpsimd.dma_start(out=msk[pi2][:],
                                         in_=maskb[h]).then_inc(sl2, 16)
                        gpsimd.dma_start(out=qdc[pi2][:],
                                         in_=qdecb[h]).then_inc(sl2, 16)

        @block.vector
        def _(vector):
            vn = [0]

            def vstep(ins):
                vn[0] += 1
                ins.then_inc(sem["s_vch2"], 1)
                return vn[0]

            def vafter(n):
                vector.wait_ge(sem["s_vch2"], n)

            vector.wait_ge(sem["s_misc"], 16 * NMISC)
            vector.wait_ge(sem["s_mmA"], 1)
            n1 = vstep(vector.scalar_tensor_tensor(
                lg[:, 0:NE], pb[3][0:1, 0:NE], 1.0 / NMASK, rb_sb[:],
                ALU.mult, ALU.add))
            vafter(n1)
            n2 = vstep(vector.tensor_reduce(lg[:, NE:NE + 1], lg[:, 0:NE],
                                            AX.X, ALU.max))
            vafter(n2)
            n3 = vstep(vector.tensor_single_scalar(
                lg[:, 2 * NE:3 * NE], lg[:, 0:NE], lg[:, NE:NE + 1], ALU.subtract))
            vafter(n3)
            vector.engine_nop().then_inc(sem["s_vA"], 1)
            vector.wait_ge(sem["s_aA"], 1)
            n4 = vstep(vector.tensor_reduce(lg[:, 4 * NE:4 * NE + 1],
                                            lg[:, 3 * NE:4 * NE], AX.X, ALU.add))
            vafter(n4)
            n5 = vstep(vector.reciprocal(lg[:, 4 * NE + 1:4 * NE + 2],
                                         lg[:, 4 * NE:4 * NE + 1]))
            vafter(n5)
            n6 = vstep(vector.tensor_single_scalar(
                lg[:, 5 * NE:6 * NE], lg[:, 3 * NE:4 * NE],
                lg[:, 4 * NE + 1:4 * NE + 2], ALU.mult))
            vafter(n6)
            vector.engine_nop().then_inc(sem["s_vA"], 1)
            vector.wait_ge(sem["s_mmA"], 2)
            n7 = vstep(vector.tensor_copy(routT_sb[:], pb[3][0:NE, 8:9]))
            vafter(n7)
            vector.engine_nop().then_inc(sem["s_vA"], 1)
            vector.wait_ge(sem["s_mmA"], 3)
            n8 = vstep(vector.tensor_copy(rout_col[:], pb[3][:, 16:17]))
            vector.wait_ge(sem["s_bt"], 16)
            vafter(n8)
            n9 = vstep(vector.tensor_scalar_mul(btsc[:], btsc[:], rout_col[:, 0:1]))
            vafter(n9)
            vector.engine_nop().then_inc(sem["s_vA"], 2)
            for half in range(2):
                for tc in range(2):
                    vector.wait_ge(sem["s_mmA"], 4 + 2 * half + tc)
                    vector.tensor_copy(
                        t_sb[:, half * TH + tc * CW_: half * TH + (tc + 1) * CW_],
                        pb[4][:, :]).then_inc(sem["s_vt"], 1)
            cur = [None, None]
            psc = {0: pb[0], 1: pb[3]}
            pC = {0: pb[2], 1: pb[5]}
            for r in range(H // 2):
                for pi in range(2):
                    h = 2 * r + pi
                    vector.wait_ge(sem[f"s_ld{pi}"], 16 * 3 * (r + 1))
                    vector.wait_ge(sem[f"s_lg{pi}"], 16 * 3 * (r + 1))
                    nacc = vstep(vector.tensor_scalar_mul(
                        kvA[pi][:], sal[pi][:, 0:HD], cf_sb[:, h * C:h * C + 1]))
                    cur[pi], oth = kvA[pi], kvB[pi]
                    for c2 in range(1, C):
                        vafter(nacc)
                        nacc = vstep(vector.scalar_tensor_tensor(
                            oth[:], sal[pi][:, c2 * HD:(c2 + 1) * HD],
                            cf_sb[:, h * C + c2:h * C + c2 + 1], cur[pi][:],
                            ALU.mult, ALU.add))
                        cur[pi], oth = oth, cur[pi]
                    vafter(nacc)
                    if r >= 1:
                        vector.wait_ge(sem[f"s_mmat{pi}"], NBL * r)
                    nkvb = vstep(vector.tensor_copy(kvb[pi][:], cur[pi][:]))
                    vafter(nkvb)
                    vector.engine_nop().then_inc(sem[f"s_vkv{pi}"], 1)
                for j in range(NBL):
                    for pi in range(2):
                        h = 2 * r + pi
                        al = r * NBL + j
                        vector.wait_ge(sem[f"s_mmsc{pi}"], al + 1)
                        if al >= 2:
                            vector.wait_ge(sem[f"s_mmat{pi}"], al - 1)
                        nsc = vstep(vector.tensor_mul(
                            scT[pi * 2 + j % 2][:], psc[pi][:, :], msk[pi][:]))
                        nq = vstep(vector.tensor_mul(
                            qdT[pi * 2 + j % 2][:],
                            qch[pi][:, j * BS:(j + 1) * BS], qdc[pi][:]))
                        vafter(max(nsc, nq))
                        vector.engine_nop().then_inc(sem[f"s_vsc{pi}"], 1)
                        if j < NBL - 1:
                            vector.wait_ge(sem[f"s_mmC{pi}"],
                                           (NBL - 1) * r + j + 1)
                            dst = kvB[pi] if cur[pi] is kvA[pi] else kvA[pi]
                            nup = vstep(vector.scalar_tensor_tensor(
                                dst[:], cur[pi][:], bd_list[h],
                                pC[pi][:, 0:HD], ALU.mult, ALU.add))
                            cur[pi] = dst
                            vafter(nup)
                            vector.wait_ge(sem[f"s_mmat{pi}"], al + 1)
                            nkb = vstep(vector.tensor_copy(kvb[pi][:],
                                                           cur[pi][:]))
                            vafter(nkb)
                            vector.engine_nop().then_inc(sem[f"s_vkv{pi}"], 1)
            vector.wait_ge(sem["s_mmcol"], NG)
            for half in range(2):
                if half == 1:
                    vector.wait_ge(sem["s_mmout"], NOI // 2)
                for tc in range(2):
                    tt = half * 2 + tc
                    vector.wait_ge(sem["s_mmsq"], (tt + 1) * NT_)
                    vector.wait_ge(sem["s_arm"], tt)
                    nr1 = vstep(vector.tensor_scalar(
                        rrow[:, 0:CW_], pb[4][0:1, :], 1.0 / D, EPS,
                        ALU.mult, ALU.add))
                    vafter(nr1)
                    vector.engine_nop().then_inc(sem["s_vrm"], 1)
                    vector.wait_ge(sem["s_arm"], tt + 1)
                    nr2 = vstep(vector.reciprocal(rrow[:, CW_:2 * CW_],
                                                  rrow[:, 2 * CW_:3 * CW_]))
                    vafter(nr2)
                    vector.engine_nop().then_inc(sem["s_vrm"], 1)
                    vector.wait_ge(sem["s_mmbc2"], tt + 1)
                    for cc in range(NT_):
                        li = tt * NT_ + cc
                        vector.wait_ge(sem[f"s_ar2{li % 2}"], 16 * (li // 2 + 1))
                        vector.wait_ge(sem[f"s_gre{li % 2}"], 16 * (li // 2 + 1))
                        vector.wait_ge(sem["s_vgn"], li)
                        ng1 = vstep(vector.scalar_tensor_tensor(
                            tmpf[:], at2[li % 2][:], nw_sb[:, cc:cc + 1],
                            pb[5][:, :], ALU.mult, ALU.mult))
                        vafter(ng1)
                        vector.tensor_mul(
                            actT[:, cc * TH + tc * CW_: cc * TH + (tc + 1) * CW_],
                            tmpf[:], gtr[li % 2][:]).then_inc(sem["s_vgn"], 1)
                for oc in range(NT_):
                    for tc in range(2):
                        oi = (half * NT_ + oc) * 2 + tc
                        vector.wait_ge(sem["s_mmout"], oi + 1)
                        vector.wait_ge(sem[f"s_g2re{oi % 2}"], 16 * (oi // 2 + 1))
                        vector.wait_ge(sem[f"s_x2re{oi % 2}"], 16 * (oi // 2 + 1))
                        if oi >= 2:
                            vector.wait_ge(sem[f"s_spo{oi % 2}"], 16 * (oi // 2))
                        vector.wait_ge(sem["s_vout"], oi)
                        nf1 = vstep(vector.tensor_mul(
                            tmpf[:], pb[oi % 2][:, :], gre[oi % 2][:]))
                        vafter(nf1)
                        vector.tensor_add(osb[oi % 2][:], tmpf[:],
                                          xre[oi % 2][:]).then_inc(sem["s_vout"], 1)

        @block.scalar
        def _(scalar):
            scalar.wait_ge(sem["s_misc"], 16 * NMISC)
            scalar.wait_ge(sem["s_vA"], 1)
            scalar.activation(lg[:, 3 * NE:4 * NE], lg[:, 2 * NE:3 * NE],
                              AF.Exp).then_inc(sem["s_aA"], 1)
            for gidx, (half, ci, kind, idx, tc) in enumerate(groups):
                e = g_evac[gidx]
                scalar.wait_ge(sem["s_mmcol"], gidx + 1)
                if e >= 4:
                    scalar.wait_ge(sem[f"s_spe{e % 4}"], 16 * (e // 4))
                b = g_bank[gidx]
                if kind == "g":
                    scalar.activation(evf[e % 4][:], pb[b][:, :],
                                      AF.Sigmoid).then_inc(sem["s_ev"], 1)
                else:
                    scalar.activation(evf[e % 4][:], pb[b][:, :], AF.Silu,
                                      bias=bc_sb[:, idx:idx + 1]).then_inc(
                        sem["s_ev"], 1)
            pat = {0: pb[1], 1: pb[4]}
            for r in range(H // 2):
                for j in range(NBL):
                    for pi in range(2):
                        h = 2 * r + pi
                        al = r * NBL + j
                        scalar.wait_ge(sem[f"s_mmT2{pi}"], al + 1)
                        if al >= 2:
                            scalar.wait_ge(sem[f"s_mmat{pi}"], al - 1)
                            scalar.wait_ge(sem[f"s_mmC{pi}"],
                                           min((NBL - 1) * (r + 1),
                                               (NBL - 1) * r + j))
                        bank = pt[pi]
                        scalar.copy(vtt2[pi * 2 + j % 2][:], bank[:, 0:BS])
                        if j < NBL - 1:
                            scalar.mul(kt2[pi * 2 + j % 2][:, 0:128],
                                       bank[:, BS:BS + 128],
                                       kd2_sb[:, 2 * h:2 * h + 1])
                            scalar.mul(kt2[pi * 2 + j % 2][:, 128:256],
                                       bank[:, BS + 128:2 * BS],
                                       kd2_sb[:, 2 * h + 1:2 * h + 2]).then_inc(
                                sem[f"s_aev{pi}"], 1)
                        else:
                            scalar.copy(kt2[pi * 2 + j % 2][:, 0:1],
                                        bank[:, BS:BS + 1]).then_inc(
                                sem[f"s_aev{pi}"], 1)
                        scalar.wait_ge(sem[f"s_mmat{pi}"], al + 1)
                        if r >= 1 or pi >= 1:
                            pass
                        if r >= 1:
                            scalar.wait_ge(sem[f"s_spa{pi * 8 + j}"], 16 * r)
                        scalar.copy(atb[pi * 8 + j][:],
                                    pat[pi][:, 0:BS]).then_inc(
                            sem[f"s_aat{pi}"], 1)
            for half in range(2):
                for tc in range(2):
                    tt = half * 2 + tc
                    for cc in range(NT_):
                        li = tt * NT_ + cc
                        scalar.wait_ge(sem[f"s_are{li % 2}"], 16 * (li // 2 + 1))
                        if li >= 2:
                            scalar.wait_ge(sem["s_mmsq"], li - 1)
                        scalar.square(sqb2[li % 2][:], atr[li % 2][:]).then_inc(
                            sem["s_ach2"], 1)
                    scalar.wait_ge(sem["s_vrm"], 2 * tt + 1)
                    scalar.sqrt(rrow[:, 2 * CW_:3 * CW_],
                                rrow[:, 0:CW_]).then_inc(sem["s_arm"], 1)

        @block.tensor
        def _(tensor):
            tensor.wait_ge(sem["s_misc"], 16 * NMISC)
            tensor.matmul(pb[3][0:1, 0:NE], ones8_sb[:], zr_sb2[:],
                          start=True, stop=True).then_inc(sem["s_mmA"], 1)
            tensor.wait_ge(sem["s_vA"], 2)
            tensor.matmul(pb[3][0:NE, 8:9], lg[:, 5 * NE:6 * NE], one1[:],
                          start=True, stop=True).then_inc(sem["s_mmA"], 1)
            tensor.wait_ge(sem["s_vA"], 3)
            tensor.matmul(pb[3][:, 16:17], sel_sb[:], routT_sb[:],
                          start=True, stop=True).then_inc(sem["s_mmA"], 1)
            for half in range(2):
                tensor.wait_ge(sem[f"s_xn{half}"], 16 * NT_)
                tensor.wait_ge(sem["s_at"], 16)
                tensor.wait_ge(sem["s_vt"], 2 * half)
                for tc in range(2):
                    for dc in range(NT_):
                        ins = tensor.matmul(
                            pb[4][:, :],
                            at_sb[:, dc * 128:(dc + 1) * 128],
                            actT[:, dc * TH + tc * CW_: dc * TH + (tc + 1) * CW_],
                            start=(dc == 0), stop=(dc == NT_ - 1))
                    ins.then_inc(sem["s_mmA"], 1)
                    if tc == 0:
                        tensor.wait_ge(sem["s_vt"], 2 * half + 1)
                for ci, (kind, idx) in enumerate(ccols):
                    gbase = half * NCC * 2 + ci * 2
                    gc = half * NCC + ci
                    tensor.wait_ge(sem[f"s_wc{gc % 3}"], 16 * (gc // 3 + 1))
                    if kind == "w":
                        tensor.wait_ge(sem["s_vA"], 5)
                        tensor.wait_ge(sem["s_vt"], 2 * half + 2)
                    for tc in range(2):
                        gidx = gbase + tc
                        b = g_bank[gidx]
                        pe_ = bank_prev_evac[gidx]
                        if pe_ is not None:
                            tensor.wait_ge(sem["s_ev"], pe_ + 1)
                        for dc in range(NT_):
                            ins = tensor.matmul(
                                pb[b][:, :],
                                wcol[gc % 3][:, dc * 128:(dc + 1) * 128],
                                actT[:, dc * TH + tc * CW_:
                                     dc * TH + (tc + 1) * CW_],
                                start=(dc == 0),
                                stop=(dc == NT_ - 1 and kind == "g"))
                        if kind == "w":
                            ins = tensor.matmul(
                                pb[b][:, :],
                                btsc[:, idx * 128:(idx + 1) * 128],
                                t_sb[:, half * TH + tc * CW_:
                                     half * TH + (tc + 1) * CW_],
                                start=False, stop=True)
                        ins.then_inc(sem["s_mmcol"], 1)
            psc = {0: pb[0], 1: pb[3]}
            pat = {0: pb[1], 1: pb[4]}
            pC = {0: pb[2], 1: pb[5]}
            tensor.wait_ge(sem["s_vA"], 5)
            tensor.wait_ge(sem["s_vt"], 4)
            for r in range(H // 2):
                for pi in range(2):
                    tensor.wait_ge(sem[f"s_ld{pi}"], 16 * 3 * (r + 1))
                    tensor.wait_ge(sem[f"s_lg{pi}"], 16 * 3 * (r + 1))
                for j in range(NBL):
                    for pi in range(2):
                        h = 2 * r + pi
                        al = r * NBL + j
                        if al >= 1:
                            tensor.wait_ge(sem[f"s_vsc{pi}"], al)
                        for mi in range(2):
                            ins = tensor.matmul(
                                psc[pi][:, mi * BS:(mi + 1) * BS],
                                kch[pi][:, j * BS + mi * 128:
                                         j * BS + (mi + 1) * 128],
                                qch[pi][:, j * BS:(j + 1) * BS],
                                start=True, stop=True)
                        ins.then_inc(sem[f"s_mmsc{pi}"], 1)
                        if al >= 1:
                            tensor.wait_ge(sem[f"s_aev{pi}"], al)
                        for mi in range(2):
                            tensor.transpose(
                                pt[pi][:, mi * 128:(mi + 1) * 128],
                                vch[pi][:, j * BS + mi * 128:
                                          j * BS + (mi + 1) * 128],
                                idb2[:])
                        for mi in range(2):
                            ins = tensor.transpose(
                                pt[pi][:, BS + mi * 128:BS + (mi + 1) * 128],
                                kch[pi][:, j * BS + mi * 128:
                                          j * BS + (mi + 1) * 128],
                                idb2[:])
                        ins.then_inc(sem[f"s_mmT2{pi}"], 1)
                        tensor.wait_ge(sem[f"s_vsc{pi}"], al + 1)
                        tensor.wait_ge(sem[f"s_aev{pi}"], al + 1)
                        tensor.wait_ge(sem[f"s_vkv{pi}"], NBL * r + j + 1)
                        if al >= 1:
                            tensor.wait_ge(sem[f"s_aat{pi}"], al)
                        for mi in range(2):
                            tensor.matmul(
                                pat[pi][:, 0:BS],
                                vtt2[pi * 2 + j % 2][:, mi * 128:(mi + 1) * 128],
                                scT[pi * 2 + j % 2][:, mi * BS:(mi + 1) * BS],
                                start=(mi == 0), stop=False)
                        tensor.matmul(
                            pat[pi][:, 0:BS], kvb[pi][:],
                            qdT[pi * 2 + j % 2][:],
                            start=False, stop=True).then_inc(
                            sem[f"s_mmat{pi}"], 1)
                        if j < NBL - 1:
                            for mi in range(2):
                                ins = tensor.matmul(
                                    pC[pi][:, 0:HD],
                                    kt2[pi * 2 + j % 2][:,
                                        mi * 128:(mi + 1) * 128],
                                    vtt2[pi * 2 + j % 2][:,
                                         mi * 128:(mi + 1) * 128],
                                    start=(mi == 0), stop=(mi == 1))
                            ins.then_inc(sem[f"s_mmC{pi}"], 1)
            for pi in range(2):
                tensor.wait_ge(sem[f"s_vsc{pi}"], H * NBL // 2)
                tensor.wait_ge(sem[f"s_aat{pi}"], H * NBL // 2)
                tensor.wait_ge(sem[f"s_vkv{pi}"], H * NBL // 2)
                tensor.wait_ge(sem[f"s_aev{pi}"], H * NBL // 2)
            for half in range(2):
                for tc in range(2):
                    tt = half * 2 + tc
                    for cc in range(NT_):
                        li = tt * NT_ + cc
                        tensor.wait_ge(sem["s_ach2"], li + 1)
                        if cc == 0 and tt > 0:
                            tensor.wait_ge(sem["s_vrm"], 2 * tt)
                        tensor.matmul(pb[4][0:1, :], onesb2[:], sqb2[li % 2][:],
                                      start=(cc == 0), stop=(cc == NT_ - 1)
                                      ).then_inc(sem["s_mmsq"], 1)
                    tensor.wait_ge(sem["s_vrm"], 2 * tt + 2)
                    if tt > 0:
                        tensor.wait_ge(sem["s_vgn"], tt * NT_)
                    tensor.matmul(pb[5][:, :], onesf2[:], rrow[:, CW_:2 * CW_],
                                  start=True, stop=True).then_inc(sem["s_mmbc2"], 1)
                for oc in range(NT_):
                    gc = 2 * NCC + half * NT_ + oc
                    tensor.wait_ge(sem[f"s_wc{gc % 3}"], 16 * (gc // 3 + 1))
                    tensor.wait_ge(sem["s_vgn"], (half + 1) * 2 * NT_)
                    for tc in range(2):
                        oi = (half * NT_ + oc) * 2 + tc
                        if oi >= 2:
                            tensor.wait_ge(sem["s_vout"], oi - 1)
                        for dc in range(NT_):
                            tensor.matmul(
                                pb[oi % 2][:, :],
                                wcol[gc % 3][:, dc * 128:(dc + 1) * 128],
                                actT[:, dc * TH + tc * CW_:
                                     dc * TH + (tc + 1) * CW_],
                                start=(dc == 0), stop=(dc == NT_ - 1),
                            ).then_inc(sem["s_mmout"], 1 if dc == NT_ - 1 else 0)

    return nc


def _host_prep_l2(inputs, res1):
    x = np.asarray(inputs["L0_attn_hidden_states"]).reshape(T, D)
    gwT = np.ascontiguousarray(np.asarray(inputs["L0_attn_gate_w"]).T)
    fwT = np.ascontiguousarray(np.asarray(inputs["L0_ffn_W"]).T)
    owT = np.ascontiguousarray(np.asarray(inputs["L0_attn_out_w"]).T)
    AT2 = np.ascontiguousarray(
        np.asarray(inputs["L0_ffn_A"]).reshape(NE * NR, D).T)
    BT2 = np.ascontiguousarray(
        np.asarray(inputs["L0_ffn_B"]).transpose(0, 2, 1).reshape(NE * NR, D))
    rbrow = np.asarray(inputs["L0_ffn_router_b"]).reshape(1, NE).astype(np.float32)
    norm_w = np.asarray(inputs["L0_attn_norm_w"]).astype(np.float32)
    b = np.asarray(inputs["L0_ffn_b"]).astype(np.float32)

    selm = np.zeros((NE, 128), np.float32)
    for p in range(128):
        selm[p // NR, p] = 2.0
    onessc = np.ones((1, 1), np.float32)
    ones8 = np.ones((C, 1), np.float32)
    bcols = np.ascontiguousarray(b.reshape(32, 128).T)
    nwcols = np.ascontiguousarray(norm_w.reshape(32, 128).T)

    q_decay, k_decay, diag, bd = _decays()
    kd2 = np.zeros((128, 2 * H), np.float32)
    for h in range(H):
        kd2[:, 2 * h] = k_decay[h, 0:128]
        kd2[:, 2 * h + 1] = k_decay[h, 128:256]
    qdecb = np.ascontiguousarray(
        np.broadcast_to(q_decay[:, None, :], (H, 128, BS))).astype(np.float32)
    maskT = diag.transpose(0, 2, 1)
    maskb = np.ascontiguousarray(
        maskT.reshape(H, 2, 128, BS).transpose(0, 2, 1, 3).reshape(H, 128, 2 * BS)
    ).astype(np.float32)
    identb2 = np.eye(128, dtype=BF)
    ones_bf2 = np.ones((128, 1), BF)
    ones_f2 = np.ones((1, 128), np.float32)

    S_allv = np.ascontiguousarray(
        np.stack([np.asarray(res1[c]["S_o"]) for c in range(C)]))
    zr_allT = np.ascontiguousarray(
        np.stack([np.asarray(res1[c]["zr_o"])[:, 0] for c in range(C)]))

    in_maps = []
    for c in range(C):
        coefb = np.zeros((128, H * C), np.float32)
        for h in range(H):
            for cp in range(c):
                coefb[:, h * C + cp] = bd[h] ** (NBL * (c - 1 - cp))
        in_maps.append(dict(
            xnT_i=np.asarray(res1[c]["xnT_o"]),
            qT_i=np.asarray(res1[c]["qT_o"]),
            kT_i=np.asarray(res1[c]["kT_o"]),
            vT_i=np.asarray(res1[c]["vT_o"]),
            S_all=S_allv, zr_allT=zr_allT,
            xT2=np.ascontiguousarray(x[c * TC:(c + 1) * TC].T),
            gwT=gwT, fwT=fwT, owT=owT, AT2=AT2, BT2=BT2,
            rbrow=rbrow, selm=selm, onessc=onessc, ones8=ones8,
            bcols=bcols, nwcols=nwcols, coefb=coefb, kd2=kd2,
            qdecb=qdecb, maskb=maskb, identb2=identb2,
            ones_bf2=ones_bf2, ones_f2=ones_f2,
        ))
    return in_maps


_NC_CACHE = {}


def kernel(**inputs):
    if "l1" not in _NC_CACHE:
        _NC_CACHE["l1"] = build_l1()
        _NC_CACHE["l2"] = build_l2()
    in1 = _host_prep_l1(inputs)
    res1 = run_bass_kernel_spmd(_NC_CACHE["l1"], in1,
                                core_ids=list(range(C))).results
    in2 = _host_prep_l2(inputs, res1)
    res2 = run_bass_kernel_spmd(_NC_CACHE["l2"], in2,
                                core_ids=list(range(C))).results
    out = np.concatenate(
        [np.asarray(res2[c]["outT_o"]).T for c in range(C)], axis=0)
    return out.reshape(1, T, D)
